# version 03
# baseline (speedup 1.0000x reference)
import contextlib
import numpy as np
import ml_dtypes
import concourse.bass as bass
import concourse.mybir as mybir
from concourse.bass_utils import run_bass_kernel_spmd

F32 = mybir.dt.float32
BF16 = mybir.dt.bfloat16
F16 = mybir.dt.float16
I32 = mybir.dt.int32
U32 = mybir.dt.uint32
U8 = mybir.dt.uint8
AF = mybir.ActivationFunctionType
ALU = mybir.AluOpType

D = 2048
L = 2048
BATCH = 16
NCORES = 8
SEQ_PER_CORE = BATCH // NCORES
T = SEQ_PER_CORE * L
DIN = 2048
NH = 32
DS = 64
HD = 64
CONVD = 6144
DIP = 8224
FFQ = 5632
FF = 8192
CHUNK = 128
GB = D // 128
EPS_LN = 1e-5
EPS_RMS = 1e-5
EPS_Q = 1e-6

C_EXPBITS = float(126 << 23)
G_SCALE = 2.0 / 3.0 / float(1 << 22)

PANEL = 512
NPANEL = T // PANEL
NT = PANEL // 128

ALL_ENG = ("tensor", "vector", "scalar", "sync", "gpsimd")


def _pi_perm(n):
    idx = np.empty(n, np.int64)
    for r in range(n // 256):
        b = 256 * r
        idx[b:b + 128] = b + 2 * np.arange(128)
        idx[b + 128:b + 256] = b + 2 * np.arange(128) + 1
    return idx


PI_D = _pi_perm(D)
PI_FF = _pi_perm(FFQ)


class Sem:
    b = 0

    def __init__(self, stack, nc, name):
        self.h = stack.enter_context(nc.semaphore(name))
        self.v = 0

    def inc(self, instr, n=1):
        instr.then_inc(self.h, n)
        self.v += n
        return instr

    def wait(self, eng, v=None):
        eng.wait_ge(self.h, self.v if v is None else v)


def _barrier(ctx):
    block = ctx["block"]
    deps = ctx.get("deps", [])
    for engname in ALL_ENG:
        def _f(e, _deps=list(deps)):
            for s, v in _deps:
                e.wait_ge(s.h, v)
        getattr(block, engname)(_f)
    ctx["deps"] = []


def _stage_out(ctx, *sems):
    ctx.setdefault("deps", [])
    for s in sems:
        ctx["deps"].append((s, s.v))


def build_program(debug_taps=(), stages=("P0", "A", "B01", "B2", "B3", "B4", "C"),
                  detect_races=True, input_bufs=()):
    nc = bass.Bass("TRN2", target_bir_lowering=False, detect_race_conditions=detect_races)
    ap = lambda t: t.ap()

    xin = ap(nc.dram_tensor("xin", [T, D], F32, kind="ExternalInput"))
    win2 = ap(nc.dram_tensor("win2", [65, 128, GB * 128], F32, kind="ExternalInput"))
    woutT = ap(nc.dram_tensor("woutT", [DIN, D], F32, kind="ExternalInput"))
    wgate = ap(nc.dram_tensor("wgate", [64, 128, GB * 128], F32, kind="ExternalInput"))
    wup = ap(nc.dram_tensor("wup", [64, 128, GB * 128], F32, kind="ExternalInput"))
    wdownT = ap(nc.dram_tensor("wdownT", [FF, D], F32, kind="ExternalInput"))
    gu_u8t = ap(nc.dram_tensor("gu_u8t", [D // 2, 2 * FFQ], U8, kind="ExternalInput"))
    gu_sc = ap(nc.dram_tensor("gu_sc", [D // 128, 2 * FFQ], F16, kind="ExternalInput"))
    dn_u8t = ap(nc.dram_tensor("dn_u8t", [FFQ // 2, D], U8, kind="ExternalInput"))
    dn_sc = ap(nc.dram_tensor("dn_sc", [FFQ // 128, D], F16, kind="ExternalInput"))
    zbias = ap(nc.dram_tensor("zbias", [DIN], F32, kind="ExternalInput"))
    convw = ap(nc.dram_tensor("convw", [CONVD, 4], F32, kind="ExternalInput"))
    convb = ap(nc.dram_tensor("convb", [CONVD], F32, kind="ExternalInput"))
    dvec = ap(nc.dram_tensor("dvec", [NH], F32, kind="ExternalInput"))
    n1w = ap(nc.dram_tensor("n1w", [D], F32, kind="ExternalInput"))
    n2w = ap(nc.dram_tensor("n2w", [D], F32, kind="ExternalInput"))
    identf = ap(nc.dram_tensor("identf", [128, 128], F32, kind="ExternalInput"))
    trilu = ap(nc.dram_tensor("trilu", [128, 128], F32, kind="ExternalInput"))
    onesf = ap(nc.dram_tensor("onesf", [128, 128], F32, kind="ExternalInput"))

    out = ap(nc.dram_tensor("out", [T, D], F32, kind="ExternalOutput"))

    kind = lambda name: ("ExternalInput" if name in input_bufs else
                         "ExternalOutput" if name in debug_taps else "Internal")
    xbuf1 = ap(nc.dram_tensor("xbuf1", [T, D], F32, kind=kind("xbuf1")))
    xbuf2 = ap(nc.dram_tensor("xbuf2", [T, D], F32, kind=kind("xbuf2")))
    vbuf = ap(nc.dram_tensor("vbuf", [T, D], F32, kind=kind("vbuf")))
    resbuf = ap(nc.dram_tensor("resbuf", [T, D], F32, kind=kind("resbuf")))
    zbuf = ap(nc.dram_tensor("zbuf", [DIN, T], F32, kind=kind("zbuf")))
    xbcbuf = ap(nc.dram_tensor("xbcbuf", [CONVD, T], BF16, kind=kind("xbcbuf")))
    dtbuf = ap(nc.dram_tensor("dtbuf", [32, T], F32, kind=kind("dtbuf")))
    prodbuf = ap(nc.dram_tensor("prodbuf", [DIN, T], BF16, kind=kind("prodbuf")))
    h2fmbuf = ap(nc.dram_tensor("h2fmbuf", [GB, 128, T], BF16, kind=kind("h2fmbuf")))
    wgu_scr = ap(nc.dram_tensor("wgu_scr", [88, 128, GB * 128], BF16, kind=kind("wgu_scr")))
    wdn_scr = ap(nc.dram_tensor("wdn_scr", [FFQ, D], BF16, kind=kind("wdn_scr")))
    wg_scr = ap(nc.dram_tensor("wg_scr", [64, 128, GB * 128], BF16, kind=kind("wg_scr")))
    wu_scr = ap(nc.dram_tensor("wu_scr", [64, 128, GB * 128], BF16, kind=kind("wu_scr")))
    wd_scr = ap(nc.dram_tensor("wd_scr", [FF, D], BF16, kind=kind("wd_scr")))
    dbg = None
    if "b2dbg" in debug_taps:
        ctx_b2dbg = dict(
            Hst=ap(nc.dram_tensor("b2_dbg_Hst", [128, 1024], F32, kind="ExternalOutput")),
            y_fm=ap(nc.dram_tensor("b2_dbg_yfm", [128, GB, 256], BF16, kind="ExternalOutput")),
            xT=ap(nc.dram_tensor("b2_dbg_xT", [128, GB * 128], BF16, kind="ExternalOutput")),
            A_bc=ap(nc.dram_tensor("b2_dbg_Abc", [128, 32], F32, kind="ExternalOutput")),
            S_sb=ap(nc.dram_tensor("b2_dbg_Ssb", [128, 64], F32, kind="ExternalOutput")),
            cum_tok=ap(nc.dram_tensor("b2_dbg_cum", [128, 32], F32, kind="ExternalOutput")),
            prodt=ap(nc.dram_tensor("b2_dbg_prodt", [128, GB, 256], BF16, kind="ExternalOutput")),
            Hst0=ap(nc.dram_tensor("b2_dbg_Hst0", [128, 1024], F32, kind="ExternalOutput")),
            Hmid=ap(nc.dram_tensor("b2_dbg_Hmid", [128, 1024], F32, kind="ExternalOutput")),
            seg=ap(nc.dram_tensor("b2_dbg_seg", [128, 128], F32, kind="ExternalOutput")),
            segE=ap(nc.dram_tensor("b2_dbg_segE", [128, 128], F32, kind="ExternalOutput")),
            E_sb=ap(nc.dram_tensor("b2_dbg_Esb", [128, 128], F32, kind="ExternalOutput")),
            sc_bf=ap(nc.dram_tensor("b2_dbg_scbf", [128, 128], BF16, kind="ExternalOutput")),
            sc_m=ap(nc.dram_tensor("b2_dbg_scm", [128, 128], BF16, kind="ExternalOutput")),
            y_sb=ap(nc.dram_tensor("b2_dbg_ysb", [128, 128], F32, kind="ExternalOutput")),
            Cs=ap(nc.dram_tensor("b2_dbg_Cs", [128, 128], F32, kind="ExternalOutput")),
            T1tmp=ap(nc.dram_tensor("b2_dbg_T1", [128, 128], F32, kind="ExternalOutput")),
            cum=ap(nc.dram_tensor("b2_dbg_cumfm", [32, 128], F32, kind="ExternalOutput")),
            cum_tok0=ap(nc.dram_tensor("b2_dbg_cumtok0", [128, 32], F32, kind="ExternalOutput")),
            ncum_tok0=ap(nc.dram_tensor("b2_dbg_ncumtok0", [128, 32], F32, kind="ExternalOutput")),
            sp_t0=ap(nc.dram_tensor("b2_dbg_spt0", [32, 128], F32, kind="ExternalOutput")),
            dtt0=ap(nc.dram_tensor("b2_dbg_dtt0", [32, 256], F32, kind="ExternalOutput")),
            scview=ap(nc.dram_tensor("b2_dbg_scview", [32, 512], F32, kind="ExternalOutput")),
        )
    else:
        ctx_b2dbg = None
    if "b01dbg" in debug_taps:
        ctx_b01dbg = ap(nc.dram_tensor("b01dbg", [128, 36], F32, kind="ExternalOutput"))
    else:
        ctx_b01dbg = None
    if "dbg_u1fm" in debug_taps:
        dbg = dict(
            u1fm=ap(nc.dram_tensor("dbg_u1fm", [128, GB, PANEL], BF16, kind="ExternalOutput")),
            hfm=ap(nc.dram_tensor("dbg_hfm", [128, 44, PANEL], BF16, kind="ExternalOutput")),
            yq=ap(nc.dram_tensor("dbg_yq", [NT * 128, D], F32, kind="ExternalOutput")),
            cols=ap(nc.dram_tensor("dbg_cols", [128, 64], F32, kind="ExternalOutput")),
            u1tm=ap(nc.dram_tensor("dbg_u1tm", [3 * 128, D], BF16, kind="ExternalOutput")),
            xt=ap(nc.dram_tensor("dbg_xt", [2 * 128, D], F32, kind="ExternalOutput")),
        )

    stack = contextlib.ExitStack()
    identf_sb = stack.enter_context(nc.sbuf_tensor("identf_sb", [128, 128], F32))
    identb_sb = stack.enter_context(nc.sbuf_tensor("identb_sb", [128, 128], BF16))
    onesf_sb = stack.enter_context(nc.sbuf_tensor("onesf_sb", [128, 128], F32))
    n1w_sb = stack.enter_context(nc.sbuf_tensor("n1w_sb", [128, D], BF16))
    n2w_sb = stack.enter_context(nc.sbuf_tensor("n2w_sb", [128, D], BF16))
    dbc_sb = stack.enter_context(nc.sbuf_tensor("dbc_sb", [128, NH], F32))
    zb_sb = stack.enter_context(nc.sbuf_tensor("zb_sb", [128, GB], F32))
    cw_sb = stack.enter_context(nc.sbuf_tensor("cw_sb", [128, 48, 4], F32))
    cb_sb = stack.enter_context(nc.sbuf_tensor("cb_sb", [128, 48], F32))
    ccol_sb = stack.enter_context(nc.sbuf_tensor("ccol_sb", [128, 8], F32))
    trilu_sb = stack.enter_context(nc.sbuf_tensor("trilu_sb", [128, 128], BF16))

    with stack:
        with nc.Block() as block:
            sem = lambda name: Sem(stack, nc, name)
            pre = sem("pre")
            pre2 = sem("pre2")

            @block.sync
            def _(s):
                pre.inc(s.dma_start(out=identf_sb[:], in_=identf[:]), 16)
                pre.inc(s.dma_start(out=onesf_sb[:], in_=onesf[:]), 16)
                pre.inc(s.dma_start(out=dbc_sb[:], in_=dvec[None, :].to_broadcast([128, NH])), 16)
                pre.inc(s.dma_start(
                    out=zb_sb[:],
                    in_=zbias.rearrange("(g p) -> p g", p=128), allow_slow_non_contiguous=True), 16)
                pre.inc(s.dma_start(out=cw_sb[:], in_=convw.rearrange("(f p) k -> p f k", p=128)), 16)
                pre.inc(s.dma_start(
                    out=cb_sb[:],
                    in_=convb.rearrange("(f p) -> p f", p=128), allow_slow_non_contiguous=True), 16)

            @block.gpsimd
            def _(g):
                pre2.inc(g.dma_start(out=identb_sb[:], in_=identf[:]), 16)
                pre2.inc(g.dma_start(out=trilu_sb[:], in_=trilu[:]), 16)
                pre2.inc(g.dma_start(out=n1w_sb[:], in_=n1w[None, :].to_broadcast([128, D])), 16)
                pre2.inc(g.dma_start(out=n2w_sb[:], in_=n2w[None, :].to_broadcast([128, D])), 16)

            @block.vector
            def _(v):
                pre.wait(v)
                pre2.wait(v)
                v.memset(ccol_sb[:, 0:1], EPS_LN)
                v.memset(ccol_sb[:, 1:2], 4.0 * EPS_Q)
                v.memset(ccol_sb[:, 2:3], EPS_RMS)
                v.memset(ccol_sb[:, 4:8], 0.0)
                pre.inc(v.memset(ccol_sb[:, 3:4], 1.0), 1)

            geng = dict(t=Sem(stack, nc, "g_t"), v=Sem(stack, nc, "g_v"),
                        s=Sem(stack, nc, "g_s"), g=Sem(stack, nc, "g_g"))
            gdma = [Sem(stack, nc, f"g_dma{i}") for i in range(12)]
            gswd = [Sem(stack, nc, f"g_swd{i}") for i in range(2)]
            ctx = dict(nc=nc, block=block, sem=sem, stack=stack, ccol=ccol_sb,
                       geng=geng, gdma=gdma, gswd=gswd,
                       b01dbg=ctx_b01dbg, b2dbg=ctx_b2dbg,
                       identf=identf_sb, identb=identb_sb, ones=onesf_sb,
                       trilu=trilu_sb,
                       n1w=n1w_sb, n2w=n2w_sb, dbc=dbc_sb, zb=zb_sb,
                       cw=cw_sb, cb=cb_sb, deps=[(pre, pre.v), (pre2, pre2.v)])

            final_sems = []
            if "P0" in stages:
                _barrier(ctx)
                emit_p0(ctx, gu_u8t, gu_sc, dn_u8t, dn_sc, wgu_scr, wdn_scr,
                        [(wgate, wg_scr, 64, GB * 128), (wup, wu_scr, 64, GB * 128)],
                        [(wdownT, wd_scr, FF)])
            if "A" in stages:
                _barrier(ctx)
                emit_ffnq(ctx, "A", xin, xbuf1, wgu_scr, wdn_scr, out_ap=None, dbg=dbg)
            if "B01" in stages:
                _barrier(ctx)
                emit_b01(ctx, xbuf1, vbuf, win2, zbuf, xbcbuf, dtbuf)
            if "B2" in stages:
                _barrier(ctx)
                emit_b2(ctx, zbuf, xbcbuf, dtbuf, prodbuf)
            if "B3" in stages:
                _barrier(ctx)
                emit_b3(ctx, prodbuf, woutT, vbuf, xbuf1, resbuf, h2fmbuf)
            if "B4" in stages:
                _barrier(ctx)
                emit_b4(ctx, h2fmbuf, wg_scr, wu_scr, wd_scr, resbuf, xbuf2)
            if "C" in stages:
                _barrier(ctx)
                emit_ffnq(ctx, "C", xbuf2, None, wgu_scr, wdn_scr, out_ap=out)
            _barrier(ctx)

    return nc


def emit_p0(ctx, gu_u8t, gu_sc, dn_u8t, dn_sc, wgu_scr, wdn_scr, tiled, rowmajor):
    nc, block, sem = ctx["nc"], ctx["block"], ctx["sem"]
    lds = [sem("p0_ld0"), sem("p0_ld1")]
    lscs = [sem("p0_lsc0"), sem("p0_lsc1")]
    cv = sem("p0_cv")
    vp = sem("p0_vp")
    sts = [sem("p0_st0"), sem("p0_st1")]
    ldcs = [sem("p0c_ld0"), sem("p0c_ld1")]
    stcs = [sem("p0c_st0"), sem("p0c_st1")]

    FG = 2816
    FD = 2048
    es = contextlib.ExitStack()
    sb = lambda n, sh, dt: es.enter_context(nc.sbuf_tensor(n, sh, dt))
    su8 = [sb(f"p0u8_{i}", [128, FG], U8) for i in range(2)]
    si32 = [sb(f"p0i32_{i}", [128, FG], U32) for i in range(2)]
    ssc = [sb(f"p0sc_{i}", [128, FG], F32) for i in range(2)]
    snib = sb("p0nib", [128, FG], U32)
    sbem = sb("p0bem", [128, FG], U32)
    sts_ = sb("p0ts", [128, FG], U32)
    sbits = sb("p0bits", [128, FG], U32)
    sg_ = sb("p0g", [128, FG], F32)
    sv1 = sb("p0v1", [128, FG], F32)
    swout = [sb(f"p0w_{i}", [128, 2 * FG], BF16) for i in range(2)]
    cbuf = [sb(f"p0c_{i}", [128, GB * 128], BF16) for i in range(2)]

    jobs = []
    for R in range(D // 256):
        for fc in range(2 * FFQ // FG):
            jobs.append(("gu", R, fc))
    for R in range(FFQ // 256):
        jobs.append(("dn", R, 0))

    def job_src(j):
        knd, R, fc = j
        if knd == "gu":
            sl = slice(fc * FG, (fc + 1) * FG)
            return (gu_u8t[128 * R:128 * R + 128, sl], gu_sc, R, sl, FG)
        return (dn_u8t[128 * R:128 * R + 128, :], dn_sc, R, slice(0, FD), FD)

    cjobs = []
    for (src, dst, S, W) in tiled:
        for s_ in range(S):
            cjobs.append((src[s_], dst[s_], W))
    for (src, dst, ROWS) in rowmajor:
        for s_ in range(ROWS // 128):
            cjobs.append((src[128 * s_:128 * s_ + 128, :], dst[128 * s_:128 * s_ + 128, :], D))

    @block.sync
    def _(s):
        for i, j in enumerate(jobs):
            u8src, _, _, _, F = job_src(j)
            if i >= 2:
                s.wait_ge(cv.h, i - 1)
            lds[i % 2].inc(s.dma_start(out=su8[i % 2][:, :F], in_=u8src), 16)

    @block.gpsimd
    def _(g):
        for i, j in enumerate(jobs):
            _, scsrc, R, sl, F = job_src(j)
            if i >= 2:
                g.wait_ge(vp.h, i - 1)
            lscs[i % 2].inc(g.dma_start(out=ssc[i % 2][0:64, :F],
                                in_=scsrc[2 * R, sl][None, :].to_broadcast([64, F])), 16)
            lscs[i % 2].inc(g.dma_start(out=ssc[i % 2][64:128, :F],
                                in_=scsrc[2 * R + 1, sl][None, :].to_broadcast([64, F])), 16)
        for i, (src, dst, W) in enumerate(cjobs):
            if i >= 2:
                g.wait_ge(stcs[i % 2].h, 16 * (i // 2))
            ldcs[i % 2].inc(g.dma_start(out=cbuf[i % 2][:, :W], in_=src), 16)

    @block.scalar
    def _(s):
        nj = len(jobs)
        for i in range(nj + 1):
            if i < nj:
                j = jobs[i]
                _, _, _, _, F = job_src(j)
                s.wait_ge(lds[i % 2].h, 16 * (i // 2 + 1))
                if i >= 2:
                    s.wait_ge(vp.h, i - 1)
                cv.inc(s.activation(si32[i % 2][:, :F], su8[i % 2][:, :F], AF.Copy), 1)
            if i >= 1:
                j = jobs[i - 1]
                knd, R, fc = j
                _, _, _, sl, F = job_src(j)
                s.wait_ge(vp.h, i)
                w = swout[(i - 1) % 2]
                stj = sts[(i - 1) % 2]
                if knd == "gu":
                    ss = 22 * fc
                    dst0 = wgu_scr[ss:ss + 22, :, 128 * (2 * R):128 * (2 * R) + 128]
                    dst1 = wgu_scr[ss:ss + 22, :, 128 * (2 * R + 1):128 * (2 * R + 1) + 128]
                    stj.inc(s.dma_start(
                        out=dst0.rearrange("s p c -> p s c"),
                        in_=w[:, 0:F].rearrange("p (s c) -> p s c", c=128)), 16)
                    stj.inc(s.dma_start(
                        out=dst1.rearrange("s p c -> p s c"),
                        in_=w[:, F:2 * F].rearrange("p (s c) -> p s c", c=128)), 16)
                else:
                    stj.inc(s.dma_start(out=wdn_scr[256 * R:256 * R + 128, :], in_=w[:, 0:F]), 16)
                    stj.inc(s.dma_start(out=wdn_scr[256 * R + 128:256 * R + 256, :],
                                       in_=w[:, F:2 * F]), 16)
        for i, (src, dst, W) in enumerate(cjobs):
            s.wait_ge(ldcs[i % 2].h, 16 * (i // 2 + 1))
            stcs[i % 2].inc(s.dma_start(out=dst, in_=cbuf[i % 2][:, :W]), 16)

    @block.vector
    def _(v):
        for i, j in enumerate(jobs):
            _, _, _, _, F = job_src(j)
            v.wait_ge(cv.h, i + 1)
            v.wait_ge(lscs[i % 2].h, 32 * (i // 2 + 1))
            if i >= 2:
                v.wait_ge(sts[i % 2].h, 32 * (i // 2))
            w = swout[i % 2]
            c = si32[i % 2]
            ins = None
            for half in (0, 1):
                if half == 0:
                    v.tensor_scalar(snib[:, :F], c[:, :F], 15, None, ALU.bitwise_and)
                else:
                    v.tensor_scalar(snib[:, :F], c[:, :F], 4, 15,
                                    ALU.logical_shift_right, ALU.bitwise_and)
                v.tensor_scalar(sbem[:, :F], snib[:, :F], 7, 22,
                                ALU.bitwise_and, ALU.logical_shift_left)
                v.tensor_scalar(sts_[:, :F], snib[:, :F], 8, 28,
                                ALU.bitwise_and, ALU.logical_shift_left)
                v.scalar_tensor_tensor(sbits[:, :F], sbem[:, :F], C_EXPBITS, sts_[:, :F],
                                       ALU.add, ALU.add)
                v.tensor_scalar(sg_[:, :F], sbem[:, :F], G_SCALE, 1.0, ALU.mult, ALU.min)
                v.tensor_tensor(sv1[:, :F], sbits[:, :F].bitcast(F32), sg_[:, :F], ALU.mult)
                ins = v.tensor_tensor(w[:, half * F:half * F + F], sv1[:, :F],
                                      ssc[i % 2][:, :F], ALU.mult)
            vp.inc(ins, 1)

    es.close()
    _stage_out(ctx, *sts, *stcs, vp, *lds, *lscs, cv, *ldcs)


def emit_ffnq(ctx, tag, xin_ap, xout_ap, wgu_scr, wdn_scr, out_ap, dbg=None):
    nc, block, sem = ctx["nc"], ctx["block"], ctx["sem"]
    identb = ctx["identb"]

    spool = ctx.setdefault("spool", {})
    def psem(name):
        if name not in spool:
            spool[name] = sem(name)
        s = spool[name]
        s.b = s.v
        return s
    lxs = [psem("FQ_lx0"), psem("FQ_lx1")]
    vself = psem("FQ_vself")
    sself = psem("FQ_sself")
    csv = psem("FQ_csv")
    css = psem("FQ_css")
    uap = psem("FQ_uap")
    utr = psem("FQ_utr")
    uev_v = psem("FQ_uevv")
    uev_s = psem("FQ_uevs")
    lws = [psem("FQ_lw0"), psem("FQ_lw1")]
    mmg = psem("FQ_mmg")
    hevs = psem("FQ_hevs")
    hevv = psem("FQ_hevv")
    ldws = [psem("FQ_ldw0"), psem("FQ_ldw1")]
    mmd = psem("FQ_mmd")
    dev = psem("FQ_dev")
    yst = psem("FQ_yst")
    ystv = psem("FQ_ystv")
    fin = psem("FQ_fin")
    stxs = [sem(f"{tag}_stx0"), sem(f"{tag}_stx1")]
    lx2s = [psem("FQ_lx20"), psem("FQ_lx21")]

    es = contextlib.ExitStack()
    sb = lambda n, sh, dt: es.enter_context(nc.sbuf_tensor(f"{tag}_{n}", sh, dt))
    xt = [sb(f"xt{i}", [128, D], F32) for i in range(2)]
    scr_f = sb("scr_f", [128, D], F32)
    u1tm = [sb(f"u1tm{i}", [128, D], BF16) for i in range(3)]
    colb = [sb(f"col{i}", [128, 8], F32) for i in range(3)]
    bnst = [sb(f"bns{i}", [128, 24], F32) for i in range(2)]
    u1fm = sb("u1fm", [128, GB, PANEL], BF16)
    hfm = sb("hfm", [128, 44, PANEL], BF16)
    wg_t = [sb(f"wg{i}", [128, 2, GB * 128], BF16) for i in range(2)]
    wu_t = [sb(f"wu{i}", [128, 2, GB * 128], BF16) for i in range(2)]
    sil = [sb(f"sil{i}", [128, PANEL], BF16) for i in range(2)]
    yq = [sb(f"yq{i}", [128, D], F32) for i in range(NT)]
    wd_t = [sb(f"wd{i}", [128, 512], BF16) for i in range(2)]
    xt2 = [sb(f"xt2{i}", [128, D], F32) for i in range(2)]
    xo = [sb(f"xo{i}", [128, D], F32) for i in range(2)]
    ycol = [sb(f"ycol{i}", [128, 4], F32) for i in range(2)]

    pes = contextlib.ExitStack()
    ps = [pes.enter_context(nc.psum_tensor(f"{tag}_ps{i}", [128, PANEL], F32))
          for i in range(4)]
    ps_tr_t = [pes.enter_context(nc.psum_tensor(f"{tag}_tr{i}", [128, 128], BF16))
               for i in range(2)]
    ps_tr = [ps_tr_t[0][:], ps_tr_t[1][:]]

    ntile = NPANEL * NT

    @block.sync
    def _(s):
        for p in range(NPANEL):
            tb = p * NT
            for t in range(NT):
                gt = tb + t
                if gt >= 2:
                    s.wait_ge(uap.h, uap.b + (gt - 1))
                lxs[gt % 2].inc(s.dma_start(out=xt[gt % 2][:],
                                            in_=xin_ap[128 * gt:128 * gt + 128, :]), 16)
            for sp2 in range(22):
                it = p * 22 + sp2
                if it >= 2:
                    s.wait_ge(mmg.h, mmg.b + (4 * (it - 1)))
                for half in (0, 1):
                    lws[it % 2].inc(s.dma_start(out=wg_t[it % 2][:, half, :],
                                                in_=wgu_scr[2 * sp2 + half, :, :]), 16)
                    lws[it % 2].inc(s.dma_start(out=wu_t[it % 2][:, half, :],
                                                in_=wgu_scr[44 + 2 * sp2 + half, :, :]), 16)
            for of4 in range(4):
                for s_ in range(44):
                    it = (p * 4 + of4) * 44 + s_
                    if it >= 2:
                        s.wait_ge(mmd.h, mmd.b + (it - 1))
                    ldws[it % 2].inc(s.dma_start(
                        out=wd_t[it % 2][:],
                        in_=wdn_scr[128 * s_:128 * s_ + 128, 512 * of4:512 * of4 + 512]), 16)
            for t in range(NT):
                gt = tb + t
                if gt >= 2:
                    s.wait_ge(fin.h, fin.b + (gt - 1))
                lx2s[gt % 2].inc(s.dma_start(out=xt2[gt % 2][:],
                                             in_=xin_ap[128 * gt:128 * gt + 128, :]), 16)

    @block.scalar
    def _(s):
        for p in range(NPANEL):
            tb = p * NT
            for t in range(NT):
                gt = tb + t
                s.wait_ge(csv.h, csv.b + (gt + 1))
                sself.inc(s.activation(colb[gt % 3][:, 7:8], colb[gt % 3][:, 5:6], AF.Sqrt,
                                       scale=1.0 / D, bias=ctx["ccol"][:, 0:1]), 1)
                s.wait_ge(sself.h, sself.v)
                css.inc(s.copy(ctx["ccol"][:, 6:7], colb[gt % 3][:, 7:8]), 1)
                for g in range(1, GB, 2):
                    s.wait_ge(utr.h, utr.b + (16 * gt + g + 1))
                    uev_s.inc(s.copy(u1fm[:, g, 128 * t:128 * t + 128], ps_tr[g % 2]), 1)
            for sp2 in range(22):
                for half in (0, 1):
                    pp = p * 44 + 2 * sp2 + half
                    s.wait_ge(mmg.h, mmg.b + (2 * pp + 1))
                    if pp >= 2:
                        s.wait_ge(hevv.h, hevv.b + (pp - 1))
                    hevs.inc(s.activation(sil[pp % 2][:], ps[pp % 2][:], AF.Silu), 1)
            for t in range(NT):
                gt = tb + t
                s.wait_ge(ystv.h, ystv.b + (gt + 1))
                sself.inc(s.activation(ycol[gt % 2][:, 1:2], ycol[gt % 2][:, 0:1], AF.Sqrt,
                                       scale=4.0 / D, bias=ctx["ccol"][:, 1:2]), 1)
                s.wait_ge(sself.h, sself.v)
                yst.inc(s.copy(ctx["ccol"][:, 6:7], ycol[gt % 2][:, 1:2]), 1)

    @block.vector
    def _(v):
        for i in range(3):
            v.memset(colb[i][:], 0.0)
        for i in range(2):
            v.memset(ycol[i][:], 0.0)
        for p in range(NPANEL):
            tb = p * NT
            for t in range(NT):
                gt = tb + t
                v.wait_ge(lxs[gt % 2].h, lxs[gt % 2].b + (16 * (gt // 2 + 1)))
                if gt >= 3:
                    v.wait_ge(uap.h, uap.b + (gt - 2))
                cb = colb[gt % 3]
                def vE(op):
                    vself.inc(op, 1)
                    v.wait_ge(vself.h, vself.v)
                    return op
                vE(v.tensor_scalar(scr_f[:], xt[gt % 2][:], 1.0, 0.0, ALU.mult, ALU.add,
                                   accum_out=cb[:, 0:1]))
                vE(v.tensor_scalar(cb[:, 2:3], cb[:, 0:1], -1.0 / D, None, ALU.mult))
                vE(v.scalar_tensor_tensor(scr_f[:], xt[gt % 2][:], cb[:, 2:3], xt[gt % 2][:],
                                          ALU.add, ALU.mult, accum_out=cb[:, 5:6]))
                csv.inc(v.tensor_copy(cb[:, 6:7], cb[:, 5:6]), 1)
                v.wait_ge(css.h, css.b + (gt + 1))
                vE(v.reciprocal(cb[:, 3:4], cb[:, 7:8]))
                if gt >= 3:
                    v.wait_ge(utr.h, utr.b + (16 * (gt - 2)))
                uap.inc(v.tensor_scalar(u1tm[gt % 3][:], xt[gt % 2][:], cb[:, 2:3], cb[:, 3:4],
                                        ALU.add, ALU.mult), 1)
                for g in range(0, GB, 2):
                    v.wait_ge(utr.h, utr.b + (16 * gt + g + 1))
                    uev_v.inc(v.tensor_copy(u1fm[:, g, 128 * t:128 * t + 128], ps_tr[g % 2]), 1)
            for sp2 in range(22):
                for half in (0, 1):
                    pp = p * 44 + 2 * sp2 + half
                    sidx = 2 * sp2 + half
                    v.wait_ge(mmg.h, mmg.b + (2 * pp + 2))
                    v.wait_ge(hevs.h, hevs.b + (pp + 1))
                    hevv.inc(v.tensor_tensor(hfm[:, sidx, :], sil[pp % 2][:], ps[2 + pp % 2][:],
                                             ALU.mult), 1)
            for of4 in range(4):
                for t in range(NT):
                    gt = tb + t
                    v.wait_ge(mmd.h, mmd.b + ((p * 4 + of4) * 44 + 44))
                    dev.inc(v.tensor_copy(yq[t][:, 512 * of4:512 * of4 + 512],
                                          ps[t][:]), 1)
            for t in range(NT):
                gt = tb + t
                if gt >= 2:
                    v.wait_ge(fin.h, fin.b + (gt - 1))
                vself.inc(v.scalar_tensor_tensor(scr_f[:], yq[t][:], 1.0, yq[t][:],
                                                 ALU.mult, ALU.mult,
                                                 accum_out=ycol[gt % 2][:, 0:1]), 1)
                v.wait_ge(vself.h, vself.v)
                ystv.inc(v.tensor_copy(ycol[gt % 2][:, 3:4], ycol[gt % 2][:, 0:1]), 1)
                v.wait_ge(yst.h, yst.b + (gt + 1))
                vself.inc(v.reciprocal(ycol[gt % 2][:, 2:3], ycol[gt % 2][:, 1:2]), 1)
                v.wait_ge(vself.h, vself.v)
                v.wait_ge(lx2s[gt % 2].h, lx2s[gt % 2].b + (16 * (gt // 2 + 1)))
                if gt >= 2:
                    v.wait_ge(stxs[gt % 2].h, 16 * (gt // 2))
                if out_ap is None:
                    ins = v.scalar_tensor_tensor(xo[gt % 2][:], yq[t][:], ycol[gt % 2][:, 2:3],
                                                 xt2[gt % 2][:], ALU.mult, ALU.add)
                else:
                    dst = xo[gt % 2][:].rearrange("p (r c q) -> p r c q", r=8, q=2)
                    ysrc = yq[t][:].rearrange("p (r h c) -> p r h c", r=8, h=2)
                    x2src = xt2[gt % 2][:].rearrange("p (r h c) -> p r h c", r=8, h=2)
                    v.scalar_tensor_tensor(
                        dst[:, :, :, 0], ysrc[:, :, 0, :], ycol[gt % 2][:, 2:3],
                        x2src[:, :, 0, :], ALU.mult, ALU.add)
                    ins = v.scalar_tensor_tensor(
                        dst[:, :, :, 1], ysrc[:, :, 1, :], ycol[gt % 2][:, 2:3],
                        x2src[:, :, 1, :], ALU.mult, ALU.add)
                fin.inc(ins, 1)
        fin.inc(v.tensor_copy(scr_f[:, 0:2], scr_f[:, 0:2]), 1)

    @block.tensor
    def _(t_):
        for p in range(NPANEL):
            tb = p * NT
            if p > 0:
                t_.wait_ge(dev.h, dev.b + (p * 4 * NT))
            for t in range(NT):
                gt = tb + t
                t_.wait_ge(uap.h, uap.b + (gt + 1))
                for g in range(GB):
                    prev = 8 * gt + g // 2
                    if prev > 0:
                        if g % 2 == 0:
                            t_.wait_ge(uev_v.h, uev_v.b + (prev))
                        else:
                            t_.wait_ge(uev_s.h, uev_s.b + (prev))
                    utr.inc(t_.transpose(ps_tr[g % 2], u1tm[gt % 3][:, 128 * g:128 * g + 128],
                                         identb[:]), 1)
            t_.wait_ge(uev_v.h, uev_v.b + (8 * (tb + NT)))
            t_.wait_ge(uev_s.h, uev_s.b + (8 * (tb + NT)))
            for sp2 in range(22):
                it = p * 22 + sp2
                t_.wait_ge(lws[it % 2].h, lws[it % 2].b + (64 * (it // 2 + 1)))
                for half in (0, 1):
                    pp = p * 44 + 2 * sp2 + half
                    if pp >= 2:
                        t_.wait_ge(hevs.h, hevs.b + (pp - 1))
                        t_.wait_ge(hevv.h, hevv.b + (pp - 1))
                    wg = wg_t[it % 2]
                    wu = wu_t[it % 2]
                    for g in range(GB):
                        mm = t_.matmul(ps[pp % 2][:],
                                       wg[:, half, 128 * g:128 * g + 128],
                                       u1fm[:, g, :],
                                       start=(g == 0), stop=(g == GB - 1))
                    mmg.inc(mm, 1)
                    for g in range(GB):
                        mm = t_.matmul(ps[2 + pp % 2][:],
                                       wu[:, half, 128 * g:128 * g + 128],
                                       u1fm[:, g, :],
                                       start=(g == 0), stop=(g == GB - 1))
                    mmg.inc(mm, 1)
            t_.wait_ge(hevv.h, hevv.b + ((p + 1) * 44))
            for of4 in range(4):
                for s_ in range(44):
                    it = (p * 4 + of4) * 44 + s_
                    t_.wait_ge(ldws[it % 2].h, ldws[it % 2].b + (16 * (it // 2 + 1)))
                    if s_ == 0 and (p > 0 or of4 > 0):
                        t_.wait_ge(dev.h, dev.b + ((p * 4 + of4) * NT))
                    for t in range(NT):
                        mm = t_.matmul(ps[t][:],
                                       hfm[:, s_, 128 * t:128 * t + 128],
                                       wd_t[it % 2][:],
                                       start=(s_ == 0), stop=(s_ == 43))
                    mmd.inc(mm, 1)

    @block.gpsimd
    def _(g):
        dst = out_ap if out_ap is not None else xout_ap
        for gt in range(ntile):
            g.wait_ge(fin.h, fin.b + (min(gt + 2, ntile + 1)))
            stxs[gt % 2].inc(g.dma_start(out=dst[128 * gt:128 * gt + 128, :],
                                         in_=xo[gt % 2][:]), 16)
        if dbg is not None:
            g.wait_ge(fin.h, fin.b + (ntile))
            stxs[0].inc(g.dma_start(out=dbg["u1fm"][:], in_=u1fm[:]), 16)
            stxs[0].inc(g.dma_start(out=dbg["hfm"][:], in_=hfm[:]), 16)
            for t in range(NT):
                stxs[0].inc(g.dma_start(out=dbg["yq"][128 * t:128 * t + 128, :], in_=yq[t][:]), 16)
            stxs[0].inc(g.dma_start(out=dbg["cols"][:, 0:8], in_=ctx["ccol"][:]), 16)
            for i in range(3):
                stxs[0].inc(g.dma_start(out=dbg["u1tm"][128 * i:128 * i + 128, :], in_=u1tm[i][:]), 16)
            for i in range(2):
                stxs[0].inc(g.dma_start(out=dbg["xt"][128 * i:128 * i + 128, :], in_=xt[i][:]), 16)
            for i in range(3):
                stxs[0].inc(g.dma_start(out=dbg["cols"][:, 8 + 8 * i:16 + 8 * i], in_=colb[i][:]), 16)
            for i in range(2):
                stxs[0].inc(g.dma_start(out=dbg["cols"][:, 32 + 4 * i:36 + 4 * i], in_=ycol[i][:]), 16)

    es.close()
    pes.close()
    _stage_out(ctx, *stxs, fin, *lxs, *lx2s, *lws, *ldws, mmg, mmd, hevs, hevv, dev,
               uap, uev_v, uev_s, utr, csv, css, yst, ystv)


def emit_b01(ctx, xbuf1, vbuf, win2, zbuf, xbcbuf, dtbuf):
    nc, block, sem = ctx["nc"], ctx["block"], ctx["sem"]
    identb = ctx["identb"]
    ntile = globals().get("B01_TILES", T // 128)
    nstrip = globals().get("B01_STRIPS", 65)

    lxs = [sem("b01_lx0"), sem("b01_lx1")]
    vself = sem("b01_vself")
    sself = sem("b01_sself")
    csv = sem("b01_csv")
    css = sem("b01_css")
    uap = sem("b01_uap")
    utr = sem("b01_utr")
    uev_v = sem("b01_uevv")
    uev_s = sem("b01_uevs")
    stv = [sem("b01_stv0"), sem("b01_stv1")]
    lwi = [sem("b01_lwi0"), sem("b01_lwi1")]
    mmi = sem("b01_mmi")
    evv = sem("b01_evv")
    evs = sem("b01_evs")
    stzv = [sem(f"b01_stzv{i}") for i in range(2)]
    stzs = [sem(f"b01_stzs{i}") for i in range(2)]

    es = contextlib.ExitStack()
    sb = lambda n, sh, dt: es.enter_context(nc.sbuf_tensor(f"b01_{n}", sh, dt))
    xt = [sb(f"xt{i}", [128, D], F32) for i in range(2)]
    scr_f = sb("scr_f", [128, D], F32)
    vt = [sb(f"vt{i}", [128, D], F32) for i in range(2)]
    h1tm = [sb(f"h1tm{i}", [128, D], BF16) for i in range(3)]
    colb = [sb(f"col{i}", [128, 12], F32) for i in range(3)]
    h1fm = es.enter_context(nc.sbuf_tensor("b01_h1fm", [128, GB, T], BF16))
    wstrip = [sb(f"wi{i}", [128, GB * 128], BF16) for i in range(2)]
    ebf_v = [sb(f"ebfv{i}", [128, 512], F32) for i in range(2)]
    ebf_s = [sb(f"ebfs{i}", [128, 512], F32) for i in range(2)]
    ebb_v = [t[:].bitcast(BF16)[:, 0:512] for t in ebf_v]
    ebb_s = [t[:].bitcast(BF16)[:, 0:512] for t in ebf_s]

    pes = contextlib.ExitStack()
    ps_tr_t = [pes.enter_context(nc.psum_tensor(f"b01_tr{i}", [128, 128], BF16))
               for i in range(2)]
    ps_tr = [ps_tr_t[0][:], ps_tr_t[1][:]]
    psA = [pes.enter_context(nc.psum_tensor(f"b01_ps{i}", [128, 512], F32))
           for i in range(2)]

    NCH = max(1, ntile * 128 // 512)

    @block.sync
    def _(s):
        for gt in range(ntile + 1):
            if gt < ntile:
                if gt >= 2:
                    s.wait_ge(uap.h, gt - 1)
                lxs[gt % 2].inc(s.dma_start(out=xt[gt % 2][:],
                                            in_=xbuf1[128 * gt:128 * gt + 128, :]), 16)
            if gt >= 1:
                g2 = gt - 1
                s.wait_ge(uap.h, g2 + 2)
                stv[g2 % 2].inc(s.dma_start(out=vbuf[128 * g2:128 * g2 + 128, :],
                                            in_=vt[g2 % 2][:]), 16)
        for s_ in range(nstrip):
            for c in range(NCH):
                idx = s_ * NCH + c
                k = idx // 2
                if c % 2 == 0:
                    s.wait_ge(evv.h, k + 1)
                    ef, eb, stk = ebf_v[k % 2][:], ebb_v[k % 2], stzv[k % 2]
                else:
                    s.wait_ge(evs.h, k + 1)
                    ef, eb, stk = ebf_s[k % 2][:], ebb_s[k % 2], stzs[k % 2]
                if s_ < 16:
                    stk.inc(s.dma_start(
                        out=zbuf[128 * s_:128 * s_ + 128, 512 * c:512 * c + 512],
                        in_=ef), 16)
                elif s_ < 64:
                    r0 = 128 * (s_ - 16)
                    stk.inc(s.dma_start(
                        out=xbcbuf[r0:r0 + 128, 512 * c:512 * c + 512],
                        in_=eb), 16)
                else:
                    stk.inc(s.dma_start(
                        out=dtbuf[:, 512 * c:512 * c + 512],
                        in_=ef[0:32, :] if hasattr(ef, '__getitem__') else ef), 16)

    b01dbg = ctx.get("b01dbg")

    @block.gpsimd
    def _(g):
        for s_ in range(nstrip):
            if s_ >= 2:
                g.wait_ge(mmi.h, NCH * (s_ - 1))
            lwi[s_ % 2].inc(g.dma_start(out=wstrip[s_ % 2][:], in_=win2[s_]), 16)
        if b01dbg is not None:
            g.wait_ge(uap.h, ntile + 1)
            for i in range(3):
                g.dma_start(out=b01dbg[:, 12 * i:12 * i + 12], in_=colb[i][:]).then_inc(
                    lwi[0].h, 16)
            lwi[0].v += 48

    @block.vector
    def _(v):
        for i in range(3):
            v.memset(colb[i][:], 0.0)
        for gt in range(ntile):
            v.wait_ge(lxs[gt % 2].h, 16 * (gt // 2 + 1))
            if gt >= 3:
                v.wait_ge(uap.h, gt - 2)
            cb = colb[gt % 3]
            def vE(op):
                vself.inc(op, 1)
                v.wait_ge(vself.h, vself.v)
                return op
            vE(v.tensor_scalar(scr_f[:], xt[gt % 2][:], 1.0, 0.0, ALU.mult, ALU.add,
                               accum_out=cb[:, 0:1]))
            vE(v.tensor_scalar(cb[:, 2:3], cb[:, 0:1], -1.0 / D, None, ALU.mult))
            vE(v.scalar_tensor_tensor(scr_f[:], xt[gt % 2][:], cb[:, 2:3], xt[gt % 2][:],
                                      ALU.add, ALU.mult, accum_out=cb[:, 5:6]))
            csv.inc(v.tensor_copy(scr_f[:, 0:1], cb[:, 5:6]), 1)
            v.wait_ge(css.h, 2 * gt + 1)
            vE(v.reciprocal(cb[:, 3:4], cb[:, 6:7]))
            vE(v.tensor_tensor(cb[:, 11:12], cb[:, 5:6], cb[:, 3:4], ALU.mult))
            vE(v.tensor_tensor(cb[:, 7:8], cb[:, 11:12], cb[:, 3:4], ALU.mult))
            csv.inc(v.tensor_copy(scr_f[:, 0:1], cb[:, 7:8]), 1)
            v.wait_ge(css.h, 2 * gt + 2)
            vE(v.reciprocal(cb[:, 9:10], cb[:, 8:9]))
            vE(v.tensor_tensor(cb[:, 10:11], cb[:, 3:4], cb[:, 9:10], ALU.mult))
            if gt >= 2:
                v.wait_ge(stv[gt % 2].h, 16 * (gt // 2))
            v.tensor_scalar(vt[gt % 2][:], xt[gt % 2][:], cb[:, 2:3], cb[:, 3:4],
                            ALU.add, ALU.mult)
            if gt >= 3:
                v.wait_ge(utr.h, 16 * (gt - 2))
            v.tensor_scalar(scr_f[:], xt[gt % 2][:], cb[:, 2:3], cb[:, 10:11],
                            ALU.add, ALU.mult)
            uap.inc(v.tensor_tensor(h1tm[gt % 3][:], scr_f[:], ctx["n1w"][:], ALU.mult), 1)
            for g in range(0, GB, 2):
                v.wait_ge(utr.h, 16 * gt + g + 1)
                uev_v.inc(v.tensor_copy(h1fm[:, g, 128 * gt:128 * gt + 128], ps_tr[g % 2]), 1)
        uap.inc(v.tensor_copy(scr_f[:, 0:2], scr_f[:, 0:2]), 1)
        for s_ in range(nstrip):
            for c in range(0, NCH, 2):
                idx = s_ * NCH + c
                k = idx // 2
                v.wait_ge(mmi.h, idx + 1)
                if k >= 2:
                    v.wait_ge(stzv[k % 2].h, 16 * (k // 2))
                dst = ebf_v[k % 2][:] if (s_ < 16 or s_ == 64) else ebb_v[k % 2]
                evv.inc(v.tensor_copy(dst, psA[c % 2][:]), 1)
        evv.inc(v.tensor_copy(scr_f[:, 0:2], scr_f[:, 0:2]), 1)

    @block.scalar
    def _(s):
        for gt in range(ntile):
            s.wait_ge(csv.h, 2 * gt + 1)
            sself.inc(s.activation(colb[gt % 3][:, 6:7], colb[gt % 3][:, 5:6], AF.Sqrt,
                                   scale=1.0 / D, bias=ctx["ccol"][:, 0:1]), 1)
            s.wait_ge(sself.h, sself.v)
            css.inc(s.copy(ctx["ccol"][:, 6:7], colb[gt % 3][:, 6:7]), 1)
            s.wait_ge(csv.h, 2 * gt + 2)
            sself.inc(s.activation(colb[gt % 3][:, 8:9], colb[gt % 3][:, 7:8], AF.Sqrt,
                                   scale=1.0 / D, bias=ctx["ccol"][:, 2:3]), 1)
            s.wait_ge(sself.h, sself.v)
            css.inc(s.copy(ctx["ccol"][:, 6:7], colb[gt % 3][:, 8:9]), 1)
            for g in range(1, GB, 2):
                s.wait_ge(utr.h, 16 * gt + g + 1)
                uev_s.inc(s.copy(h1fm[:, g, 128 * gt:128 * gt + 128], ps_tr[g % 2]), 1)
        for s_ in range(nstrip):
            for c in range(1, NCH, 2):
                idx = s_ * NCH + c
                k = idx // 2
                s.wait_ge(mmi.h, idx + 1)
                if k >= 2:
                    s.wait_ge(stzs[k % 2].h, 16 * (k // 2))
                dst = ebf_s[k % 2][:] if (s_ < 16 or s_ == 64) else ebb_s[k % 2]
                evs.inc(s.copy(dst, psA[c % 2][:]), 1)
        evs.inc(s.copy(ctx["ccol"][:, 4:5], ctx["ccol"][:, 5:6]), 1)

    @block.tensor
    def _(t_):
        for gt in range(ntile):
            t_.wait_ge(uap.h, gt + 1)
            for g in range(GB):
                prev = 8 * gt + g // 2
                if prev > 0:
                    if g % 2 == 0:
                        t_.wait_ge(uev_v.h, prev)
                    else:
                        t_.wait_ge(uev_s.h, prev)
                utr.inc(t_.transpose(ps_tr[g % 2], h1tm[gt % 3][:, 128 * g:128 * g + 128],
                                     identb[:]), 1)
        t_.wait_ge(uev_v.h, 8 * ntile)
        t_.wait_ge(uev_s.h, 8 * ntile)
        for s_ in range(nstrip):
            t_.wait_ge(lwi[s_ % 2].h, 16 * (s_ // 2 + 1))
            for c in range(NCH):
                idx = s_ * NCH + c
                if idx >= 2:
                    if c % 2 == 0:
                        t_.wait_ge(evv.h, idx // 2)
                    else:
                        t_.wait_ge(evs.h, (idx - 1) // 2)
                for g in range(GB):
                    mm = t_.matmul(psA[c % 2][:],
                                   wstrip[s_ % 2][:, 128 * g:128 * g + 128],
                                   h1fm[:, g, 512 * c:512 * c + 512],
                                   start=(g == 0), stop=(g == GB - 1))
                mmi.inc(mm, 1)

    es.close()
    pes.close()
    _stage_out(ctx, *stzv, *stzs, *stv, *lxs, *lwi, mmi, evv, evs, uap, utr,
               uev_v, uev_s, csv, css)
    return None


def emit_b2(ctx, zbuf, xbcbuf, dtbuf, prodbuf):
    nc, block, sem = ctx["nc"], ctx["block"], ctx["sem"]
    identf, identb = ctx["identf"], ctx["identb"]
    ones, cw, cbias, zb, dbc, ccol = (ctx["ones"], ctx["cw"], ctx["cb"], ctx["zb"],
                                      ctx["dbc"], ctx["ccol"])
    trilu = ctx["trilu"]
    CH = 256
    NCH2 = L // CH

    tmm, vv, ss, gg = (ctx["geng"]["t"], ctx["geng"]["v"], ctx["geng"]["s"],
                       ctx["geng"]["g"])
    ldz = ctx["gdma"][0:2]
    ldx = ctx["gdma"][2:4]
    ldd = ctx["gdma"][4:6]
    stp = ctx["gdma"][6:8]

    es = contextlib.ExitStack()
    sb = lambda n, sh, dt: es.enter_context(nc.sbuf_tensor(f"b2_{n}", sh, dt))
    Hst = sb("Hst", [128, (NH // 2) * 64], F32)
    dbgt = sb("dbgt", [32, 512], F32) if ctx.get("b2dbg") else None
    zt = [sb(f"zt{i}", [128, GB, CH], F32) for i in range(2)]
    xbc = [sb(f"xbc{i}", [128, 48, CH + 4], BF16) for i in range(2)]
    dtt = [sb(f"dtt{i}", [32, CH], F32) for i in range(2)]
    convA = sb("convA", [128, 48, 128], F32)
    convA2 = sb("convA2", [128, 48, 128], F32)
    convB = sb("convB", [128, 48, 128], BF16)
    xT = sb("xT", [128, GB * 128], BF16)
    BT = sb("BT", [128, GB * 128], BF16)
    Bd = [sb(f"Bd{i}", [128, 64], BF16) for i in range(2)]
    cum = sb("cum", [32, 128], F32)
    sp_t = sb("sp", [32, 128], F32)
    spe = sb("spe", [32, 128], F32)
    spl = sb("spl", [32, 128], F32)
    spr = sb("spr", [32, 128], F32)
    cum_tok = sb("cum_tok", [128, 32], F32)
    ncum_tok = sb("ncum_tok", [128, 32], F32)
    dec_fm = sb("dec_fm", [32, 128], F32)
    dec_tok = sb("dec_tok", [128, 32], F32)
    acol = sb("acol", [32, 1], F32)
    abct = sb("abct", [32, 128], F32)
    A_bc = sb("A_bc", [128, 32], F32)
    T1tmp = [sb(f"T1tmp{i}", [128, 128], F32) for i in range(2)]
    seg = [sb(f"seg{i}", [128, 128], F32) for i in range(2)]
    segE = [sb(f"segE{i}", [128, 128], F32) for i in range(2)]
    sc_bf = [sb(f"sc_bf{i}", [128, 128], BF16) for i in range(2)]
    sc_m = [sb(f"sc_m{i}", [128, 128], BF16) for i in range(2)]
    E_sb = [sb(f"E_sb{i}", [128, 128], F32) for i in range(2)]
    Cs = [sb(f"Cs{i}", [128, 128], F32) for i in range(2)]
    S_sb = sb("S_sb", [128, 64], F32)
    y_sb = [sb(f"y_sb{i}", [128, 128], BF16) for i in range(2)]
    y_fm = sb("y_fm", [128, GB, CH], BF16)
    silz = [sb(f"silz{i}", [128, CH], BF16) for i in range(2)]
    prodt = sb("prodt", [128, GB, CH], BF16)

    pes = contextlib.ExitStack()
    psal = lambda n, sh, dt=F32: pes.enter_context(nc.psum_tensor(f"b2_{n}", sh, dt))
    ps_cbS = [psal(f"cbS{i}", [128, 192]) for i in range(2)]
    ps_y = [psal(f"y{i}", [128, 128]) for i in range(2)]
    ps_tr = [psal(f"tr{i}", [128, 128], BF16) for i in range(2)]
    ps_trf = [psal(f"trf{i}", [128, 128]) for i in range(2)]

    dep = {}

    def W(e, key):
        d = dep.get(key)
        if d:
            if isinstance(d, list):
                for s_, v_ in d:
                    e.wait_ge(s_.h, v_)
            else:
                e.wait_ge(d[0].h, d[1])

    for b in range(SEQ_PER_CORE):
        @block.vector
        def _(v):
            W(v, "H")
            vv.inc(v.memset(Hst[:], 0.0), 1)
        dep["H"] = (vv, vv.v)
        dep["Hread"] = (vv, vv.v)

        if b == 0 and ctx.get("b2dbg"):
            dsem2 = ctx["gdma"][11]
            mk = (vv, vv.v)

            @block.sync
            def _(s):
                s.wait_ge(mk[0].h, mk[1])
                dsem2.inc(s.dma_start(out=ctx["b2dbg"]["Hst0"][:], in_=Hst[:]), 16)
            dep["Hread"] = [dep["Hread"], (dsem2, dsem2.v)]

        for cc in range(NCH2):
            it = b * NCH2 + cc
            t0 = b * L + cc * CH
            par = it % 2
            nz = it // 2 + 1

            @block.sync
            def _(s):
                W(s, f"zt{par}")
                for gq in range(4):
                    ldz[par].inc(s.dma_start(
                        out=zt[par][:, 4 * gq:4 * gq + 4, :],
                        in_=zbuf.rearrange("(g p) t -> p g t", p=128)
                            [:, 4 * gq:4 * gq + 4, t0:t0 + CH]), 16)
                W(s, f"xbc{par}")
                if cc == 0:
                    for gq in range(4):
                        ldx[par].inc(s.dma_start(
                            out=xbc[par][:, 12 * gq:12 * gq + 12, 4:4 + CH],
                            in_=xbcbuf.rearrange("(g p) t -> p g t", p=128)
                                [:, 12 * gq:12 * gq + 12, t0:t0 + CH]), 16)
                else:
                    for gq in range(4):
                        ldx[par].inc(s.dma_start(
                            out=xbc[par][:, 12 * gq:12 * gq + 12, 1:4 + CH],
                            in_=xbcbuf.rearrange("(g p) t -> p g t", p=128)
                                [:, 12 * gq:12 * gq + 12, t0 - 3:t0 + CH]), 16)
                W(s, f"dtt{par}")
                ldd[par].inc(s.dma_start(out=dtt[par][:], in_=dtbuf[:, t0:t0 + CH]), 16)
                dep[f"zld{par}"] = (ldz[par], ldz[par].v)
                dep[f"xld{par}"] = (ldx[par], ldx[par].v)
                dep[f"dld{par}"] = (ldd[par], ldd[par].v)

            if cc == 0:
                @block.vector
                def _(v):
                    W(v, f"xld{par}")
                    vv.inc(v.memset(xbc[par][:, :, 0:4], 0.0), 1)

            xbc_deps = []
            zt_deps = []
            dtt_deps = []

            for sc in range(2):
                j0 = sc * 128

                if b == 0 and cc == 0 and sc == 1 and ctx.get("b2dbg"):
                    dsem2 = ctx["gdma"][11]
                    mk = (vv, vv.v)

                    @block.sync
                    def _(s):
                        s.wait_ge(mk[0].h, mk[1])
                        dsem2.inc(s.dma_start(out=ctx["b2dbg"]["Hmid"][:], in_=Hst[:]), 16)
                    hr = dep.get("Hread")
                    dep["Hread"] = (hr if isinstance(hr, list) else [hr]) + [(dsem2, dsem2.v)]

                @block.vector
                def _(v):
                    W(v, f"xld{par}")
                    W(v, "convB")
                    for fb in range(48):
                        xs = xbc[par][:, fb, :]
                        v.tensor_scalar(convA[:, fb, :], xs[:, j0 + 1:j0 + 129],
                                        cw[:, fb, 0:1], cbias[:, fb:fb + 1],
                                        ALU.mult, ALU.add)
                        v.scalar_tensor_tensor(convA2[:, fb, :], xs[:, j0 + 2:j0 + 130],
                                               cw[:, fb, 1:2], convA[:, fb, :],
                                               ALU.mult, ALU.add)
                        v.scalar_tensor_tensor(convA[:, fb, :], xs[:, j0 + 3:j0 + 131],
                                               cw[:, fb, 2:3], convA2[:, fb, :],
                                               ALU.mult, ALU.add)
                        ins = v.scalar_tensor_tensor(convB[:, fb, :],
                                               xs[:, j0 + 4:j0 + 132],
                                               cw[:, fb, 3:4], convA[:, fb, :],
                                               ALU.mult, ALU.add)
                    vv.inc(ins, 1)
                convB_ready = (vv, vv.v)
                xbc_deps.append(convB_ready)

                @block.scalar
                def _(s):
                    W(s, f"dld{par}")
                    W(s, "spbufs")
                    dts = dtt[par][:, j0:j0 + 128]
                    cap = (b == 0 and cc == 0 and sc == 0 and dbgt is not None)
                    if cap:
                        ss.inc(s.copy(dbgt[:, 0:128], dts), 1)
                        s.wait_ge(ss.h, ss.v)
                    ss.inc(s.activation(spe[:], dts, AF.Abs), 1)
                    s.wait_ge(ss.h, ss.v)
                    ss.inc(s.activation(spl[:], spe[:], AF.Exp, scale=-1.0), 1)
                    s.wait_ge(ss.h, ss.v)
                    ss.inc(s.activation(spe[:], spl[:], AF.Ln,
                                        bias=ccol[0:32, 3:4]), 1)
                    s.wait_ge(ss.h, ss.v)
                    ss.inc(s.activation(spr[:], dts, AF.Relu), 1)
                    s.wait_ge(ss.h, ss.v)
                    if cap:
                        ss.inc(s.copy(dbgt[:, 128:256], spe[:]), 1)
                        s.wait_ge(ss.h, ss.v)
                        ss.inc(s.copy(dbgt[:, 256:384], spr[:]), 1)
                        s.wait_ge(ss.h, ss.v)
                        ss.inc(s.copy(dbgt[:, 384:512], spl[:]), 1)
                        s.wait_ge(ss.h, ss.v)
                    ss.inc(s.copy(ccol[0:32, 6:7], spr[:, 0:1]), 1)
                sp_ready = (ss, ss.v)
                dtt_deps.append(sp_ready)

                @block.vector
                def _(v):
                    v.wait_ge(sp_ready[0].h, sp_ready[1])
                    W(v, "cumbufs")
                    vv.inc(v.tensor_tensor(sp_t[:], spr[:], spe[:], ALU.add), 1)
                    v.wait_ge(vv.h, vv.v)
                    vv.inc(v.tensor_tensor_scan(cum[:], ones[0:32, :], sp_t[:],
                                                0.0, ALU.mult, ALU.subtract), 1)
                    v.wait_ge(vv.h, vv.v)
                    vv.inc(v.tensor_scalar(dec_fm[:], cum[:], cum[:, 127:128], None,
                                           ALU.subtract), 1)
                    v.wait_ge(vv.h, vv.v)
                dep["spbufs"] = (vv, vv.v)
                cum_ready = (vv, vv.v)

                @block.tensor
                def _(t_):
                    t_.wait_ge(cum_ready[0].h, cum_ready[1])
                    W(t_, "trfuse0")
                    tmm.inc(t_.transpose(ps_trf[0][0:128, 0:32], cum[:], identf[0:32, 0:32]), 1)
                cumtr = (tmm, tmm.v)

                @block.vector
                def _(v):
                    v.wait_ge(cumtr[0].h, cumtr[1])
                    W(v, "cumtokbufs")
                    vv.inc(v.tensor_copy(cum_tok[:], ps_trf[0][0:128, 0:32]), 1)
                    v.wait_ge(vv.h, vv.v)
                    vv.inc(v.tensor_scalar(ncum_tok[:], cum_tok[:], -1.0, None,
                                           ALU.mult), 1)
                    v.wait_ge(vv.h, vv.v)
                dep["trfuse0"] = (vv, vv.v)
                cumtok_ready = (vv, vv.v)

                @block.scalar
                def _(s):
                    s.wait_ge(cum_ready[0].h, cum_ready[1])
                    ss.inc(s.activation(dec_fm[:], dec_fm[:], AF.Exp, scale=-1.0), 1)
                    s.wait_ge(ss.h, ss.v)
                    ss.inc(s.activation(acol[:], cum[:, 127:128], AF.Exp), 1)
                    s.wait_ge(ss.h, ss.v)
                    ss.inc(s.copy(ccol[0:32, 6:7], acol[:, 0:1]), 1)
                dec_ready = (ss, ss.v)

                @block.vector
                def _(v):
                    v.wait_ge(dec_ready[0].h, dec_ready[1])
                    vv.inc(v.tensor_scalar(abct[:], ones[0:32, :], acol[:, 0:1],
                                           None, ALU.mult), 1)
                    v.wait_ge(vv.h, vv.v)
                abct_ready = (vv, vv.v)

                @block.tensor
                def _(t_):
                    t_.wait_ge(abct_ready[0].h, abct_ready[1])
                    W(t_, "trfuse1")
                    tmm.inc(t_.transpose(ps_trf[1][0:128, 0:32], abct[:],
                                         identf[0:32, 0:32]), 1)
                abctr = (tmm, tmm.v)

                @block.vector
                def _(v):
                    v.wait_ge(abctr[0].h, abctr[1])
                    vv.inc(v.tensor_copy(A_bc[:], ps_trf[1][0:128, 0:32]), 1)
                    v.wait_ge(vv.h, vv.v)
                dep["trfuse1"] = (vv, vv.v)
                abc_ready = (vv, vv.v)

                @block.tensor
                def _(t_):
                    t_.wait_ge(dec_ready[0].h, dec_ready[1])
                    W(t_, "trfuse0")
                    tmm.inc(t_.transpose(ps_trf[0][0:128, 0:32], dec_fm[:], identf[0:32, 0:32]), 1)
                dectr = (tmm, tmm.v)

                @block.vector
                def _(v):
                    v.wait_ge(dectr[0].h, dectr[1])
                    vv.inc(v.tensor_copy(dec_tok[:], ps_trf[0][0:128, 0:32]), 1)
                    v.wait_ge(vv.h, vv.v)
                dep["trfuse0"] = (vv, vv.v)
                dectok_ready = (vv, vv.v)

                @block.tensor
                def _(t_):
                    t_.wait_ge(convB_ready[0].h, convB_ready[1])
                    for g in range(GB):
                        W(t_, f"tr{g % 2}")
                        tmm.inc(t_.transpose(ps_tr[g % 2][:], convB[:, g, :],
                                             identb[:]), 1)
                        dep[f"trmm{g % 2}"] = (tmm, tmm.v)

                        @block.vector
                        def _(v, g=g):
                            v.wait_ge(dep[f"trmm{g % 2}"][0].h, dep[f"trmm{g % 2}"][1])
                            vv.inc(v.tensor_copy(xT[:, 128 * g:128 * g + 128],
                                                 ps_tr[g % 2][:]), 1)
                            dep[f"tr{g % 2}"] = (vv, vv.v)
                    for g in range(GB):
                        W(t_, f"tr{g % 2}")
                        tmm.inc(t_.transpose(ps_tr[g % 2][:], convB[:, 16 + g, :],
                                             identb[:]), 1)
                        dep[f"trmm{g % 2}"] = (tmm, tmm.v)

                        @block.vector
                        def _(v, g=g):
                            v.wait_ge(dep[f"trmm{g % 2}"][0].h, dep[f"trmm{g % 2}"][1])
                            vv.inc(v.tensor_copy(BT[:, 128 * g:128 * g + 128],
                                                 ps_tr[g % 2][:]), 1)
                            dep[f"tr{g % 2}"] = (vv, vv.v)
                xbt_ready = (vv, vv.v)

                for q in range(NH // 2):
                    p2 = q % 2
                    for h in (2 * q, 2 * q + 1):
                        r = h % 2
                        po = r * 64
                        bb = 16 + q
                        cbk = 32 + q

                        @block.tensor
                        def _(t_):
                            t_.wait_ge(convB_ready[0].h, convB_ready[1])
                            W(t_, f"cbuse{r}")
                            W(t_, f"Sevac{r}")
                            tmm.inc(t_.matmul(ps_cbS[r][:, 0:128],
                                              convB[po:po + 64, bb, :],
                                              convB[po:po + 64, cbk, :],
                                              start=True, stop=True), 1)
                        cb_done = (tmm, tmm.v)

                        @block.vector
                        def _(v):
                            v.wait_ge(cumtok_ready[0].h, cumtok_ready[1])
                            W(v, f"T1use{r}")
                            vv.inc(v.tensor_scalar(T1tmp[r][:], ones[:],
                                                   cum_tok[:, h:h + 1],
                                                   None, ALU.mult), 1)
                        t1tmp_ready = (vv, vv.v)

                        @block.tensor
                        def _(t_):
                            t_.wait_ge(t1tmp_ready[0].h, t1tmp_ready[1])
                            W(t_, f"trfuse{r}")
                            tmm.inc(t_.transpose(ps_trf[r][:], T1tmp[r][:], identf[:]), 1)
                        t1_done = (tmm, tmm.v)
                        dep[f"T1use{r}"] = t1_done

                        @block.vector
                        def _(v):
                            v.wait_ge(t1_done[0].h, t1_done[1])
                            W(v, f"seguse{r}")
                            vv.inc(v.tensor_scalar(seg[r][:], ps_trf[r][:],
                                                   ncum_tok[:, h:h + 1],
                                                   0.0, ALU.add, ALU.min), 1)
                        seg_ready = (vv, vv.v)

                        @block.scalar
                        def _(s):
                            s.wait_ge(seg_ready[0].h, seg_ready[1])
                            W(s, f"segEuse{r}")
                            ss.inc(s.activation(segE[r][:], seg[r][:], AF.Exp), 1)
                            if r == 0:
                                W(s, f"Euse{p2}")
                            ss.inc(s.activation(E_sb[p2][po:po + 64, :],
                                                ps_trf[r][po:po + 64, :], AF.Exp), 1)
                        segE_ready = (ss, ss.v)
                        dep[f"seguse{r}"] = segE_ready
                        dep[f"trfuse{r}"] = [seg_ready, segE_ready]

                        @block.vector
                        def _(v):
                            v.wait_ge(dectok_ready[0].h, dectok_ready[1])
                            W(v, f"Bduse{r}")
                            vv.inc(v.tensor_scalar(Bd[r][:], BT[:, 64 * h:64 * h + 64],
                                                   dec_tok[:, h:h + 1], None,
                                                   ALU.mult), 1)
                            v.wait_ge(cb_done[0].h, cb_done[1])
                            vv.inc(v.tensor_tensor(sc_bf[r][:], ps_cbS[r][:, 0:128],
                                                   trilu[:], ALU.mult), 1)
                            dep[f"cbuse{r}"] = (vv, vv.v)
                            v.wait_ge(segE_ready[0].h, segE_ready[1])
                            W(v, f"scmuse{r}")
                            vv.inc(v.tensor_tensor(sc_m[r][:], segE[r][:],
                                                   sc_bf[r][:], ALU.mult), 1)
                            dep[f"segEuse{r}"] = (vv, vv.v)
                        scm_ready = (vv, vv.v)
                        cb_masked = dep[f"cbuse{r}"]

                        @block.tensor
                        def _(t_):
                            t_.wait_ge(cb_masked[0].h, cb_masked[1])
                            tmm.inc(t_.matmul(ps_cbS[r][0:64, 128:192], Bd[r][:],
                                              xT[:, 64 * h:64 * h + 64],
                                              start=True, stop=True), 1)
                            dep[f"Bduse{r}"] = (tmm, tmm.v)
                            t_.wait_ge(scm_ready[0].h, scm_ready[1])
                            t_.wait_ge(xbt_ready[0].h, xbt_ready[1])
                            W(t_, f"yps{r}")
                            tmm.inc(t_.matmul(ps_y[r][:, 0:64], sc_m[r][:],
                                              xT[:, 64 * h:64 * h + 64],
                                              start=True, stop=False), 1)
                            dep[f"scmuse{r}"] = (tmm, tmm.v)
                        smm_done = (tmm, tmm.v)

                        @block.scalar
                        def _(s):
                            s.wait_ge(smm_done[0].h, smm_done[1])
                            W(s, "Ssbuse")
                            ss.inc(s.copy(S_sb[po:po + 64, :],
                                          ps_cbS[r][0:64, 128:192]), 1)
                        dep[f"Sevac{r}"] = (ss, ss.v)
                    sevac_done = (ss, ss.v)

                    @block.vector
                    def _(v):
                        v.wait_ge(segE_ready[0].h, segE_ready[1])
                        W(v, f"Csuse{p2}")
                        vv.inc(v.tensor_tensor(Cs[p2][:], convB[:, 32 + q, :],
                                               E_sb[p2][:], ALU.mult), 1)
                        dep[f"Euse{p2}"] = (vv, vv.v)
                    cs_ready = (vv, vv.v)

                    @block.tensor
                    def _(t_):
                        t_.wait_ge(cs_ready[0].h, cs_ready[1])
                        W(t_, "Hread")
                        for h in (2 * q, 2 * q + 1):
                            po = (h % 2) * 64
                            tmm.inc(t_.matmul(ps_y[h % 2][:, 0:64],
                                              Cs[p2][po:po + 64, :],
                                              Hst[po:po + 64, 64 * q:64 * q + 64],
                                              start=False, stop=True), 1)
                        dep[f"Csuse{p2}"] = (tmm, tmm.v)
                    yfull_done = (tmm, tmm.v)

                    @block.vector
                    def _(v):
                        v.wait_ge(yfull_done[0].h, yfull_done[1])
                        v.wait_ge(sevac_done[0].h, sevac_done[1])
                        v.wait_ge(abc_ready[0].h, abc_ready[1])
                        for h in (2 * q, 2 * q + 1):
                            po = (h % 2) * 64
                            vv.inc(v.scalar_tensor_tensor(
                                Hst[po:po + 64, 64 * q:64 * q + 64],
                                Hst[po:po + 64, 64 * q:64 * q + 64],
                                A_bc[po:po + 64, h:h + 1], S_sb[po:po + 64, :],
                                ALU.mult, ALU.add), 1)
                        v.wait_ge(vv.h, vv.v)
                        dep["Ssbuse"] = (vv, vv.v)
                        dep["Hread"] = (vv, vv.v)
                        W(v, f"ysbuse{p2}")
                        for h, po2 in ((2 * q, 0), (2 * q + 1, 64)):
                            vv.inc(v.scalar_tensor_tensor(
                                y_sb[p2][:, po2:po2 + 64],
                                xT[:, 64 * h:64 * h + 64],
                                dbc[:, h:h + 1],
                                ps_y[h % 2][:, 0:64],
                                ALU.mult, ALU.add), 1)
                        dep["yps0"] = (vv, vv.v)
                        dep["yps1"] = (vv, vv.v)
                    ysb_ready = (vv, vv.v)

                    @block.tensor
                    def _(t_):
                        t_.wait_ge(ysb_ready[0].h, ysb_ready[1])
                        W(t_, f"tr{p2}")
                        tmm.inc(t_.transpose(ps_tr[p2][:], y_sb[p2][:], identb[:]), 1)
                        dep[f"ysbuse{p2}"] = (tmm, tmm.v)
                    ytr_done = (tmm, tmm.v)

                    @block.vector
                    def _(v):
                        v.wait_ge(ytr_done[0].h, ytr_done[1])
                        vv.inc(v.tensor_copy(y_fm[:, q, j0:j0 + 128], ps_tr[p2][:]), 1)
                    dep[f"tr{p2}"] = (vv, vv.v)

                dep["convB"] = [(tmm, tmm.v), (vv, vv.v)]

            yfm_ready = (vv, vv.v)
            for g in range(GB):
                @block.scalar
                def _(s):
                    W(s, f"zld{par}")
                    W(s, f"silz{g % 2}")
                    ss.inc(s.activation(silz[g % 2][:], zt[par][:, g, :], AF.Silu,
                                        bias=zb[:, g:g + 1]), 1)
                sil_done = (ss, ss.v)
                if g == GB - 1:
                    zt_deps.append(sil_done)

                @block.vector
                def _(v, g=g):
                    v.wait_ge(sil_done[0].h, sil_done[1])
                    W(v, "prodt")
                    vv.inc(v.tensor_tensor(prodt[:, g, :], y_fm[:, g, :],
                                           silz[g % 2][:], ALU.mult), 1)
                    dep[f"silz{g % 2}"] = (vv, vv.v)
            prod_ready = (vv, vv.v)

            @block.sync
            def _(s):
                s.wait_ge(prod_ready[0].h, prod_ready[1])
                W(s, f"stp{par}")
                for gq in range(4):
                    stp[par].inc(s.dma_start(
                        out=prodbuf.rearrange("(g p) t -> p g t", p=128)
                            [:, 4 * gq:4 * gq + 4, t0:t0 + CH],
                        in_=prodt[:, 4 * gq:4 * gq + 4, :]), 16)
            dep["prodt"] = (stp[par], stp[par].v)
            dep[f"stp{par}"] = (stp[par], stp[par].v)

            dep[f"zt{par}"] = zt_deps
            dep[f"xbc{par}"] = xbc_deps
            dep[f"dtt{par}"] = dtt_deps
            dep["convB"] = [(tmm, tmm.v), (vv, vv.v)]
            dep["cumbufs"] = [(vv, vv.v), (ss, ss.v)]
            dep["cumtokbufs"] = [(vv, vv.v), (ss, ss.v)]

            if b == 0 and cc == 0 and ctx.get("b2dbg"):
                d2 = ctx["b2dbg"]
                dsem = ctx["gdma"][10]
                marks = [(vv, vv.v), (ss, ss.v), (tmm, tmm.v), (gg, gg.v)]

                @block.sync
                def _(s):
                    for sm, mv in marks:
                        s.wait_ge(sm.h, mv)
                    dsem.inc(s.dma_start(out=d2["Hst"][:], in_=Hst[:]), 16)
                    dsem.inc(s.dma_start(out=d2["y_fm"][:], in_=y_fm[:]), 16)
                    dsem.inc(s.dma_start(out=d2["xT"][:], in_=xT[:]), 16)
                    dsem.inc(s.dma_start(out=d2["A_bc"][:], in_=A_bc[:]), 16)
                    dsem.inc(s.dma_start(out=d2["S_sb"][:], in_=S_sb[:]), 16)
                    dsem.inc(s.dma_start(out=d2["cum_tok"][:], in_=cum_tok[:]), 16)
                    dsem.inc(s.dma_start(out=d2["prodt"][:], in_=prodt[:]), 16)
                dep["b2dump"] = (dsem, dsem.v)
                for k in ("H", "trfuse0", "trfuse1", "prodt", "convB", "ysbuse0",
                          "ysbuse1", "scmuse0", "scmuse1", "Csuse0", "Csuse1",
                          "Euse0", "Euse1", "yps0", "yps1", "Ssbuse", "Sevac0",
                          "Sevac1", "Hread", "cumtokbufs", "cumbufs"):
                    old = dep.get(k)
                    add = (dsem, dsem.v)
                    if old is None:
                        dep[k] = add
                    elif isinstance(old, list):
                        dep[k] = old + [add]
                    else:
                        dep[k] = [old, add]

    es.close()
    pes.close()
    _stage_out(ctx, tmm, vv, ss, gg, *ldz, *ldx, *ldd, *stp, ctx["gdma"][10], ctx["gdma"][11])


def emit_b3(ctx, prodbuf, woutT, vbuf, xbuf1, resbuf, h2fmbuf):
    nc, block, sem = ctx["nc"], ctx["block"], ctx["sem"]
    identb = ctx["identb"]
    ntile = T // 128

    lwo = ctx["gswd"][0]
    lps = ctx["gdma"][0:2]
    lvs = ctx["gdma"][2:4]
    lx1 = ctx["gdma"][4:6]
    mmo = ctx["geng"]["g"]
    tmm, vv, ss = ctx["geng"]["t"], ctx["geng"]["v"], ctx["geng"]["s"]
    sts = ctx["gdma"][6:8]
    sth = ctx["gdma"][8:10]

    es = contextlib.ExitStack()
    sb = lambda n, sh, dt: es.enter_context(nc.sbuf_tensor(f"b3_{n}", sh, dt))
    wout_sb = sb("wout", [128, GB, D], BF16)
    prodt = [sb(f"prod{i}", [128, GB, 128], BF16) for i in range(2)]
    vt = [sb(f"vt{i}", [128, D], F32) for i in range(2)]
    x1t = [sb(f"x1t{i}", [128, D], F32) for i in range(2)]
    a_t = [sb(f"at{i}", [128, D], F32) for i in range(2)]
    s_t = [sb(f"st{i}", [128, D], F32) for i in range(2)]
    h2tm = [sb(f"h2tm{i}", [128, D], BF16) for i in range(2)]
    h2f = [sb(f"h2f{i}", [128, GB, 128], BF16) for i in range(2)]
    colb = [sb(f"col{i}", [128, 8], F32) for i in range(2)]
    scr_f = sb("scr_f", [128, D], F32)

    pes = contextlib.ExitStack()
    ps_o = [pes.enter_context(nc.psum_tensor(f"b3_o{i}", [128, 512], F32))
            for i in range(4)]
    ps_tr_t = [pes.enter_context(nc.psum_tensor(f"b3_tr{i}", [128, 128], BF16))
               for i in range(2)]
    ps_tr = [ps_tr_t[0][:], ps_tr_t[1][:]]

    dep = {}

    def W(e, key):
        d = dep.get(key)
        if d:
            if isinstance(d, list):
                for s_, v_ in d:
                    e.wait_ge(s_.h, v_)
            else:
                e.wait_ge(d[0].h, d[1])

    @block.gpsimd
    def _(g):
        for k in range(GB):
            lwo.inc(g.dma_start(out=wout_sb[:, k, :],
                                in_=woutT[128 * k:128 * k + 128, :]), 16)

    for gt in range(ntile):
        par = gt % 2

        @block.sync
        def _(s):
            W(s, f"prod{par}")
            for gq in range(2):
                lps[par].inc(s.dma_start(
                    out=prodt[par][:, 8 * gq:8 * gq + 8, :],
                    in_=prodbuf.rearrange("(g p) t -> p g t", p=128)
                        [:, 8 * gq:8 * gq + 8, 128 * gt:128 * gt + 128]), 16)
            W(s, f"vt{par}")
            lvs[par].inc(s.dma_start(out=vt[par][:],
                                     in_=vbuf[128 * gt:128 * gt + 128, :]), 16)
            W(s, f"x1t{par}")
            lx1[par].inc(s.dma_start(out=x1t[par][:],
                                     in_=xbuf1[128 * gt:128 * gt + 128, :]), 16)
            dep[f"pld{par}"] = (lps[par], lps[par].v)
            dep[f"vld{par}"] = (lvs[par], lvs[par].v)
            dep[f"xld{par}"] = (lx1[par], lx1[par].v)

        @block.tensor
        def _(t_):
            if gt == 0:
                t_.wait_ge(lwo.h, lwo.v)
            W(t_, f"pld{par}")
            W(t_, "ps_o")
            for k in range(GB):
                for n4 in range(4):
                    mm = t_.matmul(ps_o[n4][:], prodt[par][:, k, :],
                                   wout_sb[:, k, 512 * n4:512 * n4 + 512],
                                   start=(k == 0), stop=(k == GB - 1))
            mmo.inc(mm, 1)
        mm_done = (mmo, mmo.v)
        dep[f"prod{par}"] = mm_done

        @block.vector
        def _(v):
            v.wait_ge(mm_done[0].h, mm_done[1])
            W(v, f"vld{par}")
            W(v, f"at{par}")
            for n4 in range(4):
                v.tensor_tensor(a_t[par][:, 512 * n4:512 * n4 + 512], ps_o[n4][:],
                                vt[par][:, 512 * n4:512 * n4 + 512], ALU.add)
            vv.inc(v.tensor_copy(scr_f[:, 0:2], scr_f[:, 0:2]), 1)
            dep["ps_o"] = (vv, vv.v)
            dep[f"vt{par}"] = (vv, vv.v)
            W(v, f"xld{par}")
            W(v, f"st{par}")
            vv.inc(v.tensor_tensor(s_t[par][:], a_t[par][:], x1t[par][:], ALU.add), 1)
            v.wait_ge(vv.h, vv.v)
            dep[f"x1t{par}"] = (vv, vv.v)
            cb = colb[par]
            vv.inc(v.scalar_tensor_tensor(scr_f[:], a_t[par][:], 1.0, a_t[par][:],
                                          ALU.mult, ALU.mult, accum_out=cb[:, 0:1]), 1)
            v.wait_ge(vv.h, vv.v)
            vv.inc(v.tensor_copy(cb[:, 1:2], cb[:, 0:1]), 1)
        stats_done = (vv, vv.v)
        s_ready = (vv, vv.v)

        @block.scalar
        def _(s):
            s.wait_ge(stats_done[0].h, stats_done[1])
            ss.inc(s.activation(colb[par][:, 2:3], colb[par][:, 0:1], AF.Sqrt,
                                scale=1.0 / D, bias=ctx["ccol"][:, 2:3]), 1)
            s.wait_ge(ss.h, ss.v)
            ss.inc(s.copy(ctx["ccol"][:, 6:7], colb[par][:, 2:3]), 1)
        sq_done = (ss, ss.v)

        @block.vector
        def _(v):
            v.wait_ge(sq_done[0].h, sq_done[1])
            vv.inc(v.reciprocal(colb[par][:, 3:4], colb[par][:, 2:3]), 1)
            v.wait_ge(vv.h, vv.v)
            W(v, f"h2tm{par}")
            vv.inc(v.tensor_scalar(scr_f[:], a_t[par][:], colb[par][:, 3:4], None,
                                   ALU.mult), 1)
            vv.inc(v.tensor_tensor(h2tm[par][:], scr_f[:], ctx["n2w"][:], ALU.mult), 1)
            dep[f"at{par}"] = (vv, vv.v)
        h2_done = (vv, vv.v)

        @block.tensor
        def _(t_):
            t_.wait_ge(h2_done[0].h, h2_done[1])
            for g in range(GB):
                W(t_, f"tr{g % 2}")
                tmm.inc(t_.transpose(ps_tr[g % 2], h2tm[par][:, 128 * g:128 * g + 128],
                                     identb[:]), 1)
                dep[f"trmm{g % 2}"] = (tmm, tmm.v)

                if g % 2 == 0:
                    @block.vector
                    def _(v, g=g):
                        v.wait_ge(dep[f"trmm{g % 2}"][0].h, dep[f"trmm{g % 2}"][1])
                        W(v, f"h2f{par}")
                        vv.inc(v.tensor_copy(h2f[par][:, g, :], ps_tr[g % 2]), 1)
                        dep[f"tr{g % 2}"] = (vv, vv.v)
                else:
                    @block.scalar
                    def _(s, g=g):
                        s.wait_ge(dep[f"trmm{g % 2}"][0].h, dep[f"trmm{g % 2}"][1])
                        W(s, f"h2f{par}")
                        ss.inc(s.copy(h2f[par][:, g, :], ps_tr[g % 2]), 1)
                        dep[f"tr{g % 2}"] = (ss, ss.v)
        h2f_done = [(vv, vv.v), (ss, ss.v)]

        @block.sync
        def _(s):
            for d in h2f_done:
                s.wait_ge(d[0].h, d[1])
            W(s, f"sth{par}")
            sth[par].inc(s.dma_start(
                out=h2fmbuf[:, :, 128 * gt:128 * gt + 128].rearrange("g p c -> p g c"),
                in_=h2f[par][:]), 16)
            s.wait_ge(s_ready[0].h, s_ready[1])
            W(s, f"sts{par}")
            sts[par].inc(s.dma_start(out=resbuf[128 * gt:128 * gt + 128, :],
                                     in_=s_t[par][:]), 16)
        dep[f"h2f{par}"] = (sth[par], sth[par].v)
        dep[f"sth{par}"] = (sth[par], sth[par].v)
        dep[f"st{par}"] = (sts[par], sts[par].v)
        dep[f"sts{par}"] = (sts[par], sts[par].v)
        dep[f"h2tm{par}"] = (tmm, tmm.v)

    es.close()
    pes.close()
    _stage_out(ctx, mmo, tmm, vv, ss, lwo, *lps, *lvs, *lx1, *sts, *sth)


def emit_b4(ctx, h2fmbuf, wg_scr, wu_scr, wd_scr, resbuf, xbuf2):
    nc, block, sem = ctx["nc"], ctx["block"], ctx["sem"]
    NT4 = PANEL // 128
    NP4 = T // PANEL
    NS = FF // 128

    lh2 = ctx["gdma"][0:2]
    lgw = ctx["gdma"][2:4]
    ldw = ctx["gdma"][4:6]
    lrs = ctx["gdma"][6:8]
    mmg = ctx["geng"]["t"]
    mmd = ctx["geng"]["g"]
    vv, ss = ctx["geng"]["v"], ctx["geng"]["s"]
    stx = ctx["gdma"][8:10]

    es = contextlib.ExitStack()
    sb = lambda n, sh, dt: es.enter_context(nc.sbuf_tensor(f"b4_{n}", sh, dt))
    h2p = [sb(f"h2p{i}", [128, GB, PANEL], BF16) for i in range(2)]
    wg_t = [sb(f"wg{i}", [128, GB * 128], BF16) for i in range(2)]
    wu_t = [sb(f"wu{i}", [128, GB * 128], BF16) for i in range(2)]
    hfm = sb("hfm", [128, NS, PANEL], BF16)
    sil = [sb(f"sil{i}", [128, PANEL], BF16) for i in range(2)]
    wd_t = [sb(f"wd{i}", [128, 512], BF16) for i in range(2)]
    yd = [sb(f"yd{i}", [128, D], F32) for i in range(NT4)]
    rs_t = [sb(f"rs{i}", [128, D], F32) for i in range(2)]
    xo = [sb(f"xo{i}", [128, D], F32) for i in range(2)]

    pes = contextlib.ExitStack()
    ps = [pes.enter_context(nc.psum_tensor(f"b4_ps{i}", [128, PANEL], F32))
          for i in range(4)]

    dep = {}

    def W(e, key):
        d = dep.get(key)
        if d:
            if isinstance(d, list):
                for s_, v_ in d:
                    e.wait_ge(s_.h, v_)
            else:
                e.wait_ge(d[0].h, d[1])

    for p in range(NP4):
        par = p % 2
        tb = p * NT4

        @block.sync
        def _(s):
            W(s, f"h2p{par}")
            for gq in range(2):
                lh2[par].inc(s.dma_start(
                    out=h2p[par][:, 8 * gq:8 * gq + 8, :],
                    in_=h2fmbuf.rearrange("g p t -> p g t")
                        [:, 8 * gq:8 * gq + 8, PANEL * p:PANEL * p + PANEL]), 16)
            dep[f"h2ld{par}"] = (lh2[par], lh2[par].v)

        for s_ in range(NS):
            it = p * NS + s_
            spar = it % 2

            @block.sync
            def _(s):
                W(s, f"wg{spar}")
                lgw[spar].inc(s.dma_start(out=wg_t[spar][:], in_=wg_scr[s_]), 16)
                lgw[spar].inc(s.dma_start(out=wu_t[spar][:], in_=wu_scr[s_]), 16)
                dep[f"gld{spar}"] = (lgw[spar], lgw[spar].v)

            @block.tensor
            def _(t_):
                W(t_, f"gld{spar}")
                W(t_, f"h2ld{par}")
                W(t_, "psd")
                W(t_, f"psg{s_ % 2}")
                for g in range(GB):
                    mm = t_.matmul(ps[s_ % 2][:], wg_t[spar][:, 128 * g:128 * g + 128],
                                   h2p[par][:, g, :], start=(g == 0), stop=(g == GB - 1))
                mmg.inc(mm, 1)
                W(t_, f"psu{s_ % 2}")
                for g in range(GB):
                    mm = t_.matmul(ps[2 + s_ % 2][:],
                                   wu_t[spar][:, 128 * g:128 * g + 128],
                                   h2p[par][:, g, :], start=(g == 0), stop=(g == GB - 1))
                mmg.inc(mm, 1)
            gu_done = (mmg, mmg.v)
            dep[f"wg{spar}"] = gu_done

            @block.scalar
            def _(s):
                s.wait_ge(gu_done[0].h, gu_done[1] - 1)
                W(s, f"sil{s_ % 2}")
                ss.inc(s.activation(sil[s_ % 2][:], ps[s_ % 2][:], AF.Silu), 1)
                dep[f"psg{s_ % 2}"] = (ss, ss.v)
            sil_done = (ss, ss.v)

            @block.vector
            def _(v):
                v.wait_ge(sil_done[0].h, sil_done[1])
                v.wait_ge(gu_done[0].h, gu_done[1])
                W(v, "hfm_rd")
                vv.inc(v.tensor_tensor(hfm[:, s_, :], sil[s_ % 2][:],
                                       ps[2 + s_ % 2][:], ALU.mult), 1)
                dep[f"psu{s_ % 2}"] = (vv, vv.v)
                dep[f"sil{s_ % 2}"] = (vv, vv.v)
        h_ready = (vv, vv.v)
        dep[f"h2p{par}"] = (mmg, mmg.v)

        for of4 in range(4):
            for s_ in range(NS):
                it2 = (p * 4 + of4) * NS + s_
                dpar = it2 % 2

                @block.sync
                def _(s):
                    W(s, f"wd{dpar}")
                    ldw[dpar].inc(s.dma_start(
                        out=wd_t[dpar][:],
                        in_=wd_scr[128 * s_:128 * s_ + 128,
                                   512 * of4:512 * of4 + 512]), 16)
                    dep[f"dwld{dpar}"] = (ldw[dpar], ldw[dpar].v)

                @block.tensor
                def _(t_):
                    W(t_, f"dwld{dpar}")
                    if s_ == 0:
                        t_.wait_ge(h_ready[0].h, h_ready[1])
                        W(t_, "psd")
                    for t in range(NT4):
                        mm = t_.matmul(ps[t][:, 0:512], hfm[:, s_, 128 * t:128 * t + 128],
                                       wd_t[dpar][:], start=(s_ == 0), stop=(s_ == NS - 1))
                    mmd.inc(mm, 1)
                dep[f"wd{dpar}"] = (mmd, mmd.v)
            dmm_done = (mmd, mmd.v)

            @block.vector
            def _(v):
                v.wait_ge(dmm_done[0].h, dmm_done[1])
                W(v, "yd")
                for t in range(NT4):
                    vv.inc(v.tensor_copy(yd[t][:, 512 * of4:512 * of4 + 512],
                                         ps[t][:, 0:512]), 1)
                v.wait_ge(vv.h, vv.v)
            dep["psd"] = (vv, vv.v)
        yd_ready = (vv, vv.v)
        dep["hfm_rd"] = (mmd, mmd.v)

        for t in range(NT4):
            gt = tb + t
            tpar = gt % 2

            @block.sync
            def _(s):
                W(s, f"rs{tpar}")
                lrs[tpar].inc(s.dma_start(out=rs_t[tpar][:],
                                          in_=resbuf[128 * gt:128 * gt + 128, :]), 16)
                dep[f"rld{tpar}"] = (lrs[tpar], lrs[tpar].v)

            @block.vector
            def _(v):
                v.wait_ge(yd_ready[0].h, yd_ready[1])
                W(v, f"rld{tpar}")
                W(v, f"xo{tpar}")
                vv.inc(v.tensor_tensor(xo[tpar][:], yd[t][:], rs_t[tpar][:],
                                       ALU.add), 1)
                v.wait_ge(vv.h, vv.v)
                dep[f"rs{tpar}"] = (vv, vv.v)
                if t == NT4 - 1:
                    dep["yd"] = (vv, vv.v)
            xo_done = (vv, vv.v)

            @block.sync
            def _(s):
                s.wait_ge(xo_done[0].h, xo_done[1])
                W(s, f"stx{tpar}")
                stx[tpar].inc(s.dma_start(out=xbuf2[128 * gt:128 * gt + 128, :],
                                          in_=xo[tpar][:]), 16)
            dep[f"xo{tpar}"] = (stx[tpar], stx[tpar].v)
            dep[f"stx{tpar}"] = (stx[tpar], stx[tpar].v)

    es.close()
    pes.close()
    _stage_out(ctx, mmg, mmd, vv, ss, *lh2, *lgw, *ldw, *lrs, *stx)


def prep_inputs(inp):
    x = np.asarray(inp["L0_attn_hidden_states"], np.float32)
    pi, pif = PI_D, PI_FF
    xp = np.ascontiguousarray(x[..., pi])

    win = np.asarray(inp["L0_attn_in_proj_w"], np.float32)
    winp = win[:, pi]
    winp_pad = np.zeros((65 * 128, D), np.float32)
    winp_pad[:DIP] = winp
    win2 = np.ascontiguousarray(
        winp_pad.reshape(65, 128, GB, 128).transpose(0, 3, 2, 1).reshape(65, 128, GB * 128))

    wout = np.asarray(inp["L0_attn_out_proj_w"], np.float32)
    woutT = np.ascontiguousarray(wout[pi, :].T)

    def tile_lhsT(w):
        wp = w[:, pi]
        return np.ascontiguousarray(
            wp.reshape(64, 128, GB, 128).transpose(0, 3, 2, 1).reshape(64, 128, GB * 128))

    wgate = tile_lhsT(np.asarray(inp["L0_attn_gate_w"], np.float32))
    wup = tile_lhsT(np.asarray(inp["L0_attn_up_w"], np.float32))
    wdownT = np.ascontiguousarray(np.asarray(inp["L0_attn_down_w"], np.float32)[pi, :].T)

    gup = np.asarray(inp["L0_ffn_gate_up_packed"], np.int32)
    gus = np.asarray(inp["L0_ffn_gate_up_scales"], np.float16)
    perm_rows = np.concatenate([pif, FFQ + pif])
    gup_p = gup[perm_rows]
    gus_p = np.ascontiguousarray(gus[:, perm_rows])
    gu_u8t = np.ascontiguousarray(gup_p.view(np.uint8).reshape(2 * FFQ, D // 2).T)

    dnp = np.asarray(inp["L0_ffn_down_packed"], np.int32)
    dns = np.asarray(inp["L0_ffn_down_scales"], np.float16)
    dnp_p = dnp[pi]
    dns_p = np.ascontiguousarray(dns[:, pi])
    dn_u8t = np.ascontiguousarray(dnp_p.view(np.uint8).reshape(D, FFQ // 2).T)

    com = dict(
        win2=win2, woutT=woutT, wgate=wgate, wup=wup, wdownT=wdownT,
        gu_u8t=gu_u8t, gu_sc=gus_p, dn_u8t=dn_u8t, dn_sc=dns_p,
        zbias=np.asarray(inp["L0_attn_z_bias"], np.float32),
        convw=np.asarray(inp["L0_attn_conv_w"], np.float32),
        convb=np.asarray(inp["L0_attn_conv_b"], np.float32),
        dvec=np.asarray(inp["L0_attn_D"], np.float32),
        n1w=np.asarray(inp["L0_attn_norm1_w"], np.float32)[pi],
        n2w=np.asarray(inp["L0_attn_norm2_w"], np.float32)[pi],
        identf=np.eye(128, dtype=np.float32),
        onesf=np.ones((128, 128), np.float32),
        trilu=np.triu(np.ones((128, 128), np.float32)),
    )
    in_maps = []
    for c in range(NCORES):
        m = dict(com)
        m["xin"] = np.ascontiguousarray(
            xp[c * SEQ_PER_CORE:(c + 1) * SEQ_PER_CORE].reshape(T, D))
        in_maps.append(m)
    return in_maps


def kernel(**inputs):
    in_maps = prep_inputs(inputs)
    nc = build_program()
    res = run_bass_kernel_spmd(nc, in_maps, core_ids=list(range(NCORES)))
    outs = [r["out"] for r in res.results]
    y = np.concatenate([o.reshape(SEQ_PER_CORE, L, D) for o in outs], axis=0)
    return y


# version 04
# speedup vs baseline: 1.1408x; 1.1408x over previous
import contextlib
import numpy as np
import ml_dtypes
import concourse.bass as bass
import concourse.mybir as mybir
from concourse.bass_utils import run_bass_kernel_spmd

F32 = mybir.dt.float32
BF16 = mybir.dt.bfloat16
F16 = mybir.dt.float16
I32 = mybir.dt.int32
U32 = mybir.dt.uint32
U8 = mybir.dt.uint8
AF = mybir.ActivationFunctionType
ALU = mybir.AluOpType

D = 2048
L = 2048
BATCH = 16
NCORES = 8
SEQ_PER_CORE = BATCH // NCORES
T = SEQ_PER_CORE * L
DIN = 2048
NH = 32
DS = 64
HD = 64
CONVD = 6144
DIP = 8224
FFQ = 5632
FF = 8192
CHUNK = 128
GB = D // 128
EPS_LN = 1e-5
EPS_RMS = 1e-5
EPS_Q = 1e-6

C_EXPBITS = float(126 << 23)
G_SCALE = 2.0 / 3.0 / float(1 << 22)

PANEL = 512
NPANEL = T // PANEL
NT = PANEL // 128

ALL_ENG = ("tensor", "vector", "scalar", "sync", "gpsimd")


def _pi_perm(n):
    idx = np.empty(n, np.int64)
    for r in range(n // 256):
        b = 256 * r
        idx[b:b + 128] = b + 2 * np.arange(128)
        idx[b + 128:b + 256] = b + 2 * np.arange(128) + 1
    return idx


PI_D = _pi_perm(D)
PI_FF = _pi_perm(FFQ)


class Sem:
    b = 0

    def __init__(self, stack, nc, name):
        self.h = stack.enter_context(nc.semaphore(name))
        self.v = 0

    def inc(self, instr, n=1):
        instr.then_inc(self.h, n)
        self.v += n
        return instr

    def wait(self, eng, v=None):
        eng.wait_ge(self.h, self.v if v is None else v)


def _barrier(ctx):
    block = ctx["block"]
    deps = ctx.get("deps", [])
    for engname in ALL_ENG:
        def _f(e, _deps=list(deps)):
            for s, v in _deps:
                e.wait_ge(s.h, v)
        getattr(block, engname)(_f)
    ctx["deps"] = []


def _stage_out(ctx, *sems):
    ctx.setdefault("deps", [])
    for s in sems:
        ctx["deps"].append((s, s.v))


def build_program(debug_taps=(), stages=("P0", "A", "B01", "B2", "B3", "B4", "C"),
                  detect_races=True, input_bufs=()):
    nc = bass.Bass("TRN2", target_bir_lowering=False, detect_race_conditions=detect_races)
    ap = lambda t: t.ap()

    xin = ap(nc.dram_tensor("xin", [T, D], F32, kind="ExternalInput"))
    win2 = ap(nc.dram_tensor("win2", [65, 128, GB * 128], F32, kind="ExternalInput"))
    woutT = ap(nc.dram_tensor("woutT", [DIN, D], F32, kind="ExternalInput"))
    wgate = ap(nc.dram_tensor("wgate", [64, 128, GB * 128], F32, kind="ExternalInput"))
    wup = ap(nc.dram_tensor("wup", [64, 128, GB * 128], F32, kind="ExternalInput"))
    wdownT = ap(nc.dram_tensor("wdownT", [FF, D], F32, kind="ExternalInput"))
    gu_u8t = ap(nc.dram_tensor("gu_u8t", [D // 2, 2 * FFQ], U8, kind="ExternalInput"))
    gu_sc = ap(nc.dram_tensor("gu_sc", [D // 128, 2 * FFQ], F16, kind="ExternalInput"))
    dn_u8t = ap(nc.dram_tensor("dn_u8t", [FFQ // 2, D], U8, kind="ExternalInput"))
    dn_sc = ap(nc.dram_tensor("dn_sc", [FFQ // 128, D], F16, kind="ExternalInput"))
    zbias = ap(nc.dram_tensor("zbias", [DIN], F32, kind="ExternalInput"))
    convw = ap(nc.dram_tensor("convw", [CONVD, 4], F32, kind="ExternalInput"))
    convb = ap(nc.dram_tensor("convb", [CONVD], F32, kind="ExternalInput"))
    dvec = ap(nc.dram_tensor("dvec", [NH], F32, kind="ExternalInput"))
    n1w = ap(nc.dram_tensor("n1w", [D], F32, kind="ExternalInput"))
    n2w = ap(nc.dram_tensor("n2w", [D], F32, kind="ExternalInput"))
    identf = ap(nc.dram_tensor("identf", [128, 128], F32, kind="ExternalInput"))
    trilu = ap(nc.dram_tensor("trilu", [128, 128], F32, kind="ExternalInput"))
    onesf = ap(nc.dram_tensor("onesf", [128, 128], F32, kind="ExternalInput"))

    out = ap(nc.dram_tensor("out", [T, D], F32, kind="ExternalOutput"))

    kind = lambda name: ("ExternalInput" if name in input_bufs else
                         "ExternalOutput" if name in debug_taps else "Internal")
    xbuf1 = ap(nc.dram_tensor("xbuf1", [T, D], F32, kind=kind("xbuf1")))
    xbuf2 = ap(nc.dram_tensor("xbuf2", [T, D], F32, kind=kind("xbuf2")))
    vbuf = ap(nc.dram_tensor("vbuf", [T, D], F32, kind=kind("vbuf")))
    resbuf = ap(nc.dram_tensor("resbuf", [T, D], F32, kind=kind("resbuf")))
    zbuf = ap(nc.dram_tensor("zbuf", [DIN, T], F32, kind=kind("zbuf")))
    xbcbuf = ap(nc.dram_tensor("xbcbuf", [CONVD, T], BF16, kind=kind("xbcbuf")))
    dtbuf = ap(nc.dram_tensor("dtbuf", [32, T], F32, kind=kind("dtbuf")))
    prodbuf = ap(nc.dram_tensor("prodbuf", [DIN, T], BF16, kind=kind("prodbuf")))
    cumdram = ap(nc.dram_tensor("cumdram", [2, 32, 128], F32, kind="Internal"))
    h2fmbuf = ap(nc.dram_tensor("h2fmbuf", [GB, 128, T], BF16, kind=kind("h2fmbuf")))
    wgu_scr = ap(nc.dram_tensor("wgu_scr", [88, 128, GB * 128], BF16, kind=kind("wgu_scr")))
    wdn_scr = ap(nc.dram_tensor("wdn_scr", [FFQ, D], BF16, kind=kind("wdn_scr")))
    wg_scr = ap(nc.dram_tensor("wg_scr", [64, 128, GB * 128], BF16, kind=kind("wg_scr")))
    wu_scr = ap(nc.dram_tensor("wu_scr", [64, 128, GB * 128], BF16, kind=kind("wu_scr")))
    wd_scr = ap(nc.dram_tensor("wd_scr", [FF, D], BF16, kind=kind("wd_scr")))
    dbg = None
    if "b2dbg" in debug_taps:
        ctx_b2dbg = dict(
            Hst=ap(nc.dram_tensor("b2_dbg_Hst", [128, 1024], F32, kind="ExternalOutput")),
            y_fm=ap(nc.dram_tensor("b2_dbg_yfm", [128, GB, 256], BF16, kind="ExternalOutput")),
            xT=ap(nc.dram_tensor("b2_dbg_xT", [128, GB * 128], BF16, kind="ExternalOutput")),
            A_bc=ap(nc.dram_tensor("b2_dbg_Abc", [128, 32], F32, kind="ExternalOutput")),
            cum_tok=ap(nc.dram_tensor("b2_dbg_cum", [128, 32], F32, kind="ExternalOutput")),
            prodt=ap(nc.dram_tensor("b2_dbg_prodt", [128, GB, 256], BF16, kind="ExternalOutput")),
            Hst0=ap(nc.dram_tensor("b2_dbg_Hst0", [128, 1024], F32, kind="ExternalOutput")),
            Hmid=ap(nc.dram_tensor("b2_dbg_Hmid", [128, 1024], F32, kind="ExternalOutput")),
            seg=ap(nc.dram_tensor("b2_dbg_seg", [128, 128], F32, kind="ExternalOutput")),
            segE=ap(nc.dram_tensor("b2_dbg_segE", [128, 128], F32, kind="ExternalOutput")),
            E_sb=ap(nc.dram_tensor("b2_dbg_Esb", [128, 128], F32, kind="ExternalOutput")),
            sc_bf=ap(nc.dram_tensor("b2_dbg_scbf", [128, 128], BF16, kind="ExternalOutput")),
            sc_m=ap(nc.dram_tensor("b2_dbg_scm", [128, 128], BF16, kind="ExternalOutput")),
            y_sb=ap(nc.dram_tensor("b2_dbg_ysb", [128, 128], F32, kind="ExternalOutput")),
            Cs=ap(nc.dram_tensor("b2_dbg_Cs", [128, 128], F32, kind="ExternalOutput")),
            T1tmp=ap(nc.dram_tensor("b2_dbg_T1", [128, 128], F32, kind="ExternalOutput")),
            cum=ap(nc.dram_tensor("b2_dbg_cumfm", [32, 128], F32, kind="ExternalOutput")),
            cum_tok0=ap(nc.dram_tensor("b2_dbg_cumtok0", [128, 32], F32, kind="ExternalOutput")),
            ncum_tok0=ap(nc.dram_tensor("b2_dbg_ncumtok0", [128, 32], F32, kind="ExternalOutput")),
            sp_t0=ap(nc.dram_tensor("b2_dbg_spt0", [32, 128], F32, kind="ExternalOutput")),
            dtt0=ap(nc.dram_tensor("b2_dbg_dtt0", [32, 256], F32, kind="ExternalOutput")),
            scview=ap(nc.dram_tensor("b2_dbg_scview", [32, 512], F32, kind="ExternalOutput")),
        )
    else:
        ctx_b2dbg = None
    if "b01dbg" in debug_taps:
        ctx_b01dbg = ap(nc.dram_tensor("b01dbg", [128, 36], F32, kind="ExternalOutput"))
    else:
        ctx_b01dbg = None
    if "dbg_u1fm" in debug_taps:
        dbg = dict(
            u1fm=ap(nc.dram_tensor("dbg_u1fm", [128, GB, PANEL], BF16, kind="ExternalOutput")),
            hfm=ap(nc.dram_tensor("dbg_hfm", [128, 44, PANEL], BF16, kind="ExternalOutput")),
            yq=ap(nc.dram_tensor("dbg_yq", [NT * 128, D], F32, kind="ExternalOutput")),
            cols=ap(nc.dram_tensor("dbg_cols", [128, 64], F32, kind="ExternalOutput")),
            u1tm=ap(nc.dram_tensor("dbg_u1tm", [3 * 128, D], BF16, kind="ExternalOutput")),
            xt=ap(nc.dram_tensor("dbg_xt", [2 * 128, D], F32, kind="ExternalOutput")),
        )

    stack = contextlib.ExitStack()
    identf_sb = stack.enter_context(nc.sbuf_tensor("identf_sb", [128, 128], F32))
    identb_sb = stack.enter_context(nc.sbuf_tensor("identb_sb", [128, 128], BF16))
    onesf_sb = stack.enter_context(nc.sbuf_tensor("onesf_sb", [128, 128], F32))
    n1w_sb = stack.enter_context(nc.sbuf_tensor("n1w_sb", [128, D], BF16))
    n2w_sb = stack.enter_context(nc.sbuf_tensor("n2w_sb", [128, D], BF16))
    dbc_sb = stack.enter_context(nc.sbuf_tensor("dbc_sb", [128, NH], F32))
    zb_sb = stack.enter_context(nc.sbuf_tensor("zb_sb", [128, GB], F32))
    cw_sb = stack.enter_context(nc.sbuf_tensor("cw_sb", [128, 48, 4], F32))
    cb_sb = stack.enter_context(nc.sbuf_tensor("cb_sb", [128, 48], F32))
    ccol_sb = stack.enter_context(nc.sbuf_tensor("ccol_sb", [128, 8], F32))
    trilu_sb = stack.enter_context(nc.sbuf_tensor("trilu_sb", [128, 128], BF16))

    with stack:
        with nc.Block() as block:
            sem = lambda name: Sem(stack, nc, name)
            pre = sem("pre")
            pre2 = sem("pre2")

            @block.sync
            def _(s):
                pre.inc(s.dma_start(out=identf_sb[:], in_=identf[:]), 16)
                pre.inc(s.dma_start(out=onesf_sb[:], in_=onesf[:]), 16)
                pre.inc(s.dma_start(out=dbc_sb[:], in_=dvec[None, :].to_broadcast([128, NH])), 16)
                pre.inc(s.dma_start(
                    out=zb_sb[:],
                    in_=zbias.rearrange("(g p) -> p g", p=128), allow_slow_non_contiguous=True), 16)
                pre.inc(s.dma_start(out=cw_sb[:], in_=convw.rearrange("(f p) k -> p f k", p=128)), 16)
                pre.inc(s.dma_start(
                    out=cb_sb[:],
                    in_=convb.rearrange("(f p) -> p f", p=128), allow_slow_non_contiguous=True), 16)

            @block.gpsimd
            def _(g):
                pre2.inc(g.dma_start(out=identb_sb[:], in_=identf[:]), 16)
                pre2.inc(g.dma_start(out=trilu_sb[:], in_=trilu[:]), 16)
                pre2.inc(g.dma_start(out=n1w_sb[:], in_=n1w[None, :].to_broadcast([128, D])), 16)
                pre2.inc(g.dma_start(out=n2w_sb[:], in_=n2w[None, :].to_broadcast([128, D])), 16)

            @block.vector
            def _(v):
                pre.wait(v)
                pre2.wait(v)
                v.memset(ccol_sb[:, 0:1], EPS_LN)
                v.memset(ccol_sb[:, 1:2], 4.0 * EPS_Q)
                v.memset(ccol_sb[:, 2:3], EPS_RMS)
                v.memset(ccol_sb[:, 4:8], 0.0)
                pre.inc(v.memset(ccol_sb[:, 3:4], 1.0), 1)

            geng = dict(t=Sem(stack, nc, "g_t"), v=Sem(stack, nc, "g_v"),
                        s=Sem(stack, nc, "g_s"), g=Sem(stack, nc, "g_g"))
            gdma = [Sem(stack, nc, f"g_dma{i}") for i in range(12)]
            gswd = [Sem(stack, nc, f"g_swd{i}") for i in range(2)]
            ctx = dict(nc=nc, block=block, sem=sem, stack=stack, ccol=ccol_sb,
                       geng=geng, gdma=gdma, gswd=gswd,
                       b01dbg=ctx_b01dbg, b2dbg=ctx_b2dbg,
                       identf=identf_sb, identb=identb_sb, ones=onesf_sb,
                       trilu=trilu_sb,
                       n1w=n1w_sb, n2w=n2w_sb, dbc=dbc_sb, zb=zb_sb,
                       cw=cw_sb, cb=cb_sb, deps=[(pre, pre.v), (pre2, pre2.v)])

            final_sems = []
            if "P0" in stages:
                _barrier(ctx)
                emit_p0(ctx, gu_u8t, gu_sc, dn_u8t, dn_sc, wgu_scr, wdn_scr,
                        [(wgate, wg_scr, 64, GB * 128), (wup, wu_scr, 64, GB * 128)],
                        [(wdownT, wd_scr, FF)])
            if "A" in stages:
                _barrier(ctx)
                emit_ffnq(ctx, "A", xin, xbuf1, wgu_scr, wdn_scr, out_ap=None, dbg=dbg)
            if "B01" in stages:
                _barrier(ctx)
                emit_b01(ctx, xbuf1, vbuf, win2, zbuf, xbcbuf, dtbuf)
            if "B2" in stages:
                _barrier(ctx)
                emit_b2(ctx, zbuf, xbcbuf, dtbuf, prodbuf, cumdram)
            if "B3" in stages:
                _barrier(ctx)
                emit_b3(ctx, prodbuf, woutT, vbuf, xbuf1, resbuf, h2fmbuf)
            if "B4" in stages:
                _barrier(ctx)
                emit_b4(ctx, h2fmbuf, wg_scr, wu_scr, wd_scr, resbuf, xbuf2)
            if "C" in stages:
                _barrier(ctx)
                emit_ffnq(ctx, "C", xbuf2, None, wgu_scr, wdn_scr, out_ap=out)
            _barrier(ctx)

    return nc


def emit_p0(ctx, gu_u8t, gu_sc, dn_u8t, dn_sc, wgu_scr, wdn_scr, tiled, rowmajor):
    nc, block, sem = ctx["nc"], ctx["block"], ctx["sem"]
    lds = [sem("p0_ld0"), sem("p0_ld1")]
    lscs = [sem("p0_lsc0"), sem("p0_lsc1")]
    cv = sem("p0_cv")
    vp = sem("p0_vp")
    sts = [sem("p0_st0"), sem("p0_st1")]
    ldcs = [sem("p0c_ld0"), sem("p0c_ld1")]
    stcs = [sem("p0c_st0"), sem("p0c_st1")]

    FG = 2816
    FD = 2048
    es = contextlib.ExitStack()
    sb = lambda n, sh, dt: es.enter_context(nc.sbuf_tensor(n, sh, dt))
    su8 = [sb(f"p0u8_{i}", [128, FG], U8) for i in range(2)]
    si32 = [sb(f"p0i32_{i}", [128, FG], U32) for i in range(2)]
    ssc = [sb(f"p0sc_{i}", [128, FG], F32) for i in range(2)]
    snib = sb("p0nib", [128, FG], U32)
    sbem = sb("p0bem", [128, FG], U32)
    sts_ = sb("p0ts", [128, FG], U32)
    sbits = sb("p0bits", [128, FG], U32)
    sg_ = sb("p0g", [128, FG], F32)
    sv1 = sb("p0v1", [128, FG], F32)
    swout = [sb(f"p0w_{i}", [128, 2 * FG], BF16) for i in range(2)]
    cbuf = [sb(f"p0c_{i}", [128, GB * 128], BF16) for i in range(2)]

    jobs = []
    for R in range(D // 256):
        for fc in range(2 * FFQ // FG):
            jobs.append(("gu", R, fc))
    for R in range(FFQ // 256):
        jobs.append(("dn", R, 0))

    def job_src(j):
        knd, R, fc = j
        if knd == "gu":
            sl = slice(fc * FG, (fc + 1) * FG)
            return (gu_u8t[128 * R:128 * R + 128, sl], gu_sc, R, sl, FG)
        return (dn_u8t[128 * R:128 * R + 128, :], dn_sc, R, slice(0, FD), FD)

    cjobs = []
    for (src, dst, S, W) in tiled:
        for s_ in range(S):
            cjobs.append((src[s_], dst[s_], W))
    for (src, dst, ROWS) in rowmajor:
        for s_ in range(ROWS // 128):
            cjobs.append((src[128 * s_:128 * s_ + 128, :], dst[128 * s_:128 * s_ + 128, :], D))

    @block.sync
    def _(s):
        for i, j in enumerate(jobs):
            u8src, _, _, _, F = job_src(j)
            if i >= 2:
                s.wait_ge(cv.h, i - 1)
            lds[i % 2].inc(s.dma_start(out=su8[i % 2][:, :F], in_=u8src), 16)

    @block.gpsimd
    def _(g):
        for i, j in enumerate(jobs):
            _, scsrc, R, sl, F = job_src(j)
            if i >= 2:
                g.wait_ge(vp.h, i - 1)
            lscs[i % 2].inc(g.dma_start(out=ssc[i % 2][0:64, :F],
                                in_=scsrc[2 * R, sl][None, :].to_broadcast([64, F])), 16)
            lscs[i % 2].inc(g.dma_start(out=ssc[i % 2][64:128, :F],
                                in_=scsrc[2 * R + 1, sl][None, :].to_broadcast([64, F])), 16)
        for i, (src, dst, W) in enumerate(cjobs):
            if i >= 2:
                g.wait_ge(stcs[i % 2].h, 16 * (i // 2))
            ldcs[i % 2].inc(g.dma_start(out=cbuf[i % 2][:, :W], in_=src), 16)

    @block.scalar
    def _(s):
        nj = len(jobs)
        for i in range(nj + 1):
            if i < nj:
                j = jobs[i]
                _, _, _, _, F = job_src(j)
                s.wait_ge(lds[i % 2].h, 16 * (i // 2 + 1))
                if i >= 2:
                    s.wait_ge(vp.h, i - 1)
                cv.inc(s.activation(si32[i % 2][:, :F], su8[i % 2][:, :F], AF.Copy), 1)
            if i >= 1:
                j = jobs[i - 1]
                knd, R, fc = j
                _, _, _, sl, F = job_src(j)
                s.wait_ge(vp.h, i)
                w = swout[(i - 1) % 2]
                stj = sts[(i - 1) % 2]
                if knd == "gu":
                    ss = 22 * fc
                    dst0 = wgu_scr[ss:ss + 22, :, 128 * (2 * R):128 * (2 * R) + 128]
                    dst1 = wgu_scr[ss:ss + 22, :, 128 * (2 * R + 1):128 * (2 * R + 1) + 128]
                    stj.inc(s.dma_start(
                        out=dst0.rearrange("s p c -> p s c"),
                        in_=w[:, 0:F].rearrange("p (s c) -> p s c", c=128)), 16)
                    stj.inc(s.dma_start(
                        out=dst1.rearrange("s p c -> p s c"),
                        in_=w[:, F:2 * F].rearrange("p (s c) -> p s c", c=128)), 16)
                else:
                    stj.inc(s.dma_start(out=wdn_scr[256 * R:256 * R + 128, :], in_=w[:, 0:F]), 16)
                    stj.inc(s.dma_start(out=wdn_scr[256 * R + 128:256 * R + 256, :],
                                       in_=w[:, F:2 * F]), 16)
        for i, (src, dst, W) in enumerate(cjobs):
            s.wait_ge(ldcs[i % 2].h, 16 * (i // 2 + 1))
            stcs[i % 2].inc(s.dma_start(out=dst, in_=cbuf[i % 2][:, :W]), 16)

    @block.vector
    def _(v):
        for i, j in enumerate(jobs):
            _, _, _, _, F = job_src(j)
            v.wait_ge(cv.h, i + 1)
            v.wait_ge(lscs[i % 2].h, 32 * (i // 2 + 1))
            if i >= 2:
                v.wait_ge(sts[i % 2].h, 32 * (i // 2))
            w = swout[i % 2]
            c = si32[i % 2]
            ins = None
            for half in (0, 1):
                if half == 0:
                    v.tensor_scalar(snib[:, :F], c[:, :F], 15, None, ALU.bitwise_and)
                else:
                    v.tensor_scalar(snib[:, :F], c[:, :F], 4, 15,
                                    ALU.logical_shift_right, ALU.bitwise_and)
                v.tensor_scalar(sbem[:, :F], snib[:, :F], 7, 22,
                                ALU.bitwise_and, ALU.logical_shift_left)
                v.tensor_scalar(sts_[:, :F], snib[:, :F], 8, 28,
                                ALU.bitwise_and, ALU.logical_shift_left)
                v.scalar_tensor_tensor(sbits[:, :F], sbem[:, :F], C_EXPBITS, sts_[:, :F],
                                       ALU.add, ALU.add)
                v.tensor_scalar(sg_[:, :F], sbem[:, :F], G_SCALE, 1.0, ALU.mult, ALU.min)
                v.tensor_tensor(sv1[:, :F], sbits[:, :F].bitcast(F32), sg_[:, :F], ALU.mult)
                ins = v.tensor_tensor(w[:, half * F:half * F + F], sv1[:, :F],
                                      ssc[i % 2][:, :F], ALU.mult)
            vp.inc(ins, 1)

    es.close()
    _stage_out(ctx, *sts, *stcs, vp, *lds, *lscs, cv, *ldcs)


def emit_ffnq(ctx, tag, xin_ap, xout_ap, wgu_scr, wdn_scr, out_ap, dbg=None):
    nc, block, sem = ctx["nc"], ctx["block"], ctx["sem"]
    identb = ctx["identb"]

    spool = ctx.setdefault("spool", {})
    def psem(name):
        if name not in spool:
            spool[name] = sem(name)
        s = spool[name]
        s.b = s.v
        return s
    lxs = [psem("FQ_lx0"), psem("FQ_lx1")]
    vself = psem("FQ_vself")
    sself = psem("FQ_sself")
    csv = psem("FQ_csv")
    css = psem("FQ_css")
    uap = psem("FQ_uap")
    utr = psem("FQ_utr")
    uev_v = psem("FQ_uevv")
    uev_s = psem("FQ_uevs")
    lws = [psem("FQ_lw0"), psem("FQ_lw1")]
    mmg = psem("FQ_mmg")
    hevs = psem("FQ_hevs")
    hevv = psem("FQ_hevv")
    ldws = [psem("FQ_ldw0"), psem("FQ_ldw1")]
    mmd = psem("FQ_mmd")
    dev = psem("FQ_dev")
    yst = psem("FQ_yst")
    ystv = psem("FQ_ystv")
    fin = psem("FQ_fin")
    stxs = [sem(f"{tag}_stx0"), sem(f"{tag}_stx1")]
    lx2s = [psem("FQ_lx20"), psem("FQ_lx21")]

    es = contextlib.ExitStack()
    sb = lambda n, sh, dt: es.enter_context(nc.sbuf_tensor(f"{tag}_{n}", sh, dt))
    xt = [sb(f"xt{i}", [128, D], F32) for i in range(2)]
    scr_f = sb("scr_f", [128, D], F32)
    u1tm = [sb(f"u1tm{i}", [128, D], BF16) for i in range(3)]
    colb = [sb(f"col{i}", [128, 8], F32) for i in range(3)]
    bnst = [sb(f"bns{i}", [128, 24], F32) for i in range(2)]
    u1fm = sb("u1fm", [128, GB, PANEL], BF16)
    hfm = sb("hfm", [128, 44, PANEL], BF16)
    wg_t = [sb(f"wg{i}", [128, 2, GB * 128], BF16) for i in range(2)]
    wu_t = [sb(f"wu{i}", [128, 2, GB * 128], BF16) for i in range(2)]
    sil = [sb(f"sil{i}", [128, PANEL], BF16) for i in range(2)]
    yq = [sb(f"yq{i}", [128, D], F32) for i in range(NT)]
    wd_t = [sb(f"wd{i}", [128, 512], BF16) for i in range(2)]
    xt2 = [sb(f"xt2{i}", [128, D], F32) for i in range(2)]
    xo = [sb(f"xo{i}", [128, D], F32) for i in range(2)]
    ycol = [sb(f"ycol{i}", [128, 4], F32) for i in range(2)]

    pes = contextlib.ExitStack()
    ps = [pes.enter_context(nc.psum_tensor(f"{tag}_ps{i}", [128, PANEL], F32))
          for i in range(4)]
    ps_tr_t = [pes.enter_context(nc.psum_tensor(f"{tag}_tr{i}", [128, 128], BF16))
               for i in range(2)]
    ps_tr = [ps_tr_t[0][:], ps_tr_t[1][:]]

    ntile = NPANEL * NT

    @block.sync
    def _(s):
        for p in range(NPANEL):
            tb = p * NT
            for t in range(NT):
                gt = tb + t
                if gt >= 2:
                    s.wait_ge(uap.h, uap.b + (gt - 1))
                lxs[gt % 2].inc(s.dma_start(out=xt[gt % 2][:],
                                            in_=xin_ap[128 * gt:128 * gt + 128, :]), 16)
            for sp2 in range(22):
                it = p * 22 + sp2
                if it >= 2:
                    s.wait_ge(mmg.h, mmg.b + (4 * (it - 1)))
                for half in (0, 1):
                    lws[it % 2].inc(s.dma_start(out=wg_t[it % 2][:, half, :],
                                                in_=wgu_scr[2 * sp2 + half, :, :]), 16)
                    lws[it % 2].inc(s.dma_start(out=wu_t[it % 2][:, half, :],
                                                in_=wgu_scr[44 + 2 * sp2 + half, :, :]), 16)
            for of4 in range(4):
                for s_ in range(44):
                    it = (p * 4 + of4) * 44 + s_
                    if it >= 2:
                        s.wait_ge(mmd.h, mmd.b + (it - 1))
                    ldws[it % 2].inc(s.dma_start(
                        out=wd_t[it % 2][:],
                        in_=wdn_scr[128 * s_:128 * s_ + 128, 512 * of4:512 * of4 + 512]), 16)
            for t in range(NT):
                gt = tb + t
                if gt >= 2:
                    s.wait_ge(fin.h, fin.b + (gt - 1))
                lx2s[gt % 2].inc(s.dma_start(out=xt2[gt % 2][:],
                                             in_=xin_ap[128 * gt:128 * gt + 128, :]), 16)

    @block.scalar
    def _(s):
        for p in range(NPANEL):
            tb = p * NT
            for t in range(NT):
                gt = tb + t
                s.wait_ge(csv.h, csv.b + (gt + 1))
                sself.inc(s.activation(colb[gt % 3][:, 7:8], colb[gt % 3][:, 5:6], AF.Sqrt,
                                       scale=1.0 / D, bias=ctx["ccol"][:, 0:1]), 1)
                s.wait_ge(sself.h, sself.v)
                css.inc(s.copy(ctx["ccol"][:, 6:7], colb[gt % 3][:, 7:8]), 1)
                for g in range(1, GB, 2):
                    s.wait_ge(utr.h, utr.b + (16 * gt + g + 1))
                    uev_s.inc(s.copy(u1fm[:, g, 128 * t:128 * t + 128], ps_tr[g % 2]), 1)
            for sp2 in range(22):
                for half in (0, 1):
                    pp = p * 44 + 2 * sp2 + half
                    s.wait_ge(mmg.h, mmg.b + (2 * pp + 1))
                    if pp >= 2:
                        s.wait_ge(hevv.h, hevv.b + (pp - 1))
                    hevs.inc(s.activation(sil[pp % 2][:], ps[pp % 2][:], AF.Silu), 1)
            for t in range(NT):
                gt = tb + t
                s.wait_ge(ystv.h, ystv.b + (gt + 1))
                sself.inc(s.activation(ycol[gt % 2][:, 1:2], ycol[gt % 2][:, 0:1], AF.Sqrt,
                                       scale=4.0 / D, bias=ctx["ccol"][:, 1:2]), 1)
                s.wait_ge(sself.h, sself.v)
                yst.inc(s.copy(ctx["ccol"][:, 6:7], ycol[gt % 2][:, 1:2]), 1)

    @block.vector
    def _(v):
        for i in range(3):
            v.memset(colb[i][:], 0.0)
        for i in range(2):
            v.memset(ycol[i][:], 0.0)
        for p in range(NPANEL):
            tb = p * NT
            for t in range(NT):
                gt = tb + t
                v.wait_ge(lxs[gt % 2].h, lxs[gt % 2].b + (16 * (gt // 2 + 1)))
                if gt >= 3:
                    v.wait_ge(uap.h, uap.b + (gt - 2))
                cb = colb[gt % 3]
                def vE(op):
                    vself.inc(op, 1)
                    v.wait_ge(vself.h, vself.v)
                    return op
                vE(v.tensor_scalar(scr_f[:], xt[gt % 2][:], 1.0, 0.0, ALU.mult, ALU.add,
                                   accum_out=cb[:, 0:1]))
                vE(v.tensor_scalar(cb[:, 2:3], cb[:, 0:1], -1.0 / D, None, ALU.mult))
                vE(v.scalar_tensor_tensor(scr_f[:], xt[gt % 2][:], cb[:, 2:3], xt[gt % 2][:],
                                          ALU.add, ALU.mult, accum_out=cb[:, 5:6]))
                csv.inc(v.tensor_copy(cb[:, 6:7], cb[:, 5:6]), 1)
                v.wait_ge(css.h, css.b + (gt + 1))
                vE(v.reciprocal(cb[:, 3:4], cb[:, 7:8]))
                if gt >= 3:
                    v.wait_ge(utr.h, utr.b + (16 * (gt - 2)))
                uap.inc(v.tensor_scalar(u1tm[gt % 3][:], xt[gt % 2][:], cb[:, 2:3], cb[:, 3:4],
                                        ALU.add, ALU.mult), 1)
                for g in range(0, GB, 2):
                    v.wait_ge(utr.h, utr.b + (16 * gt + g + 1))
                    uev_v.inc(v.tensor_copy(u1fm[:, g, 128 * t:128 * t + 128], ps_tr[g % 2]), 1)
            for sp2 in range(22):
                for half in (0, 1):
                    pp = p * 44 + 2 * sp2 + half
                    sidx = 2 * sp2 + half
                    v.wait_ge(mmg.h, mmg.b + (2 * pp + 2))
                    v.wait_ge(hevs.h, hevs.b + (pp + 1))
                    hevv.inc(v.tensor_tensor(hfm[:, sidx, :], sil[pp % 2][:], ps[2 + pp % 2][:],
                                             ALU.mult), 1)
            for of4 in range(4):
                for t in range(NT):
                    gt = tb + t
                    v.wait_ge(mmd.h, mmd.b + ((p * 4 + of4) * 44 + 44))
                    dev.inc(v.tensor_copy(yq[t][:, 512 * of4:512 * of4 + 512],
                                          ps[t][:]), 1)
            for t in range(NT):
                gt = tb + t
                if gt >= 2:
                    v.wait_ge(fin.h, fin.b + (gt - 1))
                vself.inc(v.scalar_tensor_tensor(scr_f[:], yq[t][:], 1.0, yq[t][:],
                                                 ALU.mult, ALU.mult,
                                                 accum_out=ycol[gt % 2][:, 0:1]), 1)
                v.wait_ge(vself.h, vself.v)
                ystv.inc(v.tensor_copy(ycol[gt % 2][:, 3:4], ycol[gt % 2][:, 0:1]), 1)
                v.wait_ge(yst.h, yst.b + (gt + 1))
                vself.inc(v.reciprocal(ycol[gt % 2][:, 2:3], ycol[gt % 2][:, 1:2]), 1)
                v.wait_ge(vself.h, vself.v)
                v.wait_ge(lx2s[gt % 2].h, lx2s[gt % 2].b + (16 * (gt // 2 + 1)))
                if gt >= 2:
                    v.wait_ge(stxs[gt % 2].h, 16 * (gt // 2))
                if out_ap is None:
                    ins = v.scalar_tensor_tensor(xo[gt % 2][:], yq[t][:], ycol[gt % 2][:, 2:3],
                                                 xt2[gt % 2][:], ALU.mult, ALU.add)
                else:
                    dst = xo[gt % 2][:].rearrange("p (r c q) -> p r c q", r=8, q=2)
                    ysrc = yq[t][:].rearrange("p (r h c) -> p r h c", r=8, h=2)
                    x2src = xt2[gt % 2][:].rearrange("p (r h c) -> p r h c", r=8, h=2)
                    v.scalar_tensor_tensor(
                        dst[:, :, :, 0], ysrc[:, :, 0, :], ycol[gt % 2][:, 2:3],
                        x2src[:, :, 0, :], ALU.mult, ALU.add)
                    ins = v.scalar_tensor_tensor(
                        dst[:, :, :, 1], ysrc[:, :, 1, :], ycol[gt % 2][:, 2:3],
                        x2src[:, :, 1, :], ALU.mult, ALU.add)
                fin.inc(ins, 1)
        fin.inc(v.tensor_copy(scr_f[:, 0:2], scr_f[:, 0:2]), 1)

    @block.tensor
    def _(t_):
        for p in range(NPANEL):
            tb = p * NT
            if p > 0:
                t_.wait_ge(dev.h, dev.b + (p * 4 * NT))
            for t in range(NT):
                gt = tb + t
                t_.wait_ge(uap.h, uap.b + (gt + 1))
                for g in range(GB):
                    prev = 8 * gt + g // 2
                    if prev > 0:
                        if g % 2 == 0:
                            t_.wait_ge(uev_v.h, uev_v.b + (prev))
                        else:
                            t_.wait_ge(uev_s.h, uev_s.b + (prev))
                    utr.inc(t_.transpose(ps_tr[g % 2], u1tm[gt % 3][:, 128 * g:128 * g + 128],
                                         identb[:]), 1)
            t_.wait_ge(uev_v.h, uev_v.b + (8 * (tb + NT)))
            t_.wait_ge(uev_s.h, uev_s.b + (8 * (tb + NT)))
            for sp2 in range(22):
                it = p * 22 + sp2
                t_.wait_ge(lws[it % 2].h, lws[it % 2].b + (64 * (it // 2 + 1)))
                for half in (0, 1):
                    pp = p * 44 + 2 * sp2 + half
                    if pp >= 2:
                        t_.wait_ge(hevs.h, hevs.b + (pp - 1))
                        t_.wait_ge(hevv.h, hevv.b + (pp - 1))
                    wg = wg_t[it % 2]
                    wu = wu_t[it % 2]
                    for g in range(GB):
                        mm = t_.matmul(ps[pp % 2][:],
                                       wg[:, half, 128 * g:128 * g + 128],
                                       u1fm[:, g, :],
                                       start=(g == 0), stop=(g == GB - 1))
                    mmg.inc(mm, 1)
                    for g in range(GB):
                        mm = t_.matmul(ps[2 + pp % 2][:],
                                       wu[:, half, 128 * g:128 * g + 128],
                                       u1fm[:, g, :],
                                       start=(g == 0), stop=(g == GB - 1))
                    mmg.inc(mm, 1)
            t_.wait_ge(hevv.h, hevv.b + ((p + 1) * 44))
            for of4 in range(4):
                for s_ in range(44):
                    it = (p * 4 + of4) * 44 + s_
                    t_.wait_ge(ldws[it % 2].h, ldws[it % 2].b + (16 * (it // 2 + 1)))
                    if s_ == 0 and (p > 0 or of4 > 0):
                        t_.wait_ge(dev.h, dev.b + ((p * 4 + of4) * NT))
                    for t in range(NT):
                        mm = t_.matmul(ps[t][:],
                                       hfm[:, s_, 128 * t:128 * t + 128],
                                       wd_t[it % 2][:],
                                       start=(s_ == 0), stop=(s_ == 43))
                    mmd.inc(mm, 1)

    @block.gpsimd
    def _(g):
        dst = out_ap if out_ap is not None else xout_ap
        for gt in range(ntile):
            g.wait_ge(fin.h, fin.b + (min(gt + 2, ntile + 1)))
            stxs[gt % 2].inc(g.dma_start(out=dst[128 * gt:128 * gt + 128, :],
                                         in_=xo[gt % 2][:]), 16)
        if dbg is not None:
            g.wait_ge(fin.h, fin.b + (ntile))
            stxs[0].inc(g.dma_start(out=dbg["u1fm"][:], in_=u1fm[:]), 16)
            stxs[0].inc(g.dma_start(out=dbg["hfm"][:], in_=hfm[:]), 16)
            for t in range(NT):
                stxs[0].inc(g.dma_start(out=dbg["yq"][128 * t:128 * t + 128, :], in_=yq[t][:]), 16)
            stxs[0].inc(g.dma_start(out=dbg["cols"][:, 0:8], in_=ctx["ccol"][:]), 16)
            for i in range(3):
                stxs[0].inc(g.dma_start(out=dbg["u1tm"][128 * i:128 * i + 128, :], in_=u1tm[i][:]), 16)
            for i in range(2):
                stxs[0].inc(g.dma_start(out=dbg["xt"][128 * i:128 * i + 128, :], in_=xt[i][:]), 16)
            for i in range(3):
                stxs[0].inc(g.dma_start(out=dbg["cols"][:, 8 + 8 * i:16 + 8 * i], in_=colb[i][:]), 16)
            for i in range(2):
                stxs[0].inc(g.dma_start(out=dbg["cols"][:, 32 + 4 * i:36 + 4 * i], in_=ycol[i][:]), 16)

    es.close()
    pes.close()
    _stage_out(ctx, *stxs, fin, *lxs, *lx2s, *lws, *ldws, mmg, mmd, hevs, hevv, dev,
               uap, uev_v, uev_s, utr, csv, css, yst, ystv)


def emit_b01(ctx, xbuf1, vbuf, win2, zbuf, xbcbuf, dtbuf):
    nc, block, sem = ctx["nc"], ctx["block"], ctx["sem"]
    identb = ctx["identb"]
    ntile = globals().get("B01_TILES", T // 128)
    nstrip = globals().get("B01_STRIPS", 65)

    lxs = [sem("b01_lx0"), sem("b01_lx1")]
    vself = sem("b01_vself")
    sself = sem("b01_sself")
    csv = sem("b01_csv")
    css = sem("b01_css")
    uap = sem("b01_uap")
    utr = sem("b01_utr")
    uev_v = sem("b01_uevv")
    uev_s = sem("b01_uevs")
    stv = [sem("b01_stv0"), sem("b01_stv1")]
    lwi = [sem("b01_lwi0"), sem("b01_lwi1")]
    mmi = sem("b01_mmi")
    evv = sem("b01_evv")
    evs = sem("b01_evs")
    stzv = [sem(f"b01_stzv{i}") for i in range(2)]
    stzs = [sem(f"b01_stzs{i}") for i in range(2)]

    es = contextlib.ExitStack()
    sb = lambda n, sh, dt: es.enter_context(nc.sbuf_tensor(f"b01_{n}", sh, dt))
    xt = [sb(f"xt{i}", [128, D], F32) for i in range(2)]
    scr_f = sb("scr_f", [128, D], F32)
    vt = [sb(f"vt{i}", [128, D], F32) for i in range(2)]
    h1tm = [sb(f"h1tm{i}", [128, D], BF16) for i in range(3)]
    colb = [sb(f"col{i}", [128, 12], F32) for i in range(3)]
    h1fm = es.enter_context(nc.sbuf_tensor("b01_h1fm", [128, GB, T], BF16))
    wstrip = [sb(f"wi{i}", [128, GB * 128], BF16) for i in range(2)]
    ebf_v = [sb(f"ebfv{i}", [128, 512], F32) for i in range(2)]
    ebf_s = [sb(f"ebfs{i}", [128, 512], F32) for i in range(2)]
    ebb_v = [t[:].bitcast(BF16)[:, 0:512] for t in ebf_v]
    ebb_s = [t[:].bitcast(BF16)[:, 0:512] for t in ebf_s]

    pes = contextlib.ExitStack()
    ps_tr_t = [pes.enter_context(nc.psum_tensor(f"b01_tr{i}", [128, 128], BF16))
               for i in range(2)]
    ps_tr = [ps_tr_t[0][:], ps_tr_t[1][:]]
    psA = [pes.enter_context(nc.psum_tensor(f"b01_ps{i}", [128, 512], F32))
           for i in range(2)]

    NCH = max(1, ntile * 128 // 512)

    @block.sync
    def _(s):
        for gt in range(ntile + 1):
            if gt < ntile:
                if gt >= 2:
                    s.wait_ge(uap.h, gt - 1)
                lxs[gt % 2].inc(s.dma_start(out=xt[gt % 2][:],
                                            in_=xbuf1[128 * gt:128 * gt + 128, :]), 16)
            if gt >= 1:
                g2 = gt - 1
                s.wait_ge(uap.h, g2 + 2)
                stv[g2 % 2].inc(s.dma_start(out=vbuf[128 * g2:128 * g2 + 128, :],
                                            in_=vt[g2 % 2][:]), 16)
        for s_ in range(nstrip):
            for c in range(NCH):
                idx = s_ * NCH + c
                k = idx // 2
                if c % 2 == 0:
                    s.wait_ge(evv.h, k + 1)
                    ef, eb, stk = ebf_v[k % 2][:], ebb_v[k % 2], stzv[k % 2]
                else:
                    s.wait_ge(evs.h, k + 1)
                    ef, eb, stk = ebf_s[k % 2][:], ebb_s[k % 2], stzs[k % 2]
                if s_ < 16:
                    stk.inc(s.dma_start(
                        out=zbuf[128 * s_:128 * s_ + 128, 512 * c:512 * c + 512],
                        in_=ef), 16)
                elif s_ < 64:
                    r0 = 128 * (s_ - 16)
                    stk.inc(s.dma_start(
                        out=xbcbuf[r0:r0 + 128, 512 * c:512 * c + 512],
                        in_=eb), 16)
                else:
                    stk.inc(s.dma_start(
                        out=dtbuf[:, 512 * c:512 * c + 512],
                        in_=ef[0:32, :] if hasattr(ef, '__getitem__') else ef), 16)

    b01dbg = ctx.get("b01dbg")

    @block.gpsimd
    def _(g):
        for s_ in range(nstrip):
            if s_ >= 2:
                g.wait_ge(mmi.h, NCH * (s_ - 1))
            lwi[s_ % 2].inc(g.dma_start(out=wstrip[s_ % 2][:], in_=win2[s_]), 16)
        if b01dbg is not None:
            g.wait_ge(uap.h, ntile + 1)
            for i in range(3):
                g.dma_start(out=b01dbg[:, 12 * i:12 * i + 12], in_=colb[i][:]).then_inc(
                    lwi[0].h, 16)
            lwi[0].v += 48

    @block.vector
    def _(v):
        for i in range(3):
            v.memset(colb[i][:], 0.0)
        for gt in range(ntile):
            v.wait_ge(lxs[gt % 2].h, 16 * (gt // 2 + 1))
            if gt >= 3:
                v.wait_ge(uap.h, gt - 2)
            cb = colb[gt % 3]
            def vE(op):
                vself.inc(op, 1)
                v.wait_ge(vself.h, vself.v)
                return op
            vE(v.tensor_scalar(scr_f[:], xt[gt % 2][:], 1.0, 0.0, ALU.mult, ALU.add,
                               accum_out=cb[:, 0:1]))
            vE(v.tensor_scalar(cb[:, 2:3], cb[:, 0:1], -1.0 / D, None, ALU.mult))
            vE(v.scalar_tensor_tensor(scr_f[:], xt[gt % 2][:], cb[:, 2:3], xt[gt % 2][:],
                                      ALU.add, ALU.mult, accum_out=cb[:, 5:6]))
            csv.inc(v.tensor_copy(scr_f[:, 0:1], cb[:, 5:6]), 1)
            v.wait_ge(css.h, 2 * gt + 1)
            vE(v.reciprocal(cb[:, 3:4], cb[:, 6:7]))
            vE(v.tensor_tensor(cb[:, 11:12], cb[:, 5:6], cb[:, 3:4], ALU.mult))
            vE(v.tensor_tensor(cb[:, 7:8], cb[:, 11:12], cb[:, 3:4], ALU.mult))
            csv.inc(v.tensor_copy(scr_f[:, 0:1], cb[:, 7:8]), 1)
            v.wait_ge(css.h, 2 * gt + 2)
            vE(v.reciprocal(cb[:, 9:10], cb[:, 8:9]))
            vE(v.tensor_tensor(cb[:, 10:11], cb[:, 3:4], cb[:, 9:10], ALU.mult))
            if gt >= 2:
                v.wait_ge(stv[gt % 2].h, 16 * (gt // 2))
            v.tensor_scalar(vt[gt % 2][:], xt[gt % 2][:], cb[:, 2:3], cb[:, 3:4],
                            ALU.add, ALU.mult)
            if gt >= 3:
                v.wait_ge(utr.h, 16 * (gt - 2))
            v.tensor_scalar(scr_f[:], xt[gt % 2][:], cb[:, 2:3], cb[:, 10:11],
                            ALU.add, ALU.mult)
            uap.inc(v.tensor_tensor(h1tm[gt % 3][:], scr_f[:], ctx["n1w"][:], ALU.mult), 1)
            for g in range(0, GB, 2):
                v.wait_ge(utr.h, 16 * gt + g + 1)
                uev_v.inc(v.tensor_copy(h1fm[:, g, 128 * gt:128 * gt + 128], ps_tr[g % 2]), 1)
        uap.inc(v.tensor_copy(scr_f[:, 0:2], scr_f[:, 0:2]), 1)
        for s_ in range(nstrip):
            for c in range(0, NCH, 2):
                idx = s_ * NCH + c
                k = idx // 2
                v.wait_ge(mmi.h, idx + 1)
                if k >= 2:
                    v.wait_ge(stzv[k % 2].h, 16 * (k // 2))
                dst = ebf_v[k % 2][:] if (s_ < 16 or s_ == 64) else ebb_v[k % 2]
                evv.inc(v.tensor_copy(dst, psA[c % 2][:]), 1)
        evv.inc(v.tensor_copy(scr_f[:, 0:2], scr_f[:, 0:2]), 1)

    @block.scalar
    def _(s):
        for gt in range(ntile):
            s.wait_ge(csv.h, 2 * gt + 1)
            sself.inc(s.activation(colb[gt % 3][:, 6:7], colb[gt % 3][:, 5:6], AF.Sqrt,
                                   scale=1.0 / D, bias=ctx["ccol"][:, 0:1]), 1)
            s.wait_ge(sself.h, sself.v)
            css.inc(s.copy(ctx["ccol"][:, 6:7], colb[gt % 3][:, 6:7]), 1)
            s.wait_ge(csv.h, 2 * gt + 2)
            sself.inc(s.activation(colb[gt % 3][:, 8:9], colb[gt % 3][:, 7:8], AF.Sqrt,
                                   scale=1.0 / D, bias=ctx["ccol"][:, 2:3]), 1)
            s.wait_ge(sself.h, sself.v)
            css.inc(s.copy(ctx["ccol"][:, 6:7], colb[gt % 3][:, 8:9]), 1)
            for g in range(1, GB, 2):
                s.wait_ge(utr.h, 16 * gt + g + 1)
                uev_s.inc(s.copy(h1fm[:, g, 128 * gt:128 * gt + 128], ps_tr[g % 2]), 1)
        for s_ in range(nstrip):
            for c in range(1, NCH, 2):
                idx = s_ * NCH + c
                k = idx // 2
                s.wait_ge(mmi.h, idx + 1)
                if k >= 2:
                    s.wait_ge(stzs[k % 2].h, 16 * (k // 2))
                dst = ebf_s[k % 2][:] if (s_ < 16 or s_ == 64) else ebb_s[k % 2]
                evs.inc(s.copy(dst, psA[c % 2][:]), 1)
        evs.inc(s.copy(ctx["ccol"][:, 4:5], ctx["ccol"][:, 5:6]), 1)

    @block.tensor
    def _(t_):
        for gt in range(ntile):
            t_.wait_ge(uap.h, gt + 1)
            for g in range(GB):
                prev = 8 * gt + g // 2
                if prev > 0:
                    if g % 2 == 0:
                        t_.wait_ge(uev_v.h, prev)
                    else:
                        t_.wait_ge(uev_s.h, prev)
                utr.inc(t_.transpose(ps_tr[g % 2], h1tm[gt % 3][:, 128 * g:128 * g + 128],
                                     identb[:]), 1)
        t_.wait_ge(uev_v.h, 8 * ntile)
        t_.wait_ge(uev_s.h, 8 * ntile)
        for s_ in range(nstrip):
            t_.wait_ge(lwi[s_ % 2].h, 16 * (s_ // 2 + 1))
            for c in range(NCH):
                idx = s_ * NCH + c
                if idx >= 2:
                    if c % 2 == 0:
                        t_.wait_ge(evv.h, idx // 2)
                    else:
                        t_.wait_ge(evs.h, (idx - 1) // 2)
                for g in range(GB):
                    mm = t_.matmul(psA[c % 2][:],
                                   wstrip[s_ % 2][:, 128 * g:128 * g + 128],
                                   h1fm[:, g, 512 * c:512 * c + 512],
                                   start=(g == 0), stop=(g == GB - 1))
                mmi.inc(mm, 1)

    es.close()
    pes.close()
    _stage_out(ctx, *stzv, *stzs, *stv, *lxs, *lwi, mmi, evv, evs, uap, utr,
               uev_v, uev_s, csv, css)
    return None



def mk_W(ctx, dep):
    wm = ctx.setdefault("wmark", {})

    def WW(e, s_, v_):
        if v_ <= 0:
            return
        k = (e.__class__.__name__, id(s_))
        if wm.get(k, -1) >= v_:
            return
        e.wait_ge(s_.h, v_)
        wm[k] = v_

    def W(e, key):
        d = dep.get(key)
        if d:
            if isinstance(d, list):
                for s_, v_ in d:
                    WW(e, s_, v_)
            else:
                WW(e, d[0], d[1])
    return W, WW


def emit_b2(ctx, zbuf, xbcbuf, dtbuf, prodbuf, cumdram):
    nc, block, sem = ctx["nc"], ctx["block"], ctx["sem"]
    identf, identb = ctx["identf"], ctx["identb"]
    ones, cw, cbias, zb, dbc, ccol = (ctx["ones"], ctx["cw"], ctx["cb"], ctx["zb"],
                                      ctx["dbc"], ctx["ccol"])
    trilu = ctx["trilu"]
    CH = 256
    NCH2 = L // CH

    tmm, vv, ss, gg = (ctx["geng"]["t"], ctx["geng"]["v"], ctx["geng"]["s"],
                       ctx["geng"]["g"])
    ldz = [ctx["gswd"][1], Sem(ctx["stack"], ctx["nc"], "b2_zswd")]
    ldx = ctx["gdma"][2:4]
    ldd = ctx["gdma"][4:6]
    stp = ctx["gdma"][6:8]

    es = contextlib.ExitStack()
    sb = lambda n, sh, dt: es.enter_context(nc.sbuf_tensor(f"b2_{n}", sh, dt))
    Hst = sb("Hst", [128, (NH // 2) * 64], F32)
    zt = [sb(f"zt{i}", [128, GB, CH], BF16) for i in range(2)]
    zsil = sb("zsil", [128, GB, CH], BF16)
    xbc = [sb(f"xbc{i}", [128, 48, CH + 4], BF16) for i in range(2)]
    dtt = [sb(f"dtt{i}", [32, CH], F32) for i in range(2)]
    convA = sb("convA", [128, 48, 128], BF16)
    convT = sb("convT", [128, 48, 128], BF16)
    convB = [sb(f"convB{i}", [128, 48, 128], BF16) for i in range(2)]
    xT = sb("xT", [128, GB * 128], BF16)
    BT = sb("BT", [128, GB * 128], BF16)
    Bd_all = sb("Bd_all", [128, NH, 64], BF16)
    cum = sb("cum", [32, 128], F32)
    sp_t = sb("sp", [32, 128], F32)
    spe = sb("spe", [32, 128], F32)
    spl = sb("spl", [32, 128], F32)
    spr = sb("spr", [32, 128], F32)
    cum_tok = sb("cum_tok", [128, 32], F32)
    ncum_tok = sb("ncum_tok", [128, 32], F32)
    dec_fm = sb("dec_fm", [32, 128], F32)
    dec_tok = sb("dec_tok", [128, 32], F32)
    acol = sb("acol", [32, 1], F32)
    abct = sb("abct", [32, 128], F32)
    A_bc = sb("A_bc", [128, 32], F32)
    seg_all = sb("seg_all", [128, 32, 128], BF16)
    sc_m = [sb(f"sc_m{i}", [128, 128], BF16) for i in range(2)]
    R_all = sb("R_all", [128, 32, 128], F32)
    E_all = sb("E_all", [128, 32, 128], BF16)
    Cs = [sb(f"Cs{i}", [128, 128], F32) for i in range(2)]
    y_sb = [sb(f"y_sb{i}", [128, 128], BF16) for i in range(2)]
    y_fm = sb("y_fm", [128, GB, CH], BF16)

    pes = contextlib.ExitStack()
    psal = lambda n, sh, dt=F32: pes.enter_context(nc.psum_tensor(f"b2_{n}", sh, dt))
    ps_cbS = [psal(f"cbS{i}", [128, 192]) for i in range(2)]
    ps_y = [psal(f"y{i}", [128, 128]) for i in range(2)]
    ps_tr = [psal(f"tr{i}", [128, 128], BF16) for i in range(2)]
    ps_trf = [psal("trf0", [128, 128])]

    dep = {}
    W, WW = mk_W(ctx, dep)

    for b in range(SEQ_PER_CORE):
        @block.vector
        def _(v):
            W(v, "H")
            vv.inc(v.memset(Hst[:], 0.0), 1)
        dep["H"] = (vv, vv.v)
        dep["Hread"] = (vv, vv.v)

        if b == 0 and ctx.get("b2dbg"):
            dsem2 = ctx["gdma"][11]
            mk = (vv, vv.v)

            @block.sync
            def _(s):
                WW(s, mk[0], mk[1])
                dsem2.inc(s.dma_start(out=ctx["b2dbg"]["Hst0"][:], in_=Hst[:]), 16)
            dep["Hread"] = [dep["Hread"], (dsem2, dsem2.v)]

        for cc in range(NCH2):
            it = b * NCH2 + cc
            t0 = b * L + cc * CH
            par = it % 2
            nz = it // 2 + 1

            @block.gpsimd
            def _(g):
                W(g, f"zt{par}")
                for gq in range(4):
                    ldz[par].inc(g.dma_start(
                        out=zt[par][:, 4 * gq:4 * gq + 4, :],
                        in_=zbuf.rearrange("(g p) t -> p g t", p=128)
                            [:, 4 * gq:4 * gq + 4, t0:t0 + CH]), 16)

            @block.sync
            def _(s):
                W(s, f"xbc{par}")
                if cc == 0:
                    for gq in range(4):
                        ldx[par].inc(s.dma_start(
                            out=xbc[par][:, 12 * gq:12 * gq + 12, 4:4 + CH],
                            in_=xbcbuf.rearrange("(g p) t -> p g t", p=128)
                                [:, 12 * gq:12 * gq + 12, t0:t0 + CH]), 16)
                else:
                    for gq in range(4):
                        ldx[par].inc(s.dma_start(
                            out=xbc[par][:, 12 * gq:12 * gq + 12, 1:4 + CH],
                            in_=xbcbuf.rearrange("(g p) t -> p g t", p=128)
                                [:, 12 * gq:12 * gq + 12, t0 - 3:t0 + CH]), 16)
                W(s, f"dtt{par}")
                ldd[par].inc(s.dma_start(out=dtt[par][:], in_=dtbuf[:, t0:t0 + CH]), 16)
                dep[f"zld{par}"] = (ldz[par], ldz[par].v)
                dep[f"xld{par}"] = (ldx[par], ldx[par].v)
                dep[f"dld{par}"] = (ldd[par], ldd[par].v)

            if cc == 0:
                @block.vector
                def _(v):
                    W(v, f"xld{par}")
                    vv.inc(v.memset(xbc[par][:, :, 0:4], 0.0), 1)

            xbc_deps = []
            zt_deps = []
            dtt_deps = []

            for sc in range(2):
                j0 = sc * 128
                isc = it * 2 + sc
                par2 = isc % 2

                if b == 0 and cc == 0 and sc == 1 and ctx.get("b2dbg"):
                    dsem2 = ctx["gdma"][11]
                    mk = (vv, vv.v)

                    @block.sync
                    def _(s):
                        WW(s, mk[0], mk[1])
                        dsem2.inc(s.dma_start(out=ctx["b2dbg"]["Hmid"][:], in_=Hst[:]), 16)
                    hr = dep.get("Hread")
                    dep["Hread"] = (hr if isinstance(hr, list) else [hr]) + [(dsem2, dsem2.v)]

                @block.vector
                def _(v):
                    W(v, f"xld{par}")
                    W(v, f"convB{par2}")
                    xsl = lambda k: xbc[par][:, :, j0 + 1 + k:j0 + 129 + k]
                    cwB = lambda k: cw[:, :, k:k + 1].to_broadcast([128, 48, 128])
                    cbB = cbias[:, :, None].to_broadcast([128, 48, 128])
                    v.tensor_tensor(convA[:], xsl(0), cwB(0), ALU.mult)
                    v.tensor_tensor(convT[:], xsl(1), cwB(1), ALU.mult)
                    v.tensor_tensor(convA[:], convA[:], convT[:], ALU.add)
                    v.tensor_tensor(convT[:], xsl(2), cwB(2), ALU.mult)
                    v.tensor_tensor(convA[:], convA[:], convT[:], ALU.add)
                    v.tensor_tensor(convT[:], xsl(3), cwB(3), ALU.mult)
                    v.tensor_tensor(convA[:], convA[:], convT[:], ALU.add)
                    ins = v.tensor_tensor(convB[par2][:], convA[:], cbB, ALU.add)
                    vv.inc(ins, 1)
                convB_ready = (vv, vv.v)
                xbc_deps.append(convB_ready)

                @block.scalar
                def _(s):
                    W(s, f"dld{par}")
                    W(s, "spbufs")
                    dts = dtt[par][:, j0:j0 + 128]
                    ss.inc(s.activation(spe[:], dts, AF.Abs), 1)
                    s.wait_ge(ss.h, ss.v)
                    ss.inc(s.activation(spl[:], spe[:], AF.Exp, scale=-1.0), 1)
                    s.wait_ge(ss.h, ss.v)
                    ss.inc(s.activation(spe[:], spl[:], AF.Ln,
                                        bias=ccol[0:32, 3:4]), 1)
                    s.wait_ge(ss.h, ss.v)
                    ss.inc(s.activation(spr[:], dts, AF.Relu), 1)
                    s.wait_ge(ss.h, ss.v)
                    ss.inc(s.copy(ccol[0:32, 6:7], spr[:, 0:1]), 1)
                sp_ready = (ss, ss.v)
                dtt_deps.append(sp_ready)

                @block.vector
                def _(v):
                    WW(v, sp_ready[0], sp_ready[1])
                    W(v, "cumbufs")
                    vv.inc(v.tensor_tensor(sp_t[:], spr[:], spe[:], ALU.add), 1)
                    v.wait_ge(vv.h, vv.v)
                    vv.inc(v.tensor_tensor_scan(cum[:], ones[0:32, :], sp_t[:],
                                                0.0, ALU.mult, ALU.subtract), 1)
                    v.wait_ge(vv.h, vv.v)
                    vv.inc(v.tensor_scalar(dec_fm[:], cum[:], cum[:, 127:128], None,
                                           ALU.subtract), 1)
                    v.wait_ge(vv.h, vv.v)
                dep["spbufs"] = (vv, vv.v)
                cum_ready = (vv, vv.v)

                @block.tensor
                def _(t_):
                    WW(t_, cum_ready[0], cum_ready[1])
                    W(t_, "trfuse0")
                    tmm.inc(t_.transpose(ps_trf[0][0:128, 0:32], cum[:], identf[0:32, 0:32]), 1)
                cumtr = (tmm, tmm.v)

                @block.vector
                def _(v):
                    WW(v, cumtr[0], cumtr[1])
                    W(v, "cumtokbufs")
                    vv.inc(v.tensor_copy(cum_tok[:], ps_trf[0][0:128, 0:32]), 1)
                    v.wait_ge(vv.h, vv.v)
                    vv.inc(v.tensor_scalar(ncum_tok[:], cum_tok[:], -1.0, None,
                                           ALU.mult), 1)
                    v.wait_ge(vv.h, vv.v)
                dep["trfuse0"] = (vv, vv.v)
                cumtok_ready = (vv, vv.v)

                cdsem = ctx["gdma"][8 + par2]

                @block.sync
                def _(s):
                    WW(s, cum_ready[0], cum_ready[1])
                    W(s, f"cumd{par2}")
                    cdsem.inc(s.dma_start(out=cumdram[par2], in_=cum[:]), 16)
                    s.wait_ge(cdsem.h, cdsem.v)
                    W(s, "Ruse")
                    W(s, "RuseE")
                    cdsem.inc(s.dma_start(
                        out=R_all[:].rearrange("p h f -> p (h f)"),
                        in_=cumdram.rearrange("s h f -> s (h f)")[par2][None, :]
                            .to_broadcast([128, 32 * 128])), 16)
                    dep[f"cumd{par2}"] = (cdsem, cdsem.v)
                rload = (cdsem, cdsem.v)

                @block.scalar
                def _(s):
                    WW(s, rload[0], rload[1])
                    W(s, "Eallguard")
                    ss.inc(s.activation(E_all[:].rearrange("p h f -> p (h f)"),
                                        R_all[:].rearrange("p h f -> p (h f)"),
                                        AF.Exp), 1)
                eall_ready = (ss, ss.v)
                dep["RuseE"] = eall_ready

                @block.vector
                def _(v):
                    WW(v, rload[0], rload[1])
                    WW(v, cumtok_ready[0], cumtok_ready[1])
                    W(v, "segallE")
                    ncB = ncum_tok[:, :, None].to_broadcast([128, 32, 128])
                    vv.inc(v.tensor_tensor(seg_all[:], R_all[:], ncB,
                                           ALU.add), 1)
                    v.wait_ge(vv.h, vv.v)
                    vv.inc(v.tensor_scalar(
                        seg_all[:].rearrange("p h f -> p (h f)"),
                        seg_all[:].rearrange("p h f -> p (h f)"),
                        0.0, None, ALU.min), 1)
                segadd_ready = (vv, vv.v)
                dep["Ruse"] = segadd_ready

                @block.scalar
                def _(s):
                    WW(s, segadd_ready[0], segadd_ready[1])
                    ss.inc(s.activation(seg_all[:].rearrange("p h f -> p (h f)"),
                                        seg_all[:].rearrange("p h f -> p (h f)"),
                                        AF.Exp), 1)
                segexp_ready = (ss, ss.v)
                dep["segallE"] = segexp_ready

                @block.vector
                def _(v):
                    WW(v, segexp_ready[0], segexp_ready[1])
                    trilB = trilu[:, None, :].to_broadcast([128, 32, 128])
                    vv.inc(v.tensor_tensor(seg_all[:], seg_all[:], trilB,
                                           ALU.mult), 1)
                segm_ready = (vv, vv.v)

                @block.scalar
                def _(s):
                    WW(s, cum_ready[0], cum_ready[1])
                    ss.inc(s.activation(dec_fm[:], dec_fm[:], AF.Exp, scale=-1.0), 1)
                    s.wait_ge(ss.h, ss.v)
                    ss.inc(s.activation(acol[:], cum[:, 127:128], AF.Exp), 1)
                    s.wait_ge(ss.h, ss.v)
                    ss.inc(s.copy(ccol[0:32, 6:7], acol[:, 0:1]), 1)
                dec_ready = (ss, ss.v)

                @block.vector
                def _(v):
                    WW(v, dec_ready[0], dec_ready[1])
                    vv.inc(v.tensor_scalar(abct[:], ones[0:32, :], acol[:, 0:1],
                                           None, ALU.mult), 1)
                    v.wait_ge(vv.h, vv.v)
                abct_ready = (vv, vv.v)

                @block.tensor
                def _(t_):
                    WW(t_, abct_ready[0], abct_ready[1])
                    W(t_, "trfuse0")
                    tmm.inc(t_.transpose(ps_trf[0][0:128, 0:32], abct[:],
                                         identf[0:32, 0:32]), 1)
                abctr = (tmm, tmm.v)

                @block.vector
                def _(v):
                    WW(v, abctr[0], abctr[1])
                    vv.inc(v.tensor_copy(A_bc[:], ps_trf[0][0:128, 0:32]), 1)
                    v.wait_ge(vv.h, vv.v)
                dep["trfuse0"] = (vv, vv.v)
                abc_ready = (vv, vv.v)

                @block.tensor
                def _(t_):
                    WW(t_, dec_ready[0], dec_ready[1])
                    W(t_, "trfuse0")
                    tmm.inc(t_.transpose(ps_trf[0][0:128, 0:32], dec_fm[:], identf[0:32, 0:32]), 1)
                dectr = (tmm, tmm.v)

                @block.vector
                def _(v):
                    WW(v, dectr[0], dectr[1])
                    vv.inc(v.tensor_copy(dec_tok[:], ps_trf[0][0:128, 0:32]), 1)
                    v.wait_ge(vv.h, vv.v)
                dep["trfuse0"] = (vv, vv.v)
                dectok_ready = (vv, vv.v)


                @block.tensor
                def _(t_):
                    WW(t_, convB_ready[0], convB_ready[1])
                    for g in range(GB):
                        W(t_, f"tr{g % 2}")
                        tmm.inc(t_.transpose(ps_tr[g % 2][:], convB[par2][:, g, :],
                                             identb[:]), 1)
                        dep[f"trmm{g % 2}"] = (tmm, tmm.v)

                        @block.vector
                        def _(v, g=g):
                            v.wait_ge(dep[f"trmm{g % 2}"][0].h, dep[f"trmm{g % 2}"][1])
                            vv.inc(v.tensor_copy(xT[:, 128 * g:128 * g + 128],
                                                 ps_tr[g % 2][:]), 1)
                            dep[f"tr{g % 2}"] = (vv, vv.v)
                    for g in range(GB):
                        W(t_, f"tr{g % 2}")
                        tmm.inc(t_.transpose(ps_tr[g % 2][:], convB[par2][:, 16 + g, :],
                                             identb[:]), 1)
                        dep[f"trmm{g % 2}"] = (tmm, tmm.v)

                        @block.vector
                        def _(v, g=g):
                            v.wait_ge(dep[f"trmm{g % 2}"][0].h, dep[f"trmm{g % 2}"][1])
                            vv.inc(v.tensor_copy(BT[:, 128 * g:128 * g + 128],
                                                 ps_tr[g % 2][:]), 1)
                            dep[f"tr{g % 2}"] = (vv, vv.v)
                xbt_ready = (vv, vv.v)

                @block.vector
                def _(v):
                    WW(v, dectok_ready[0], dectok_ready[1])
                    W(v, "Bdall")
                    dec64 = dec_tok[:, :, None].to_broadcast([128, NH, 64])
                    vv.inc(v.tensor_tensor(
                        Bd_all[:], BT[:].rearrange("p (h c) -> p h c", c=64),
                        dec64, ALU.mult), 1)
                bdall_ready = (vv, vv.v)

                pipe = {}

                def emit_g0(h):
                    r = h % 2
                    po = r * 64
                    q = h // 2
                    bb = 16 + q
                    cbk = 32 + q

                    @block.tensor
                    def _(t_):
                        WW(t_, convB_ready[0], convB_ready[1])
                        W(t_, f"cbuse{r}")
                        W(t_, f"Sevac{r}")
                        tmm.inc(t_.matmul(ps_cbS[r][:, 0:128],
                                          convB[par2][po:po + 64, bb, :],
                                          convB[par2][po:po + 64, cbk, :],
                                          start=True, stop=True), 1)
                    cb_done = (tmm, tmm.v)
                    pipe[h] = dict(cb_done=cb_done)

                def emit_g1(h):
                    r = h % 2
                    po = r * 64
                    q = h // 2
                    p2 = q % 2
                    cb_done = pipe[h]["cb_done"]

                    @block.vector
                    def _(v):
                        WW(v, cb_done[0], cb_done[1])
                        WW(v, segm_ready[0], segm_ready[1])
                        W(v, f"scmuse{r}")
                        vv.inc(v.tensor_tensor(sc_m[r][:], seg_all[:, h, :],
                                               ps_cbS[r][:, 0:128], ALU.mult), 1)
                        dep[f"cbuse{r}"] = (vv, vv.v)
                    pipe[h]["scm_ready"] = (vv, vv.v)
                    pipe[h]["cb_masked"] = (vv, vv.v)
                    pipe[h]["segE_ready"] = (vv, vv.v)

                def emit_g2(h):
                    r = h % 2
                    po = r * 64
                    scm_ready = pipe[h]["scm_ready"]
                    cb_masked = pipe[h]["cb_masked"]

                    @block.tensor
                    def _(t_):
                        WW(t_, cb_masked[0], cb_masked[1])
                        WW(t_, bdall_ready[0], bdall_ready[1])
                        tmm.inc(t_.matmul(ps_cbS[r][0:64, 128:192], Bd_all[:, h, :],
                                          xT[:, 64 * h:64 * h + 64],
                                          start=True, stop=True), 1)
                        if h >= NH - 1:
                            dep["Bdall"] = (tmm, tmm.v)
                        WW(t_, scm_ready[0], scm_ready[1])
                        WW(t_, xbt_ready[0], xbt_ready[1])
                        W(t_, f"yps{r}")
                        tmm.inc(t_.matmul(ps_y[r][:, 0:64], sc_m[r][:],
                                          xT[:, 64 * h:64 * h + 64],
                                          start=True, stop=False), 1)
                        dep[f"scmuse{r}"] = (tmm, tmm.v)
                    smm_done = (tmm, tmm.v)
                    pipe[h]["smm"] = smm_done

                def emit_pair(q):
                    p2 = q % 2
                    segE_ready = pipe[2 * q + 1]["segE_ready"]
                    smm_done = pipe[2 * q + 1]["smm"]

                    @block.vector
                    def _(v):
                        WW(v, eall_ready[0], eall_ready[1])
                        W(v, f"Csuse{p2}")
                        vv.inc(v.tensor_tensor(Cs[p2][0:64, :],
                                               convB[par2][0:64, 32 + q, :],
                                               E_all[0:64, 2 * q, :], ALU.mult), 1)
                        vv.inc(v.tensor_tensor(Cs[p2][64:128, :],
                                               convB[par2][64:128, 32 + q, :],
                                               E_all[64:128, 2 * q + 1, :], ALU.mult), 1)
                        dep["Eallguard"] = (vv, vv.v)
                    cs_ready = (vv, vv.v)

                    @block.tensor
                    def _(t_):
                        WW(t_, cs_ready[0], cs_ready[1])
                        W(t_, "Hread")
                        for h in (2 * q, 2 * q + 1):
                            po = (h % 2) * 64
                            tmm.inc(t_.matmul(ps_y[h % 2][:, 0:64],
                                              Cs[p2][po:po + 64, :],
                                              Hst[po:po + 64, 64 * q:64 * q + 64],
                                              start=False, stop=True), 1)
                        dep[f"Csuse{p2}"] = (tmm, tmm.v)
                    yfull_done = (tmm, tmm.v)

                    @block.vector
                    def _(v):
                        WW(v, yfull_done[0], yfull_done[1])
                        WW(v, smm_done[0], smm_done[1])
                        WW(v, abc_ready[0], abc_ready[1])
                        for h in (2 * q, 2 * q + 1):
                            po = (h % 2) * 64
                            vv.inc(v.scalar_tensor_tensor(
                                Hst[po:po + 64, 64 * q:64 * q + 64],
                                Hst[po:po + 64, 64 * q:64 * q + 64],
                                A_bc[po:po + 64, h:h + 1],
                                ps_cbS[h % 2][0:64, 128:192],
                                ALU.mult, ALU.add), 1)
                        v.wait_ge(vv.h, vv.v)
                        dep["Sevac0"] = (vv, vv.v)
                        dep["Sevac1"] = (vv, vv.v)
                        dep["Hread"] = (vv, vv.v)
                        W(v, f"ysbuse{p2}")
                        for h, po2 in ((2 * q, 0), (2 * q + 1, 64)):
                            vv.inc(v.scalar_tensor_tensor(
                                y_sb[p2][:, po2:po2 + 64],
                                xT[:, 64 * h:64 * h + 64],
                                dbc[:, h:h + 1],
                                ps_y[h % 2][:, 0:64],
                                ALU.mult, ALU.add), 1)
                        dep["yps0"] = (vv, vv.v)
                        dep["yps1"] = (vv, vv.v)
                    ysb_ready = (vv, vv.v)

                    @block.tensor
                    def _(t_):
                        WW(t_, ysb_ready[0], ysb_ready[1])
                        W(t_, f"tr{p2}")
                        tmm.inc(t_.transpose(ps_tr[p2][:], y_sb[p2][:], identb[:]), 1)
                        dep[f"ysbuse{p2}"] = (tmm, tmm.v)
                    ytr_done = (tmm, tmm.v)

                    @block.vector
                    def _(v):
                        WW(v, ytr_done[0], ytr_done[1])
                        W(v, "prodt")
                        vv.inc(v.tensor_copy(y_fm[:, q, j0:j0 + 128], ps_tr[p2][:]), 1)
                    dep[f"tr{p2}"] = (vv, vv.v)

                for step in range(NH + 2):
                    if 0 <= step - 2 < NH:
                        emit_g2(step - 2)
                        if (step - 2) % 2 == 1:
                            emit_pair((step - 2) // 2)
                    if 0 <= step - 1 < NH:
                        emit_g1(step - 1)
                    if step < NH:
                        emit_g0(step)

                dep[f"convB{par2}"] = [(tmm, tmm.v), (vv, vv.v)]

            yfm_ready = (vv, vv.v)

            @block.vector
            def _(v):
                W(v, f"zld{par}")
                W(v, "zsiluse")
                zbB = zb[:, :, None].to_broadcast([128, GB, CH])
                vv.inc(v.tensor_tensor(zsil[:], zt[par][:], zbB, ALU.add), 1)
            zsil_ready = (vv, vv.v)
            zt_deps.append(zsil_ready)

            @block.scalar
            def _(s):
                WW(s, zsil_ready[0], zsil_ready[1])
                ss.inc(s.activation(zsil[:].rearrange("p g t -> p (g t)"),
                                    zsil[:].rearrange("p g t -> p (g t)"),
                                    AF.Silu), 1)
            sil_done = (ss, ss.v)
            dep["zsiluse"] = sil_done

            @block.vector
            def _(v):
                WW(v, sil_done[0], sil_done[1])
                W(v, "prodt")
                vv.inc(v.tensor_tensor(y_fm[:], y_fm[:], zsil[:], ALU.mult), 1)
                dep["zsiluse"] = (vv, vv.v)
            prod_ready = (vv, vv.v)

            @block.sync
            def _(s):
                WW(s, prod_ready[0], prod_ready[1])
                W(s, f"stp{par}")
                for gq in range(4):
                    stp[par].inc(s.dma_start(
                        out=prodbuf.rearrange("(g p) t -> p g t", p=128)
                            [:, 4 * gq:4 * gq + 4, t0:t0 + CH],
                        in_=y_fm[:, 4 * gq:4 * gq + 4, :]), 16)
            dep["prodt"] = (stp[par], stp[par].v)
            dep[f"stp{par}"] = (stp[par], stp[par].v)

            dep[f"zt{par}"] = zt_deps
            dep[f"xbc{par}"] = xbc_deps
            dep[f"dtt{par}"] = dtt_deps
            dep["cumbufs"] = [(vv, vv.v), (ss, ss.v)]
            dep["cumtokbufs"] = [(vv, vv.v), (ss, ss.v)]

            if b == 0 and cc == 0 and ctx.get("b2dbg"):
                d2 = ctx["b2dbg"]
                dsem = ctx["gdma"][10]
                marks = [(vv, vv.v), (ss, ss.v), (tmm, tmm.v), (gg, gg.v)]

                @block.sync
                def _(s):
                    for sm, mv in marks:
                        s.wait_ge(sm.h, mv)
                    dsem.inc(s.dma_start(out=d2["Hst"][:], in_=Hst[:]), 16)
                    dsem.inc(s.dma_start(out=d2["y_fm"][:], in_=y_fm[:]), 16)
                    dsem.inc(s.dma_start(out=d2["xT"][:], in_=xT[:]), 16)
                    dsem.inc(s.dma_start(out=d2["A_bc"][:], in_=A_bc[:]), 16)
                    dsem.inc(s.dma_start(out=d2["cum_tok"][:], in_=cum_tok[:]), 16)
                dep["b2dump"] = (dsem, dsem.v)
                for k in ("H", "trfuse0", "trfuse1", "prodt", "convB", "ysbuse0",
                          "ysbuse1", "scmuse0", "scmuse1", "Csuse0", "Csuse1",
                          "Euse0", "Euse1", "yps0", "yps1", "Ssbuse", "Sevac0",
                          "Sevac1", "Hread", "cumtokbufs", "cumbufs"):
                    old = dep.get(k)
                    add = (dsem, dsem.v)
                    if old is None:
                        dep[k] = add
                    elif isinstance(old, list):
                        dep[k] = old + [add]
                    else:
                        dep[k] = [old, add]

    es.close()
    pes.close()
    _stage_out(ctx, tmm, vv, ss, gg, *ldz, *ldx, *ldd, *stp, ctx["gdma"][10], ctx["gdma"][11])


def emit_b3(ctx, prodbuf, woutT, vbuf, xbuf1, resbuf, h2fmbuf):
    nc, block, sem = ctx["nc"], ctx["block"], ctx["sem"]
    identb = ctx["identb"]
    ntile = T // 128

    lwo = ctx["gswd"][0]
    lps = ctx["gdma"][0:2]
    lvs = ctx["gdma"][2:4]
    lx1 = ctx["gdma"][4:6]
    mmo = ctx["geng"]["g"]
    tmm, vv, ss = ctx["geng"]["t"], ctx["geng"]["v"], ctx["geng"]["s"]
    sts = ctx["gdma"][6:8]
    sth = ctx["gdma"][8:10]

    es = contextlib.ExitStack()
    sb = lambda n, sh, dt: es.enter_context(nc.sbuf_tensor(f"b3_{n}", sh, dt))
    wout_sb = sb("wout", [128, GB, D], BF16)
    prodt = [sb(f"prod{i}", [128, GB, 128], BF16) for i in range(2)]
    vt = [sb(f"vt{i}", [128, D], F32) for i in range(2)]
    x1t = [sb(f"x1t{i}", [128, D], F32) for i in range(2)]
    a_t = [sb(f"at{i}", [128, D], F32) for i in range(2)]
    s_t = [sb(f"st{i}", [128, D], F32) for i in range(2)]
    h2tm = [sb(f"h2tm{i}", [128, D], BF16) for i in range(2)]
    h2f = [sb(f"h2f{i}", [128, GB, 128], BF16) for i in range(2)]
    colb = [sb(f"col{i}", [128, 8], F32) for i in range(2)]
    scr_f = sb("scr_f", [128, D], F32)

    pes = contextlib.ExitStack()
    ps_o = [pes.enter_context(nc.psum_tensor(f"b3_o{i}", [128, 512], F32))
            for i in range(4)]
    ps_tr_t = [pes.enter_context(nc.psum_tensor(f"b3_tr{i}", [128, 128], BF16))
               for i in range(2)]
    ps_tr = [ps_tr_t[0][:], ps_tr_t[1][:]]

    dep = {}
    W, WW = mk_W(ctx, dep)

    @block.gpsimd
    def _(g):
        for k in range(GB):
            lwo.inc(g.dma_start(out=wout_sb[:, k, :],
                                in_=woutT[128 * k:128 * k + 128, :]), 16)

    for gt in range(ntile):
        par = gt % 2

        @block.sync
        def _(s):
            W(s, f"prod{par}")
            for gq in range(2):
                lps[par].inc(s.dma_start(
                    out=prodt[par][:, 8 * gq:8 * gq + 8, :],
                    in_=prodbuf.rearrange("(g p) t -> p g t", p=128)
                        [:, 8 * gq:8 * gq + 8, 128 * gt:128 * gt + 128]), 16)
            W(s, f"vt{par}")
            lvs[par].inc(s.dma_start(out=vt[par][:],
                                     in_=vbuf[128 * gt:128 * gt + 128, :]), 16)
            W(s, f"x1t{par}")
            lx1[par].inc(s.dma_start(out=x1t[par][:],
                                     in_=xbuf1[128 * gt:128 * gt + 128, :]), 16)
            dep[f"pld{par}"] = (lps[par], lps[par].v)
            dep[f"vld{par}"] = (lvs[par], lvs[par].v)
            dep[f"xld{par}"] = (lx1[par], lx1[par].v)

        @block.tensor
        def _(t_):
            if gt == 0:
                t_.wait_ge(lwo.h, lwo.v)
            W(t_, f"pld{par}")
            W(t_, "ps_o")
            for k in range(GB):
                for n4 in range(4):
                    mm = t_.matmul(ps_o[n4][:], prodt[par][:, k, :],
                                   wout_sb[:, k, 512 * n4:512 * n4 + 512],
                                   start=(k == 0), stop=(k == GB - 1))
            mmo.inc(mm, 1)
        mm_done = (mmo, mmo.v)
        dep[f"prod{par}"] = mm_done

        @block.vector
        def _(v):
            WW(v, mm_done[0], mm_done[1])
            W(v, f"vld{par}")
            W(v, f"at{par}")
            for n4 in range(4):
                v.tensor_tensor(a_t[par][:, 512 * n4:512 * n4 + 512], ps_o[n4][:],
                                vt[par][:, 512 * n4:512 * n4 + 512], ALU.add)
            vv.inc(v.tensor_copy(scr_f[:, 0:2], scr_f[:, 0:2]), 1)
            dep["ps_o"] = (vv, vv.v)
            dep[f"vt{par}"] = (vv, vv.v)
            W(v, f"xld{par}")
            W(v, f"st{par}")
            vv.inc(v.tensor_tensor(s_t[par][:], a_t[par][:], x1t[par][:], ALU.add), 1)
            v.wait_ge(vv.h, vv.v)
            dep[f"x1t{par}"] = (vv, vv.v)
            cb = colb[par]
            vv.inc(v.scalar_tensor_tensor(scr_f[:], a_t[par][:], 1.0, a_t[par][:],
                                          ALU.mult, ALU.mult, accum_out=cb[:, 0:1]), 1)
            v.wait_ge(vv.h, vv.v)
            vv.inc(v.tensor_copy(cb[:, 1:2], cb[:, 0:1]), 1)
        stats_done = (vv, vv.v)
        s_ready = (vv, vv.v)

        @block.scalar
        def _(s):
            WW(s, stats_done[0], stats_done[1])
            ss.inc(s.activation(colb[par][:, 2:3], colb[par][:, 0:1], AF.Sqrt,
                                scale=1.0 / D, bias=ctx["ccol"][:, 2:3]), 1)
            s.wait_ge(ss.h, ss.v)
            ss.inc(s.copy(ctx["ccol"][:, 6:7], colb[par][:, 2:3]), 1)
        sq_done = (ss, ss.v)

        @block.vector
        def _(v):
            WW(v, sq_done[0], sq_done[1])
            vv.inc(v.reciprocal(colb[par][:, 3:4], colb[par][:, 2:3]), 1)
            v.wait_ge(vv.h, vv.v)
            W(v, f"h2tm{par}")
            vv.inc(v.tensor_scalar(scr_f[:], a_t[par][:], colb[par][:, 3:4], None,
                                   ALU.mult), 1)
            vv.inc(v.tensor_tensor(h2tm[par][:], scr_f[:], ctx["n2w"][:], ALU.mult), 1)
            dep[f"at{par}"] = (vv, vv.v)
        h2_done = (vv, vv.v)

        @block.tensor
        def _(t_):
            WW(t_, h2_done[0], h2_done[1])
            for g in range(GB):
                W(t_, f"tr{g % 2}")
                tmm.inc(t_.transpose(ps_tr[g % 2], h2tm[par][:, 128 * g:128 * g + 128],
                                     identb[:]), 1)
                dep[f"trmm{g % 2}"] = (tmm, tmm.v)

                if g % 2 == 0:
                    @block.vector
                    def _(v, g=g):
                        v.wait_ge(dep[f"trmm{g % 2}"][0].h, dep[f"trmm{g % 2}"][1])
                        W(v, f"h2f{par}")
                        vv.inc(v.tensor_copy(h2f[par][:, g, :], ps_tr[g % 2]), 1)
                        dep[f"tr{g % 2}"] = (vv, vv.v)
                else:
                    @block.scalar
                    def _(s, g=g):
                        s.wait_ge(dep[f"trmm{g % 2}"][0].h, dep[f"trmm{g % 2}"][1])
                        W(s, f"h2f{par}")
                        ss.inc(s.copy(h2f[par][:, g, :], ps_tr[g % 2]), 1)
                        dep[f"tr{g % 2}"] = (ss, ss.v)
        h2f_done = [(vv, vv.v), (ss, ss.v)]

        @block.sync
        def _(s):
            for d in h2f_done:
                WW(s, d[0], d[1])
            W(s, f"sth{par}")
            sth[par].inc(s.dma_start(
                out=h2fmbuf[:, :, 128 * gt:128 * gt + 128].rearrange("g p c -> p g c"),
                in_=h2f[par][:]), 16)
            WW(s, s_ready[0], s_ready[1])
            W(s, f"sts{par}")
            sts[par].inc(s.dma_start(out=resbuf[128 * gt:128 * gt + 128, :],
                                     in_=s_t[par][:]), 16)
        dep[f"h2f{par}"] = (sth[par], sth[par].v)
        dep[f"sth{par}"] = (sth[par], sth[par].v)
        dep[f"st{par}"] = (sts[par], sts[par].v)
        dep[f"sts{par}"] = (sts[par], sts[par].v)
        dep[f"h2tm{par}"] = (tmm, tmm.v)

    es.close()
    pes.close()
    _stage_out(ctx, mmo, tmm, vv, ss, lwo, *lps, *lvs, *lx1, *sts, *sth)


def emit_b4(ctx, h2fmbuf, wg_scr, wu_scr, wd_scr, resbuf, xbuf2):
    nc, block, sem = ctx["nc"], ctx["block"], ctx["sem"]
    NT4 = PANEL // 128
    NP4 = T // PANEL
    NS = FF // 128

    lh2 = ctx["gdma"][0:2]
    lgw = ctx["gdma"][2:4]
    ldw = ctx["gdma"][4:6]
    lrs = ctx["gdma"][6:8]
    mmg = ctx["geng"]["t"]
    mmd = ctx["geng"]["g"]
    vv, ss = ctx["geng"]["v"], ctx["geng"]["s"]
    stx = ctx["gdma"][8:10]

    es = contextlib.ExitStack()
    sb = lambda n, sh, dt: es.enter_context(nc.sbuf_tensor(f"b4_{n}", sh, dt))
    h2p = [sb(f"h2p{i}", [128, GB, PANEL], BF16) for i in range(2)]
    wg_t = [sb(f"wg{i}", [128, GB * 128], BF16) for i in range(2)]
    wu_t = [sb(f"wu{i}", [128, GB * 128], BF16) for i in range(2)]
    hfm = sb("hfm", [128, NS, PANEL], BF16)
    sil = [sb(f"sil{i}", [128, PANEL], BF16) for i in range(2)]
    wd_t = [sb(f"wd{i}", [128, 512], BF16) for i in range(2)]
    yd = [sb(f"yd{i}", [128, D], F32) for i in range(NT4)]
    rs_t = [sb(f"rs{i}", [128, D], F32) for i in range(2)]
    xo = [sb(f"xo{i}", [128, D], F32) for i in range(2)]

    pes = contextlib.ExitStack()
    ps = [pes.enter_context(nc.psum_tensor(f"b4_ps{i}", [128, PANEL], F32))
          for i in range(4)]

    dep = {}
    W, WW = mk_W(ctx, dep)

    for p in range(NP4):
        par = p % 2
        tb = p * NT4

        @block.sync
        def _(s):
            W(s, f"h2p{par}")
            for gq in range(2):
                lh2[par].inc(s.dma_start(
                    out=h2p[par][:, 8 * gq:8 * gq + 8, :],
                    in_=h2fmbuf.rearrange("g p t -> p g t")
                        [:, 8 * gq:8 * gq + 8, PANEL * p:PANEL * p + PANEL]), 16)
            dep[f"h2ld{par}"] = (lh2[par], lh2[par].v)

        for s_ in range(NS):
            it = p * NS + s_
            spar = it % 2

            @block.sync
            def _(s):
                W(s, f"wg{spar}")
                lgw[spar].inc(s.dma_start(out=wg_t[spar][:], in_=wg_scr[s_]), 16)
                lgw[spar].inc(s.dma_start(out=wu_t[spar][:], in_=wu_scr[s_]), 16)
                dep[f"gld{spar}"] = (lgw[spar], lgw[spar].v)

            @block.tensor
            def _(t_):
                W(t_, f"gld{spar}")
                W(t_, f"h2ld{par}")
                W(t_, "psd")
                W(t_, f"psg{s_ % 2}")
                for g in range(GB):
                    mm = t_.matmul(ps[s_ % 2][:], wg_t[spar][:, 128 * g:128 * g + 128],
                                   h2p[par][:, g, :], start=(g == 0), stop=(g == GB - 1))
                mmg.inc(mm, 1)
                W(t_, f"psu{s_ % 2}")
                for g in range(GB):
                    mm = t_.matmul(ps[2 + s_ % 2][:],
                                   wu_t[spar][:, 128 * g:128 * g + 128],
                                   h2p[par][:, g, :], start=(g == 0), stop=(g == GB - 1))
                mmg.inc(mm, 1)
            gu_done = (mmg, mmg.v)
            dep[f"wg{spar}"] = gu_done

            @block.scalar
            def _(s):
                s.wait_ge(gu_done[0].h, gu_done[1] - 1)
                W(s, f"sil{s_ % 2}")
                ss.inc(s.activation(sil[s_ % 2][:], ps[s_ % 2][:], AF.Silu), 1)
                dep[f"psg{s_ % 2}"] = (ss, ss.v)
            sil_done = (ss, ss.v)

            @block.vector
            def _(v):
                WW(v, sil_done[0], sil_done[1])
                WW(v, gu_done[0], gu_done[1])
                W(v, "hfm_rd")
                vv.inc(v.tensor_tensor(hfm[:, s_, :], sil[s_ % 2][:],
                                       ps[2 + s_ % 2][:], ALU.mult), 1)
                dep[f"psu{s_ % 2}"] = (vv, vv.v)
                dep[f"sil{s_ % 2}"] = (vv, vv.v)
        h_ready = (vv, vv.v)
        dep[f"h2p{par}"] = (mmg, mmg.v)

        for of4 in range(4):
            for s_ in range(NS):
                it2 = (p * 4 + of4) * NS + s_
                dpar = it2 % 2

                @block.sync
                def _(s):
                    W(s, f"wd{dpar}")
                    ldw[dpar].inc(s.dma_start(
                        out=wd_t[dpar][:],
                        in_=wd_scr[128 * s_:128 * s_ + 128,
                                   512 * of4:512 * of4 + 512]), 16)
                    dep[f"dwld{dpar}"] = (ldw[dpar], ldw[dpar].v)

                @block.tensor
                def _(t_):
                    W(t_, f"dwld{dpar}")
                    if s_ == 0:
                        WW(t_, h_ready[0], h_ready[1])
                        W(t_, "psd")
                    for t in range(NT4):
                        mm = t_.matmul(ps[t][:, 0:512], hfm[:, s_, 128 * t:128 * t + 128],
                                       wd_t[dpar][:], start=(s_ == 0), stop=(s_ == NS - 1))
                    mmd.inc(mm, 1)
                dep[f"wd{dpar}"] = (mmd, mmd.v)
            dmm_done = (mmd, mmd.v)

            @block.vector
            def _(v):
                WW(v, dmm_done[0], dmm_done[1])
                W(v, "yd")
                for t in range(NT4):
                    vv.inc(v.tensor_copy(yd[t][:, 512 * of4:512 * of4 + 512],
                                         ps[t][:, 0:512]), 1)
                v.wait_ge(vv.h, vv.v)
            dep["psd"] = (vv, vv.v)
        yd_ready = (vv, vv.v)
        dep["hfm_rd"] = (mmd, mmd.v)

        for t in range(NT4):
            gt = tb + t
            tpar = gt % 2

            @block.sync
            def _(s):
                W(s, f"rs{tpar}")
                lrs[tpar].inc(s.dma_start(out=rs_t[tpar][:],
                                          in_=resbuf[128 * gt:128 * gt + 128, :]), 16)
                dep[f"rld{tpar}"] = (lrs[tpar], lrs[tpar].v)

            @block.vector
            def _(v):
                WW(v, yd_ready[0], yd_ready[1])
                W(v, f"rld{tpar}")
                W(v, f"xo{tpar}")
                vv.inc(v.tensor_tensor(xo[tpar][:], yd[t][:], rs_t[tpar][:],
                                       ALU.add), 1)
                v.wait_ge(vv.h, vv.v)
                dep[f"rs{tpar}"] = (vv, vv.v)
                if t == NT4 - 1:
                    dep["yd"] = (vv, vv.v)
            xo_done = (vv, vv.v)

            @block.sync
            def _(s):
                WW(s, xo_done[0], xo_done[1])
                W(s, f"stx{tpar}")
                stx[tpar].inc(s.dma_start(out=xbuf2[128 * gt:128 * gt + 128, :],
                                          in_=xo[tpar][:]), 16)
            dep[f"xo{tpar}"] = (stx[tpar], stx[tpar].v)
            dep[f"stx{tpar}"] = (stx[tpar], stx[tpar].v)

    es.close()
    pes.close()
    _stage_out(ctx, mmg, mmd, vv, ss, *lh2, *lgw, *ldw, *lrs, *stx)


def prep_inputs(inp):
    x = np.asarray(inp["L0_attn_hidden_states"], np.float32)
    pi, pif = PI_D, PI_FF
    xp = np.ascontiguousarray(x[..., pi])

    win = np.asarray(inp["L0_attn_in_proj_w"], np.float32)
    winp = win[:, pi]
    winp_pad = np.zeros((65 * 128, D), np.float32)
    winp_pad[:DIP] = winp
    win2 = np.ascontiguousarray(
        winp_pad.reshape(65, 128, GB, 128).transpose(0, 3, 2, 1).reshape(65, 128, GB * 128))

    wout = np.asarray(inp["L0_attn_out_proj_w"], np.float32)
    woutT = np.ascontiguousarray(wout[pi, :].T)

    def tile_lhsT(w):
        wp = w[:, pi]
        return np.ascontiguousarray(
            wp.reshape(64, 128, GB, 128).transpose(0, 3, 2, 1).reshape(64, 128, GB * 128))

    wgate = tile_lhsT(np.asarray(inp["L0_attn_gate_w"], np.float32))
    wup = tile_lhsT(np.asarray(inp["L0_attn_up_w"], np.float32))
    wdownT = np.ascontiguousarray(np.asarray(inp["L0_attn_down_w"], np.float32)[pi, :].T)

    gup = np.asarray(inp["L0_ffn_gate_up_packed"], np.int32)
    gus = np.asarray(inp["L0_ffn_gate_up_scales"], np.float16)
    perm_rows = np.concatenate([pif, FFQ + pif])
    gup_p = gup[perm_rows]
    gus_p = np.ascontiguousarray(gus[:, perm_rows])
    gu_u8t = np.ascontiguousarray(gup_p.view(np.uint8).reshape(2 * FFQ, D // 2).T)

    dnp = np.asarray(inp["L0_ffn_down_packed"], np.int32)
    dns = np.asarray(inp["L0_ffn_down_scales"], np.float16)
    dnp_p = dnp[pi]
    dns_p = np.ascontiguousarray(dns[:, pi])
    dn_u8t = np.ascontiguousarray(dnp_p.view(np.uint8).reshape(D, FFQ // 2).T)

    com = dict(
        win2=win2, woutT=woutT, wgate=wgate, wup=wup, wdownT=wdownT,
        gu_u8t=gu_u8t, gu_sc=gus_p, dn_u8t=dn_u8t, dn_sc=dns_p,
        zbias=np.asarray(inp["L0_attn_z_bias"], np.float32),
        convw=np.asarray(inp["L0_attn_conv_w"], np.float32),
        convb=np.asarray(inp["L0_attn_conv_b"], np.float32),
        dvec=np.asarray(inp["L0_attn_D"], np.float32),
        n1w=np.asarray(inp["L0_attn_norm1_w"], np.float32)[pi],
        n2w=np.asarray(inp["L0_attn_norm2_w"], np.float32)[pi],
        identf=np.eye(128, dtype=np.float32),
        onesf=np.ones((128, 128), np.float32),
        trilu=np.triu(np.ones((128, 128), np.float32)),
    )
    in_maps = []
    for c in range(NCORES):
        m = dict(com)
        m["xin"] = np.ascontiguousarray(
            xp[c * SEQ_PER_CORE:(c + 1) * SEQ_PER_CORE].reshape(T, D))
        in_maps.append(m)
    return in_maps


def kernel(**inputs):
    in_maps = prep_inputs(inputs)
    nc = build_program()
    res = run_bass_kernel_spmd(nc, in_maps, core_ids=list(range(NCORES)))
    outs = [r["out"] for r in res.results]
    y = np.concatenate([o.reshape(SEQ_PER_CORE, L, D) for o in outs], axis=0)
    return y


# version 05
# speedup vs baseline: 1.1830x; 1.0370x over previous
import contextlib
import numpy as np
import ml_dtypes
import concourse.bass as bass
import concourse.mybir as mybir
from concourse.bass_utils import run_bass_kernel_spmd

F32 = mybir.dt.float32
BF16 = mybir.dt.bfloat16
F16 = mybir.dt.float16
I32 = mybir.dt.int32
U32 = mybir.dt.uint32
U8 = mybir.dt.uint8
AF = mybir.ActivationFunctionType
ALU = mybir.AluOpType

D = 2048
L = 2048
BATCH = 16
NCORES = 8
SEQ_PER_CORE = BATCH // NCORES
T = SEQ_PER_CORE * L
DIN = 2048
NH = 32
DS = 64
HD = 64
CONVD = 6144
DIP = 8224
FFQ = 5632
FF = 8192
CHUNK = 128
GB = D // 128
EPS_LN = 1e-5
EPS_RMS = 1e-5
EPS_Q = 1e-6

C_EXPBITS = float(126 << 23)
G_SCALE = 2.0 / 3.0 / float(1 << 22)

PANEL = 512
NPANEL = T // PANEL
NT = PANEL // 128

ALL_ENG = ("tensor", "vector", "scalar", "sync", "gpsimd")


def _pi_perm(n):
    idx = np.empty(n, np.int64)
    for r in range(n // 256):
        b = 256 * r
        idx[b:b + 128] = b + 2 * np.arange(128)
        idx[b + 128:b + 256] = b + 2 * np.arange(128) + 1
    return idx


PI_D = _pi_perm(D)
PI_FF = _pi_perm(FFQ)


class Sem:
    b = 0

    def __init__(self, stack, nc, name):
        self.h = stack.enter_context(nc.semaphore(name))
        self.v = 0

    def inc(self, instr, n=1):
        instr.then_inc(self.h, n)
        self.v += n
        return instr

    def wait(self, eng, v=None):
        eng.wait_ge(self.h, self.v if v is None else v)


def _barrier(ctx):
    block = ctx["block"]
    deps = ctx.get("deps", [])
    for engname in ALL_ENG:
        def _f(e, _deps=list(deps)):
            for s, v in _deps:
                e.wait_ge(s.h, v)
        getattr(block, engname)(_f)
    ctx["deps"] = []


def _stage_out(ctx, *sems):
    ctx.setdefault("deps", [])
    for s in sems:
        ctx["deps"].append((s, s.v))


def build_program(debug_taps=(), stages=("P0", "A", "B01", "B2", "B3", "B4", "C"),
                  detect_races=True, input_bufs=()):
    nc = bass.Bass("TRN2", target_bir_lowering=False, detect_race_conditions=detect_races)
    ap = lambda t: t.ap()

    xin = ap(nc.dram_tensor("xin", [T, D], F32, kind="ExternalInput"))
    win2 = ap(nc.dram_tensor("win2", [65, 128, GB * 128], F32, kind="ExternalInput"))
    woutT = ap(nc.dram_tensor("woutT", [DIN, D], F32, kind="ExternalInput"))
    wgate = ap(nc.dram_tensor("wgate", [64, 128, GB * 128], F32, kind="ExternalInput"))
    wup = ap(nc.dram_tensor("wup", [64, 128, GB * 128], F32, kind="ExternalInput"))
    wdownT = ap(nc.dram_tensor("wdownT", [FF, D], F32, kind="ExternalInput"))
    gu_u8t = ap(nc.dram_tensor("gu_u8t", [D // 2, 2 * FFQ], U8, kind="ExternalInput"))
    gu_sc = ap(nc.dram_tensor("gu_sc", [D // 128, 2 * FFQ], F16, kind="ExternalInput"))
    dn_u8t = ap(nc.dram_tensor("dn_u8t", [FFQ // 2, D], U8, kind="ExternalInput"))
    dn_sc = ap(nc.dram_tensor("dn_sc", [FFQ // 128, D], F16, kind="ExternalInput"))
    zbias = ap(nc.dram_tensor("zbias", [DIN], F32, kind="ExternalInput"))
    convw = ap(nc.dram_tensor("convw", [CONVD, 4], F32, kind="ExternalInput"))
    convb = ap(nc.dram_tensor("convb", [CONVD], F32, kind="ExternalInput"))
    dvec = ap(nc.dram_tensor("dvec", [NH], F32, kind="ExternalInput"))
    n1w = ap(nc.dram_tensor("n1w", [D], F32, kind="ExternalInput"))
    n2w = ap(nc.dram_tensor("n2w", [D], F32, kind="ExternalInput"))
    identf = ap(nc.dram_tensor("identf", [128, 128], F32, kind="ExternalInput"))
    trilu = ap(nc.dram_tensor("trilu", [128, 128], F32, kind="ExternalInput"))
    onesf = ap(nc.dram_tensor("onesf", [128, 128], F32, kind="ExternalInput"))

    out = ap(nc.dram_tensor("out", [T, D], F32, kind="ExternalOutput"))

    kind = lambda name: ("ExternalInput" if name in input_bufs else
                         "ExternalOutput" if name in debug_taps else "Internal")
    xbuf1 = ap(nc.dram_tensor("xbuf1", [T, D], F32, kind=kind("xbuf1")))
    xbuf2 = ap(nc.dram_tensor("xbuf2", [T, D], F32, kind=kind("xbuf2")))
    vbuf = ap(nc.dram_tensor("vbuf", [T, D], F32, kind=kind("vbuf")))
    resbuf = ap(nc.dram_tensor("resbuf", [T, D], F32, kind=kind("resbuf")))
    zbuf = ap(nc.dram_tensor("zbuf", [DIN, T], F32, kind=kind("zbuf")))
    xbcbuf = ap(nc.dram_tensor("xbcbuf", [CONVD, T], BF16, kind=kind("xbcbuf")))
    dtbuf = ap(nc.dram_tensor("dtbuf", [32, T], F32, kind=kind("dtbuf")))
    prodbuf = ap(nc.dram_tensor("prodbuf", [DIN, T], BF16, kind=kind("prodbuf")))
    cumdram = ap(nc.dram_tensor("cumdram", [2, 32, 128], F32, kind="Internal"))
    h2fmbuf = ap(nc.dram_tensor("h2fmbuf", [GB, 128, T], BF16, kind=kind("h2fmbuf")))
    wgu_scr = ap(nc.dram_tensor("wgu_scr", [88, 128, GB * 128], BF16, kind=kind("wgu_scr")))
    wdn_scr = ap(nc.dram_tensor("wdn_scr", [FFQ, D], BF16, kind=kind("wdn_scr")))
    wg_scr = ap(nc.dram_tensor("wg_scr", [64, 128, GB * 128], BF16, kind=kind("wg_scr")))
    wu_scr = ap(nc.dram_tensor("wu_scr", [64, 128, GB * 128], BF16, kind=kind("wu_scr")))
    wd_scr = ap(nc.dram_tensor("wd_scr", [FF, D], BF16, kind=kind("wd_scr")))
    dbg = None
    if "b2dbg" in debug_taps:
        ctx_b2dbg = dict(
            Hst=ap(nc.dram_tensor("b2_dbg_Hst", [128, 1024], F32, kind="ExternalOutput")),
            y_fm=ap(nc.dram_tensor("b2_dbg_yfm", [128, GB, 256], BF16, kind="ExternalOutput")),
            xT=ap(nc.dram_tensor("b2_dbg_xT", [128, GB * 128], BF16, kind="ExternalOutput")),
            A_bc=ap(nc.dram_tensor("b2_dbg_Abc", [128, 32], F32, kind="ExternalOutput")),
            cum_tok=ap(nc.dram_tensor("b2_dbg_cum", [128, 32], F32, kind="ExternalOutput")),
            prodt=ap(nc.dram_tensor("b2_dbg_prodt", [128, GB, 256], BF16, kind="ExternalOutput")),
            Hst0=ap(nc.dram_tensor("b2_dbg_Hst0", [128, 1024], F32, kind="ExternalOutput")),
            Hmid=ap(nc.dram_tensor("b2_dbg_Hmid", [128, 1024], F32, kind="ExternalOutput")),
            seg=ap(nc.dram_tensor("b2_dbg_seg", [128, 128], F32, kind="ExternalOutput")),
            segE=ap(nc.dram_tensor("b2_dbg_segE", [128, 128], F32, kind="ExternalOutput")),
            E_sb=ap(nc.dram_tensor("b2_dbg_Esb", [128, 128], F32, kind="ExternalOutput")),
            sc_bf=ap(nc.dram_tensor("b2_dbg_scbf", [128, 128], BF16, kind="ExternalOutput")),
            sc_m=ap(nc.dram_tensor("b2_dbg_scm", [128, 128], BF16, kind="ExternalOutput")),
            y_sb=ap(nc.dram_tensor("b2_dbg_ysb", [128, 128], F32, kind="ExternalOutput")),
            Cs=ap(nc.dram_tensor("b2_dbg_Cs", [128, 128], F32, kind="ExternalOutput")),
            T1tmp=ap(nc.dram_tensor("b2_dbg_T1", [128, 128], F32, kind="ExternalOutput")),
            cum=ap(nc.dram_tensor("b2_dbg_cumfm", [32, 128], F32, kind="ExternalOutput")),
            cum_tok0=ap(nc.dram_tensor("b2_dbg_cumtok0", [128, 32], F32, kind="ExternalOutput")),
            ncum_tok0=ap(nc.dram_tensor("b2_dbg_ncumtok0", [128, 32], F32, kind="ExternalOutput")),
            sp_t0=ap(nc.dram_tensor("b2_dbg_spt0", [32, 128], F32, kind="ExternalOutput")),
            dtt0=ap(nc.dram_tensor("b2_dbg_dtt0", [32, 256], F32, kind="ExternalOutput")),
            scview=ap(nc.dram_tensor("b2_dbg_scview", [32, 512], F32, kind="ExternalOutput")),
        )
    else:
        ctx_b2dbg = None
    if "b01dbg" in debug_taps:
        ctx_b01dbg = ap(nc.dram_tensor("b01dbg", [128, 36], F32, kind="ExternalOutput"))
    else:
        ctx_b01dbg = None
    if "dbg_u1fm" in debug_taps:
        dbg = dict(
            u1fm=ap(nc.dram_tensor("dbg_u1fm", [128, GB, PANEL], BF16, kind="ExternalOutput")),
            hfm=ap(nc.dram_tensor("dbg_hfm", [128, 44, PANEL], BF16, kind="ExternalOutput")),
            yq=ap(nc.dram_tensor("dbg_yq", [NT * 128, D], F32, kind="ExternalOutput")),
            cols=ap(nc.dram_tensor("dbg_cols", [128, 64], F32, kind="ExternalOutput")),
            u1tm=ap(nc.dram_tensor("dbg_u1tm", [3 * 128, D], BF16, kind="ExternalOutput")),
            xt=ap(nc.dram_tensor("dbg_xt", [2 * 128, D], F32, kind="ExternalOutput")),
        )

    stack = contextlib.ExitStack()
    identf_sb = stack.enter_context(nc.sbuf_tensor("identf_sb", [128, 128], F32))
    identb_sb = stack.enter_context(nc.sbuf_tensor("identb_sb", [128, 128], BF16))
    onesf_sb = stack.enter_context(nc.sbuf_tensor("onesf_sb", [128, 128], F32))
    n1w_sb = stack.enter_context(nc.sbuf_tensor("n1w_sb", [128, D], BF16))
    n2w_sb = stack.enter_context(nc.sbuf_tensor("n2w_sb", [128, D], BF16))
    dbc_sb = stack.enter_context(nc.sbuf_tensor("dbc_sb", [128, NH], F32))
    zb_sb = stack.enter_context(nc.sbuf_tensor("zb_sb", [128, GB], F32))
    cw_sb = stack.enter_context(nc.sbuf_tensor("cw_sb", [128, 48, 4], F32))
    cb_sb = stack.enter_context(nc.sbuf_tensor("cb_sb", [128, 48], F32))
    ccol_sb = stack.enter_context(nc.sbuf_tensor("ccol_sb", [128, 8], F32))
    trilu_sb = stack.enter_context(nc.sbuf_tensor("trilu_sb", [128, 128], BF16))

    with stack:
        with nc.Block() as block:
            sem = lambda name: Sem(stack, nc, name)
            pre = sem("pre")
            pre2 = sem("pre2")

            @block.sync
            def _(s):
                pre.inc(s.dma_start(out=identf_sb[:], in_=identf[:]), 16)
                pre.inc(s.dma_start(out=onesf_sb[:], in_=onesf[:]), 16)
                pre.inc(s.dma_start(out=dbc_sb[:], in_=dvec[None, :].to_broadcast([128, NH])), 16)
                pre.inc(s.dma_start(
                    out=zb_sb[:],
                    in_=zbias.rearrange("(g p) -> p g", p=128), allow_slow_non_contiguous=True), 16)
                pre.inc(s.dma_start(out=cw_sb[:], in_=convw.rearrange("(f p) k -> p f k", p=128)), 16)
                pre.inc(s.dma_start(
                    out=cb_sb[:],
                    in_=convb.rearrange("(f p) -> p f", p=128), allow_slow_non_contiguous=True), 16)

            @block.gpsimd
            def _(g):
                pre2.inc(g.dma_start(out=identb_sb[:], in_=identf[:]), 16)
                pre2.inc(g.dma_start(out=trilu_sb[:], in_=trilu[:]), 16)
                pre2.inc(g.dma_start(out=n1w_sb[:], in_=n1w[None, :].to_broadcast([128, D])), 16)
                pre2.inc(g.dma_start(out=n2w_sb[:], in_=n2w[None, :].to_broadcast([128, D])), 16)

            @block.vector
            def _(v):
                pre.wait(v)
                pre2.wait(v)
                v.memset(ccol_sb[:, 0:1], EPS_LN)
                v.memset(ccol_sb[:, 1:2], 4.0 * EPS_Q)
                v.memset(ccol_sb[:, 2:3], EPS_RMS)
                v.memset(ccol_sb[:, 4:8], 0.0)
                pre.inc(v.memset(ccol_sb[:, 3:4], 1.0), 1)

            geng = dict(t=Sem(stack, nc, "g_t"), v=Sem(stack, nc, "g_v"),
                        s=Sem(stack, nc, "g_s"), g=Sem(stack, nc, "g_g"))
            gdma = [Sem(stack, nc, f"g_dma{i}") for i in range(12)]
            gswd = [Sem(stack, nc, f"g_swd{i}") for i in range(2)]
            ctx = dict(nc=nc, block=block, sem=sem, stack=stack, ccol=ccol_sb,
                       geng=geng, gdma=gdma, gswd=gswd,
                       b01dbg=ctx_b01dbg, b2dbg=ctx_b2dbg,
                       identf=identf_sb, identb=identb_sb, ones=onesf_sb,
                       trilu=trilu_sb,
                       n1w=n1w_sb, n2w=n2w_sb, dbc=dbc_sb, zb=zb_sb,
                       cw=cw_sb, cb=cb_sb, deps=[(pre, pre.v), (pre2, pre2.v)])

            final_sems = []
            if "P0" in stages:
                _barrier(ctx)
                emit_p0(ctx, gu_u8t, gu_sc, dn_u8t, dn_sc, wgu_scr, wdn_scr,
                        [(wgate, wg_scr, 64, GB * 128), (wup, wu_scr, 64, GB * 128)],
                        [(wdownT, wd_scr, FF)])
            if "A" in stages:
                _barrier(ctx)
                emit_ffnq(ctx, "A", xin, xbuf1, wgu_scr, wdn_scr, out_ap=None, dbg=dbg)
            if "B01" in stages:
                _barrier(ctx)
                emit_b01(ctx, xbuf1, vbuf, win2, zbuf, xbcbuf, dtbuf)
            if "B2" in stages:
                _barrier(ctx)
                emit_b2(ctx, zbuf, xbcbuf, dtbuf, prodbuf, cumdram)
            if "B3" in stages:
                _barrier(ctx)
                emit_b3(ctx, prodbuf, woutT, vbuf, xbuf1, resbuf, h2fmbuf)
            if "B4" in stages:
                _barrier(ctx)
                emit_b4(ctx, h2fmbuf, wg_scr, wu_scr, wd_scr, resbuf, xbuf2)
            if "C" in stages:
                _barrier(ctx)
                emit_ffnq(ctx, "C", xbuf2, None, wgu_scr, wdn_scr, out_ap=out)
            _barrier(ctx)

    return nc


def emit_p0(ctx, gu_u8t, gu_sc, dn_u8t, dn_sc, wgu_scr, wdn_scr, tiled, rowmajor):
    nc, block, sem = ctx["nc"], ctx["block"], ctx["sem"]
    lds = [sem("p0_ld0"), sem("p0_ld1")]
    lscs = [sem("p0_lsc0"), sem("p0_lsc1")]
    cv = sem("p0_cv")
    vp = sem("p0_vp")
    sts = [sem("p0_st0"), sem("p0_st1")]
    ldcs = [sem("p0c_ld0"), sem("p0c_ld1")]
    stcs = [sem("p0c_st0"), sem("p0c_st1")]

    FG = 2816
    FD = 2048
    es = contextlib.ExitStack()
    sb = lambda n, sh, dt: es.enter_context(nc.sbuf_tensor(n, sh, dt))
    su8 = [sb(f"p0u8_{i}", [128, FG], U8) for i in range(2)]
    si32 = [sb(f"p0i32_{i}", [128, FG], U32) for i in range(2)]
    ssc = [sb(f"p0sc_{i}", [128, FG], F32) for i in range(2)]
    snib = sb("p0nib", [128, FG], U32)
    sbem = sb("p0bem", [128, FG], U32)
    sts_ = sb("p0ts", [128, FG], U32)
    sbits = sb("p0bits", [128, FG], U32)
    sg_ = sb("p0g", [128, FG], F32)
    sv1 = sb("p0v1", [128, FG], F32)
    swout = [sb(f"p0w_{i}", [128, 2 * FG], BF16) for i in range(2)]
    cbuf = [sb(f"p0c_{i}", [128, GB * 128], BF16) for i in range(2)]

    jobs = []
    for R in range(D // 256):
        for fc in range(2 * FFQ // FG):
            jobs.append(("gu", R, fc))
    for R in range(FFQ // 256):
        jobs.append(("dn", R, 0))

    def job_src(j):
        knd, R, fc = j
        if knd == "gu":
            sl = slice(fc * FG, (fc + 1) * FG)
            return (gu_u8t[128 * R:128 * R + 128, sl], gu_sc, R, sl, FG)
        return (dn_u8t[128 * R:128 * R + 128, :], dn_sc, R, slice(0, FD), FD)

    cjobs = []
    for (src, dst, S, W) in tiled:
        for s_ in range(S):
            cjobs.append((src[s_], dst[s_], W))
    for (src, dst, ROWS) in rowmajor:
        for s_ in range(ROWS // 128):
            cjobs.append((src[128 * s_:128 * s_ + 128, :], dst[128 * s_:128 * s_ + 128, :], D))

    @block.sync
    def _(s):
        for i, j in enumerate(jobs):
            u8src, _, _, _, F = job_src(j)
            if i >= 2:
                s.wait_ge(cv.h, i - 1)
            lds[i % 2].inc(s.dma_start(out=su8[i % 2][:, :F], in_=u8src), 16)

    @block.gpsimd
    def _(g):
        for i, j in enumerate(jobs):
            _, scsrc, R, sl, F = job_src(j)
            if i >= 2:
                g.wait_ge(vp.h, i - 1)
            lscs[i % 2].inc(g.dma_start(out=ssc[i % 2][0:64, :F],
                                in_=scsrc[2 * R, sl][None, :].to_broadcast([64, F])), 16)
            lscs[i % 2].inc(g.dma_start(out=ssc[i % 2][64:128, :F],
                                in_=scsrc[2 * R + 1, sl][None, :].to_broadcast([64, F])), 16)
        for i, (src, dst, W) in enumerate(cjobs):
            if i >= 2:
                g.wait_ge(stcs[i % 2].h, 16 * (i // 2))
            ldcs[i % 2].inc(g.dma_start(out=cbuf[i % 2][:, :W], in_=src), 16)

    @block.scalar
    def _(s):
        nj = len(jobs)
        for i in range(nj + 1):
            if i < nj:
                j = jobs[i]
                _, _, _, _, F = job_src(j)
                s.wait_ge(lds[i % 2].h, 16 * (i // 2 + 1))
                if i >= 2:
                    s.wait_ge(vp.h, i - 1)
                cv.inc(s.activation(si32[i % 2][:, :F], su8[i % 2][:, :F], AF.Copy), 1)
            if i >= 1:
                j = jobs[i - 1]
                knd, R, fc = j
                _, _, _, sl, F = job_src(j)
                s.wait_ge(vp.h, i)
                w = swout[(i - 1) % 2]
                stj = sts[(i - 1) % 2]
                if knd == "gu":
                    ss = 22 * fc
                    dst0 = wgu_scr[ss:ss + 22, :, 128 * (2 * R):128 * (2 * R) + 128]
                    dst1 = wgu_scr[ss:ss + 22, :, 128 * (2 * R + 1):128 * (2 * R + 1) + 128]
                    stj.inc(s.dma_start(
                        out=dst0.rearrange("s p c -> p s c"),
                        in_=w[:, 0:F].rearrange("p (s c) -> p s c", c=128)), 16)
                    stj.inc(s.dma_start(
                        out=dst1.rearrange("s p c -> p s c"),
                        in_=w[:, F:2 * F].rearrange("p (s c) -> p s c", c=128)), 16)
                else:
                    stj.inc(s.dma_start(out=wdn_scr[256 * R:256 * R + 128, :], in_=w[:, 0:F]), 16)
                    stj.inc(s.dma_start(out=wdn_scr[256 * R + 128:256 * R + 256, :],
                                       in_=w[:, F:2 * F]), 16)
        for i, (src, dst, W) in enumerate(cjobs):
            s.wait_ge(ldcs[i % 2].h, 16 * (i // 2 + 1))
            stcs[i % 2].inc(s.dma_start(out=dst, in_=cbuf[i % 2][:, :W]), 16)

    @block.vector
    def _(v):
        for i, j in enumerate(jobs):
            _, _, _, _, F = job_src(j)
            v.wait_ge(cv.h, i + 1)
            v.wait_ge(lscs[i % 2].h, 32 * (i // 2 + 1))
            if i >= 2:
                v.wait_ge(sts[i % 2].h, 32 * (i // 2))
            w = swout[i % 2]
            c = si32[i % 2]
            ins = None
            for half in (0, 1):
                if half == 0:
                    v.tensor_scalar(snib[:, :F], c[:, :F], 15, None, ALU.bitwise_and)
                else:
                    v.tensor_scalar(snib[:, :F], c[:, :F], 4, 15,
                                    ALU.logical_shift_right, ALU.bitwise_and)
                v.tensor_scalar(sbem[:, :F], snib[:, :F], 7, 22,
                                ALU.bitwise_and, ALU.logical_shift_left)
                v.tensor_scalar(sts_[:, :F], snib[:, :F], 8, 28,
                                ALU.bitwise_and, ALU.logical_shift_left)
                v.scalar_tensor_tensor(sbits[:, :F], sbem[:, :F], C_EXPBITS, sts_[:, :F],
                                       ALU.add, ALU.add)
                v.tensor_scalar(sg_[:, :F], sbem[:, :F], G_SCALE, 1.0, ALU.mult, ALU.min)
                v.tensor_tensor(sv1[:, :F], sbits[:, :F].bitcast(F32), sg_[:, :F], ALU.mult)
                ins = v.tensor_tensor(w[:, half * F:half * F + F], sv1[:, :F],
                                      ssc[i % 2][:, :F], ALU.mult)
            vp.inc(ins, 1)

    es.close()
    _stage_out(ctx, *sts, *stcs, vp, *lds, *lscs, cv, *ldcs)


def emit_ffnq(ctx, tag, xin_ap, xout_ap, wgu_scr, wdn_scr, out_ap, dbg=None):
    nc, block, sem = ctx["nc"], ctx["block"], ctx["sem"]
    identb = ctx["identb"]

    spool = ctx.setdefault("spool", {})
    def psem(name):
        if name not in spool:
            spool[name] = sem(name)
        s = spool[name]
        s.b = s.v
        return s
    lxs = [psem("FQ_lx0"), psem("FQ_lx1")]
    vself = psem("FQ_vself")
    sself = psem("FQ_sself")
    csv = psem("FQ_csv")
    css = psem("FQ_css")
    uap = psem("FQ_uap")
    utr = psem("FQ_utr")
    uev_v = psem("FQ_uevv")
    uev_s = psem("FQ_uevs")
    lws = [psem("FQ_lw0"), psem("FQ_lw1")]
    mmg = psem("FQ_mmg")
    hevs = psem("FQ_hevs")
    hevv = psem("FQ_hevv")
    ldws = [psem("FQ_ldw0"), psem("FQ_ldw1")]
    mmd = psem("FQ_mmd")
    dev = psem("FQ_dev")
    yst = psem("FQ_yst")
    ystv = psem("FQ_ystv")
    fin = psem("FQ_fin")
    stxs = [sem(f"{tag}_stx0"), sem(f"{tag}_stx1")]
    lxg = [sem(f"{tag}_lxg0"), sem(f"{tag}_lxg1")]
    lx2s = [psem("FQ_lx20"), psem("FQ_lx21")]

    es = contextlib.ExitStack()
    sb = lambda n, sh, dt: es.enter_context(nc.sbuf_tensor(f"{tag}_{n}", sh, dt))
    xt = [sb(f"xt{i}", [128, D], BF16) for i in range(2)]
    scr_f = sb("scr_f", [128, D], F32)
    u1tm = [sb(f"u1tm{i}", [128, D], BF16) for i in range(5)]
    colb = [sb(f"col{i}", [128, 8], F32) for i in range(3)]
    bnst = [sb(f"bns{i}", [128, 24], F32) for i in range(2)]
    u1fm = sb("u1fm", [128, GB, PANEL], BF16)
    hfm = sb("hfm", [128, 44, PANEL], BF16)
    wg_t = [sb(f"wg{i}", [128, 2, GB * 128], BF16) for i in range(2)]
    wu_t = [sb(f"wu{i}", [128, 2, GB * 128], BF16) for i in range(2)]
    sil = [sb(f"sil{i}", [128, PANEL], BF16) for i in range(2)]
    yq = [sb(f"yq{i}", [128, D], F32) for i in range(NT)]
    wd_t = [sb(f"wd{i}", [128, 512], BF16) for i in range(2)]
    xt2 = [sb(f"xt2{i}", [128, D], F32) for i in range(2)]
    xo = [sb(f"xo{i}", [128, D], F32) for i in range(2)]
    ycol = [sb(f"ycol{i}", [128, 4], F32) for i in range(2)]

    pes = contextlib.ExitStack()
    ps = [pes.enter_context(nc.psum_tensor(f"{tag}_ps{i}", [128, PANEL], F32))
          for i in range(4)]
    ps_tr_t = [pes.enter_context(nc.psum_tensor(f"{tag}_tr{i}", [128, 128], BF16))
               for i in range(2)]
    ps_tr = [ps_tr_t[0][:], ps_tr_t[1][:]]

    ntile = NPANEL * NT

    def sy_wload(s, p, sp2):
        it = p * 22 + sp2
        if it >= 2:
            s.wait_ge(mmg.h, mmg.b + (4 * (it - 1)))
        for half in (0, 1):
            lws[it % 2].inc(s.dma_start(out=wg_t[it % 2][:, half, :],
                                        in_=wgu_scr[2 * sp2 + half, :, :]), 16)
            lws[it % 2].inc(s.dma_start(out=wu_t[it % 2][:, half, :],
                                        in_=wgu_scr[44 + 2 * sp2 + half, :, :]), 16)

    @block.sync
    def _(s):
        for p in range(NPANEL):
            tb = p * NT
            for sp2 in range(0 if p == 0 else 2, 22):
                sy_wload(s, p, sp2)
            for of4 in range(4):
                for s_ in range(44):
                    it = (p * 4 + of4) * 44 + s_
                    if it >= 2:
                        s.wait_ge(mmd.h, mmd.b + (it - 1))
                    ldws[it % 2].inc(s.dma_start(
                        out=wd_t[it % 2][:],
                        in_=wdn_scr[128 * s_:128 * s_ + 128, 512 * of4:512 * of4 + 512]), 16)
                if of4 == 0 and p + 1 < NPANEL:
                    sy_wload(s, p + 1, 0)
                    sy_wload(s, p + 1, 1)
            for t in range(NT):
                gt = tb + t
                if gt >= 2:
                    s.wait_ge(fin.h, fin.b + (gt - 1))
                lx2s[gt % 2].inc(s.dma_start(out=xt2[gt % 2][:],
                                             in_=xin_ap[128 * gt:128 * gt + 128, :]), 16)

    def s_stats(s, p):
        for t in range(NT):
            gt = p * NT + t
            s.wait_ge(csv.h, csv.b + (gt + 1))
            sself.inc(s.activation(colb[gt % 3][:, 7:8], colb[gt % 3][:, 5:6], AF.Sqrt,
                                   scale=1.0 / D, bias=ctx["ccol"][:, 0:1]), 1)
            s.wait_ge(sself.h, sself.v)
            css.inc(s.copy(ctx["ccol"][:, 6:7], colb[gt % 3][:, 7:8]), 1)

    def s_trevac(s, p):
        for t in range(NT):
            gt = p * NT + t
            for g in range(1, GB, 2):
                s.wait_ge(utr.h, utr.b + (16 * gt + g + 1))
                uev_s.inc(s.copy(u1fm[:, g, 128 * t:128 * t + 128], ps_tr[g % 2]), 1)

    def s_silu(s, p):
        for sp2 in range(22):
            for half in (0, 1):
                pp = p * 44 + 2 * sp2 + half
                s.wait_ge(mmg.h, mmg.b + (2 * pp + 1))
                if pp >= 2:
                    s.wait_ge(hevv.h, hevv.b + (pp - 1))
                hevs.inc(s.activation(sil[pp % 2][:], ps[pp % 2][:], AF.Silu), 1)

    def s_ystat(s, p):
        for t in range(NT):
            gt = p * NT + t
            s.wait_ge(ystv.h, ystv.b + (gt + 1))
            sself.inc(s.activation(ycol[gt % 2][:, 1:2], ycol[gt % 2][:, 0:1], AF.Sqrt,
                                   scale=4.0 / D, bias=ctx["ccol"][:, 1:2]), 1)
            s.wait_ge(sself.h, sself.v)
            yst.inc(s.copy(ctx["ccol"][:, 6:7], ycol[gt % 2][:, 1:2]), 1)

    @block.scalar
    def _(s):
        s_stats(s, 0)
        s_trevac(s, 0)
        for p in range(NPANEL):
            if p + 1 < NPANEL:
                s_stats(s, p + 1)
            s_silu(s, p)
            if p + 1 < NPANEL:
                s_trevac(s, p + 1)
            s_ystat(s, p)

    def v_stats(v, p):
        for t in range(NT):
            gt = p * NT + t
            v.wait_ge(lxg[gt % 2].h, 16 * (gt // 2 + 1))
            if gt >= 3:
                v.wait_ge(uap.h, uap.b + (gt - 2))
            cb = colb[gt % 3]
            def vE(op):
                vself.inc(op, 1)
                v.wait_ge(vself.h, vself.v)
                return op
            vE(v.tensor_scalar(scr_f[:], xt[gt % 2][:], 1.0, 0.0, ALU.mult, ALU.add,
                               accum_out=cb[:, 0:1]))
            vE(v.tensor_scalar(cb[:, 2:3], cb[:, 0:1], -1.0 / D, None, ALU.mult))
            vE(v.scalar_tensor_tensor(scr_f[:], xt[gt % 2][:], cb[:, 2:3], xt[gt % 2][:],
                                      ALU.add, ALU.mult, accum_out=cb[:, 5:6]))
            csv.inc(v.tensor_copy(cb[:, 6:7], cb[:, 5:6]), 1)
            v.wait_ge(css.h, css.b + (gt + 1))
            vE(v.reciprocal(cb[:, 3:4], cb[:, 7:8]))
            if gt >= 5:
                v.wait_ge(utr.h, utr.b + (16 * (gt - 4)))
            uap.inc(v.tensor_scalar(u1tm[gt % 5][:], xt[gt % 2][:], cb[:, 2:3], cb[:, 3:4],
                                    ALU.add, ALU.mult), 1)

    def v_trevac(v, p):
        for t in range(NT):
            gt = p * NT + t
            for g in range(0, GB, 2):
                v.wait_ge(utr.h, utr.b + (16 * gt + g + 1))
                uev_v.inc(v.tensor_copy(u1fm[:, g, 128 * t:128 * t + 128], ps_tr[g % 2]), 1)

    def v_hmul(v, p):
        for sp2 in range(22):
            for half in (0, 1):
                pp = p * 44 + 2 * sp2 + half
                sidx = 2 * sp2 + half
                v.wait_ge(mmg.h, mmg.b + (2 * pp + 2))
                v.wait_ge(hevs.h, hevs.b + (pp + 1))
                hevv.inc(v.tensor_tensor(hfm[:, sidx, :], sil[pp % 2][:], ps[2 + pp % 2][:],
                                         ALU.mult), 1)

    def v_devac(v, p):
        for of4 in range(4):
            for t in range(NT):
                v.wait_ge(mmd.h, mmd.b + ((p * 4 + of4) * 44 + 44))
                dev.inc(v.tensor_copy(yq[t][:, 512 * of4:512 * of4 + 512],
                                      ps[t][:]), 1)

    def v_final(v, p):
        for t in range(NT):
            gt = p * NT + t
            if gt >= 2:
                v.wait_ge(fin.h, fin.b + (gt - 1))
            vself.inc(v.scalar_tensor_tensor(scr_f[:], yq[t][:], 1.0, yq[t][:],
                                             ALU.mult, ALU.mult,
                                             accum_out=ycol[gt % 2][:, 0:1]), 1)
            v.wait_ge(vself.h, vself.v)
            ystv.inc(v.tensor_copy(ycol[gt % 2][:, 3:4], ycol[gt % 2][:, 0:1]), 1)
            v.wait_ge(yst.h, yst.b + (gt + 1))
            vself.inc(v.reciprocal(ycol[gt % 2][:, 2:3], ycol[gt % 2][:, 1:2]), 1)
            v.wait_ge(vself.h, vself.v)
            v.wait_ge(lx2s[gt % 2].h, lx2s[gt % 2].b + (16 * (gt // 2 + 1)))
            if gt >= 2:
                v.wait_ge(stxs[gt % 2].h, 16 * (gt // 2))
            if out_ap is None:
                ins = v.scalar_tensor_tensor(xo[gt % 2][:], yq[t][:], ycol[gt % 2][:, 2:3],
                                             xt2[gt % 2][:], ALU.mult, ALU.add)
            else:
                dst = xo[gt % 2][:].rearrange("p (r c q) -> p r c q", r=8, q=2)
                ysrc = yq[t][:].rearrange("p (r h c) -> p r h c", r=8, h=2)
                x2src = xt2[gt % 2][:].rearrange("p (r h c) -> p r h c", r=8, h=2)
                v.scalar_tensor_tensor(
                    dst[:, :, :, 0], ysrc[:, :, 0, :], ycol[gt % 2][:, 2:3],
                    x2src[:, :, 0, :], ALU.mult, ALU.add)
                ins = v.scalar_tensor_tensor(
                    dst[:, :, :, 1], ysrc[:, :, 1, :], ycol[gt % 2][:, 2:3],
                    x2src[:, :, 1, :], ALU.mult, ALU.add)
            fin.inc(ins, 1)

    @block.vector
    def _(v):
        for i in range(3):
            v.memset(colb[i][:], 0.0)
        for i in range(2):
            v.memset(ycol[i][:], 0.0)
        v_stats(v, 0)
        v_trevac(v, 0)
        for p in range(NPANEL):
            if p + 1 < NPANEL:
                v_stats(v, p + 1)
            v_hmul(v, p)
            v_devac(v, p)
            if p + 1 < NPANEL:
                v_trevac(v, p + 1)
            v_final(v, p)
        fin.inc(v.tensor_copy(scr_f[:, 0:2], scr_f[:, 0:2]), 1)

    def t_tr(t_, p):
        for t in range(NT):
            gt = p * NT + t
            t_.wait_ge(uap.h, uap.b + (gt + 1))
            for g in range(GB):
                prev = 8 * gt + g // 2
                if prev > 0:
                    if g % 2 == 0:
                        t_.wait_ge(uev_v.h, uev_v.b + (prev))
                    else:
                        t_.wait_ge(uev_s.h, uev_s.b + (prev))
                utr.inc(t_.transpose(ps_tr[g % 2], u1tm[gt % 5][:, 128 * g:128 * g + 128],
                                     identb[:]), 1)

    @block.tensor
    def _(t_):
        t_tr(t_, 0)
        for p in range(NPANEL):
            tb = p * NT
            if p > 0:
                t_.wait_ge(dev.h, dev.b + (p * 4 * NT))
            t_.wait_ge(uev_v.h, uev_v.b + (8 * (tb + NT)))
            t_.wait_ge(uev_s.h, uev_s.b + (8 * (tb + NT)))
            for sp2 in range(22):
                it = p * 22 + sp2
                t_.wait_ge(lws[it % 2].h, lws[it % 2].b + (64 * (it // 2 + 1)))
                for half in (0, 1):
                    pp = p * 44 + 2 * sp2 + half
                    if pp >= 2:
                        t_.wait_ge(hevs.h, hevs.b + (pp - 1))
                        t_.wait_ge(hevv.h, hevv.b + (pp - 1))
                    wg = wg_t[it % 2]
                    wu = wu_t[it % 2]
                    for g in range(GB):
                        mm = t_.matmul(ps[pp % 2][:],
                                       wg[:, half, 128 * g:128 * g + 128],
                                       u1fm[:, g, :],
                                       start=(g == 0), stop=(g == GB - 1))
                    mmg.inc(mm, 1)
                    for g in range(GB):
                        mm = t_.matmul(ps[2 + pp % 2][:],
                                       wu[:, half, 128 * g:128 * g + 128],
                                       u1fm[:, g, :],
                                       start=(g == 0), stop=(g == GB - 1))
                    mmg.inc(mm, 1)
            t_.wait_ge(hevv.h, hevv.b + ((p + 1) * 44))
            for of4 in range(4):
                for s_ in range(44):
                    it = (p * 4 + of4) * 44 + s_
                    t_.wait_ge(ldws[it % 2].h, ldws[it % 2].b + (16 * (it // 2 + 1)))
                    if s_ == 0 and (p > 0 or of4 > 0):
                        t_.wait_ge(dev.h, dev.b + ((p * 4 + of4) * NT))
                    for t in range(NT):
                        mm = t_.matmul(ps[t][:],
                                       hfm[:, s_, 128 * t:128 * t + 128],
                                       wd_t[it % 2][:],
                                       start=(s_ == 0), stop=(s_ == 43))
                    mmd.inc(mm, 1)
            if p + 1 < NPANEL:
                t_tr(t_, p + 1)

    def g_xload(g, p):
        for t in range(NT):
            gt = p * NT + t
            if gt >= 2:
                g.wait_ge(uap.h, uap.b + (gt - 1))
            lxg[gt % 2].inc(g.dma_start(out=xt[gt % 2][:],
                                        in_=xin_ap[128 * gt:128 * gt + 128, :]), 16)

    @block.gpsimd
    def _(g):
        dst = out_ap if out_ap is not None else xout_ap
        g_xload(g, 0)
        g_xload(g, 1)
        for gt in range(ntile):
            g.wait_ge(fin.h, fin.b + (gt + 1))
            stxs[gt % 2].inc(g.dma_start(out=dst[128 * gt:128 * gt + 128, :],
                                         in_=xo[gt % 2][:]), 16)
            if gt % NT == NT - 1 and gt // NT + 2 < NPANEL:
                g_xload(g, gt // NT + 2)
        if dbg is not None:
            g.wait_ge(fin.h, fin.b + (ntile))
            stxs[0].inc(g.dma_start(out=dbg["u1fm"][:], in_=u1fm[:]), 16)
            stxs[0].inc(g.dma_start(out=dbg["hfm"][:], in_=hfm[:]), 16)
            for t in range(NT):
                stxs[0].inc(g.dma_start(out=dbg["yq"][128 * t:128 * t + 128, :], in_=yq[t][:]), 16)
            stxs[0].inc(g.dma_start(out=dbg["cols"][:, 0:8], in_=ctx["ccol"][:]), 16)
            for i in range(3):
                stxs[0].inc(g.dma_start(out=dbg["u1tm"][128 * i:128 * i + 128, :], in_=u1tm[i][:]), 16)
            for i in range(2):
                stxs[0].inc(g.dma_start(out=dbg["xt"][128 * i:128 * i + 128, :], in_=xt[i][:]), 16)
            for i in range(3):
                stxs[0].inc(g.dma_start(out=dbg["cols"][:, 8 + 8 * i:16 + 8 * i], in_=colb[i][:]), 16)
            for i in range(2):
                stxs[0].inc(g.dma_start(out=dbg["cols"][:, 32 + 4 * i:36 + 4 * i], in_=ycol[i][:]), 16)

    es.close()
    pes.close()
    _stage_out(ctx, *stxs, fin, *lxs, *lx2s, *lws, *ldws, mmg, mmd, hevs, hevv, dev,
               uap, uev_v, uev_s, utr, csv, css, yst, ystv)


def emit_b01(ctx, xbuf1, vbuf, win2, zbuf, xbcbuf, dtbuf):
    nc, block, sem = ctx["nc"], ctx["block"], ctx["sem"]
    identb = ctx["identb"]
    ntile = globals().get("B01_TILES", T // 128)
    nstrip = globals().get("B01_STRIPS", 65)

    lxs = [sem("b01_lx0"), sem("b01_lx1")]
    vself = sem("b01_vself")
    sself = sem("b01_sself")
    csv = sem("b01_csv")
    css = sem("b01_css")
    uap = sem("b01_uap")
    utr = sem("b01_utr")
    uev_v = sem("b01_uevv")
    uev_s = sem("b01_uevs")
    stv = [sem("b01_stv0"), sem("b01_stv1")]
    lwi = [sem("b01_lwi0"), sem("b01_lwi1")]
    mmi = sem("b01_mmi")
    evv = sem("b01_evv")
    evs = sem("b01_evs")
    stzv = [sem(f"b01_stzv{i}") for i in range(2)]
    stzs = [sem(f"b01_stzs{i}") for i in range(2)]

    es = contextlib.ExitStack()
    sb = lambda n, sh, dt: es.enter_context(nc.sbuf_tensor(f"b01_{n}", sh, dt))
    xt = [sb(f"xt{i}", [128, D], F32) for i in range(2)]
    scr_f = sb("scr_f", [128, D], F32)
    vt = [sb(f"vt{i}", [128, D], F32) for i in range(2)]
    h1tm = [sb(f"h1tm{i}", [128, D], BF16) for i in range(3)]
    colb = [sb(f"col{i}", [128, 12], F32) for i in range(3)]
    h1fm = es.enter_context(nc.sbuf_tensor("b01_h1fm", [128, GB, T], BF16))
    wstrip = [sb(f"wi{i}", [128, GB * 128], BF16) for i in range(2)]
    ebf_v = [sb(f"ebfv{i}", [128, 512], F32) for i in range(2)]
    ebf_s = [sb(f"ebfs{i}", [128, 512], F32) for i in range(2)]
    ebb_v = [t[:].bitcast(BF16)[:, 0:512] for t in ebf_v]
    ebb_s = [t[:].bitcast(BF16)[:, 0:512] for t in ebf_s]

    pes = contextlib.ExitStack()
    ps_tr_t = [pes.enter_context(nc.psum_tensor(f"b01_tr{i}", [128, 128], BF16))
               for i in range(2)]
    ps_tr = [ps_tr_t[0][:], ps_tr_t[1][:]]
    psA = [pes.enter_context(nc.psum_tensor(f"b01_ps{i}", [128, 512], F32))
           for i in range(2)]

    NCH = max(1, ntile * 128 // 512)

    @block.sync
    def _(s):
        for gt in range(ntile + 1):
            if gt < ntile:
                if gt >= 2:
                    s.wait_ge(uap.h, gt - 1)
                lxs[gt % 2].inc(s.dma_start(out=xt[gt % 2][:],
                                            in_=xbuf1[128 * gt:128 * gt + 128, :]), 16)
            if gt >= 1:
                g2 = gt - 1
                s.wait_ge(uap.h, g2 + 2)
                stv[g2 % 2].inc(s.dma_start(out=vbuf[128 * g2:128 * g2 + 128, :],
                                            in_=vt[g2 % 2][:]), 16)
        for s_ in range(nstrip):
            for c in range(NCH):
                idx = s_ * NCH + c
                k = idx // 2
                if c % 2 == 0:
                    s.wait_ge(evv.h, k + 1)
                    ef, eb, stk = ebf_v[k % 2][:], ebb_v[k % 2], stzv[k % 2]
                else:
                    s.wait_ge(evs.h, k + 1)
                    ef, eb, stk = ebf_s[k % 2][:], ebb_s[k % 2], stzs[k % 2]
                if s_ < 16:
                    stk.inc(s.dma_start(
                        out=zbuf[128 * s_:128 * s_ + 128, 512 * c:512 * c + 512],
                        in_=ef), 16)
                elif s_ < 64:
                    r0 = 128 * (s_ - 16)
                    stk.inc(s.dma_start(
                        out=xbcbuf[r0:r0 + 128, 512 * c:512 * c + 512],
                        in_=eb), 16)
                else:
                    stk.inc(s.dma_start(
                        out=dtbuf[:, 512 * c:512 * c + 512],
                        in_=ef[0:32, :] if hasattr(ef, '__getitem__') else ef), 16)

    b01dbg = ctx.get("b01dbg")

    @block.gpsimd
    def _(g):
        for s_ in range(nstrip):
            if s_ >= 2:
                g.wait_ge(mmi.h, NCH * (s_ - 1))
            lwi[s_ % 2].inc(g.dma_start(out=wstrip[s_ % 2][:], in_=win2[s_]), 16)
        if b01dbg is not None:
            g.wait_ge(uap.h, ntile + 1)
            for i in range(3):
                g.dma_start(out=b01dbg[:, 12 * i:12 * i + 12], in_=colb[i][:]).then_inc(
                    lwi[0].h, 16)
            lwi[0].v += 48

    @block.vector
    def _(v):
        for i in range(3):
            v.memset(colb[i][:], 0.0)
        for gt in range(ntile):
            v.wait_ge(lxs[gt % 2].h, 16 * (gt // 2 + 1))
            if gt >= 3:
                v.wait_ge(uap.h, gt - 2)
            cb = colb[gt % 3]
            def vE(op):
                vself.inc(op, 1)
                v.wait_ge(vself.h, vself.v)
                return op
            vE(v.tensor_scalar(scr_f[:], xt[gt % 2][:], 1.0, 0.0, ALU.mult, ALU.add,
                               accum_out=cb[:, 0:1]))
            vE(v.tensor_scalar(cb[:, 2:3], cb[:, 0:1], -1.0 / D, None, ALU.mult))
            vE(v.scalar_tensor_tensor(scr_f[:], xt[gt % 2][:], cb[:, 2:3], xt[gt % 2][:],
                                      ALU.add, ALU.mult, accum_out=cb[:, 5:6]))
            csv.inc(v.tensor_copy(scr_f[:, 0:1], cb[:, 5:6]), 1)
            v.wait_ge(css.h, 2 * gt + 1)
            vE(v.reciprocal(cb[:, 3:4], cb[:, 6:7]))
            vE(v.tensor_tensor(cb[:, 11:12], cb[:, 5:6], cb[:, 3:4], ALU.mult))
            vE(v.tensor_tensor(cb[:, 7:8], cb[:, 11:12], cb[:, 3:4], ALU.mult))
            csv.inc(v.tensor_copy(scr_f[:, 0:1], cb[:, 7:8]), 1)
            v.wait_ge(css.h, 2 * gt + 2)
            vE(v.reciprocal(cb[:, 9:10], cb[:, 8:9]))
            vE(v.tensor_tensor(cb[:, 10:11], cb[:, 3:4], cb[:, 9:10], ALU.mult))
            if gt >= 2:
                v.wait_ge(stv[gt % 2].h, 16 * (gt // 2))
            v.tensor_scalar(vt[gt % 2][:], xt[gt % 2][:], cb[:, 2:3], cb[:, 3:4],
                            ALU.add, ALU.mult)
            if gt >= 3:
                v.wait_ge(utr.h, 16 * (gt - 2))
            v.tensor_scalar(scr_f[:], xt[gt % 2][:], cb[:, 2:3], cb[:, 10:11],
                            ALU.add, ALU.mult)
            uap.inc(v.tensor_tensor(h1tm[gt % 3][:], scr_f[:], ctx["n1w"][:], ALU.mult), 1)
            for g in range(0, GB, 2):
                v.wait_ge(utr.h, 16 * gt + g + 1)
                uev_v.inc(v.tensor_copy(h1fm[:, g, 128 * gt:128 * gt + 128], ps_tr[g % 2]), 1)
        uap.inc(v.tensor_copy(scr_f[:, 0:2], scr_f[:, 0:2]), 1)
        for s_ in range(nstrip):
            for c in range(0, NCH, 2):
                idx = s_ * NCH + c
                k = idx // 2
                v.wait_ge(mmi.h, idx + 1)
                if k >= 2:
                    v.wait_ge(stzv[k % 2].h, 16 * (k // 2))
                dst = ebf_v[k % 2][:] if (s_ < 16 or s_ == 64) else ebb_v[k % 2]
                evv.inc(v.tensor_copy(dst, psA[c % 2][:]), 1)
        evv.inc(v.tensor_copy(scr_f[:, 0:2], scr_f[:, 0:2]), 1)

    @block.scalar
    def _(s):
        for gt in range(ntile):
            s.wait_ge(csv.h, 2 * gt + 1)
            sself.inc(s.activation(colb[gt % 3][:, 6:7], colb[gt % 3][:, 5:6], AF.Sqrt,
                                   scale=1.0 / D, bias=ctx["ccol"][:, 0:1]), 1)
            s.wait_ge(sself.h, sself.v)
            css.inc(s.copy(ctx["ccol"][:, 6:7], colb[gt % 3][:, 6:7]), 1)
            s.wait_ge(csv.h, 2 * gt + 2)
            sself.inc(s.activation(colb[gt % 3][:, 8:9], colb[gt % 3][:, 7:8], AF.Sqrt,
                                   scale=1.0 / D, bias=ctx["ccol"][:, 2:3]), 1)
            s.wait_ge(sself.h, sself.v)
            css.inc(s.copy(ctx["ccol"][:, 6:7], colb[gt % 3][:, 8:9]), 1)
            for g in range(1, GB, 2):
                s.wait_ge(utr.h, 16 * gt + g + 1)
                uev_s.inc(s.copy(h1fm[:, g, 128 * gt:128 * gt + 128], ps_tr[g % 2]), 1)
        for s_ in range(nstrip):
            for c in range(1, NCH, 2):
                idx = s_ * NCH + c
                k = idx // 2
                s.wait_ge(mmi.h, idx + 1)
                if k >= 2:
                    s.wait_ge(stzs[k % 2].h, 16 * (k // 2))
                dst = ebf_s[k % 2][:] if (s_ < 16 or s_ == 64) else ebb_s[k % 2]
                evs.inc(s.copy(dst, psA[c % 2][:]), 1)
        evs.inc(s.copy(ctx["ccol"][:, 4:5], ctx["ccol"][:, 5:6]), 1)

    @block.tensor
    def _(t_):
        for gt in range(ntile):
            t_.wait_ge(uap.h, gt + 1)
            for g in range(GB):
                prev = 8 * gt + g // 2
                if prev > 0:
                    if g % 2 == 0:
                        t_.wait_ge(uev_v.h, prev)
                    else:
                        t_.wait_ge(uev_s.h, prev)
                utr.inc(t_.transpose(ps_tr[g % 2], h1tm[gt % 3][:, 128 * g:128 * g + 128],
                                     identb[:]), 1)
        t_.wait_ge(uev_v.h, 8 * ntile)
        t_.wait_ge(uev_s.h, 8 * ntile)
        for s_ in range(nstrip):
            t_.wait_ge(lwi[s_ % 2].h, 16 * (s_ // 2 + 1))
            for c in range(NCH):
                idx = s_ * NCH + c
                if idx >= 2:
                    if c % 2 == 0:
                        t_.wait_ge(evv.h, idx // 2)
                    else:
                        t_.wait_ge(evs.h, (idx - 1) // 2)
                for g in range(GB):
                    mm = t_.matmul(psA[c % 2][:],
                                   wstrip[s_ % 2][:, 128 * g:128 * g + 128],
                                   h1fm[:, g, 512 * c:512 * c + 512],
                                   start=(g == 0), stop=(g == GB - 1))
                mmi.inc(mm, 1)

    es.close()
    pes.close()
    _stage_out(ctx, *stzv, *stzs, *stv, *lxs, *lwi, mmi, evv, evs, uap, utr,
               uev_v, uev_s, csv, css)
    return None



def mk_W(ctx, dep):
    wm = ctx.setdefault("wmark", {})

    def WW(e, s_, v_):
        if v_ <= 0:
            return
        k = (e.__class__.__name__, id(s_))
        if wm.get(k, -1) >= v_:
            return
        e.wait_ge(s_.h, v_)
        wm[k] = v_

    def W(e, key):
        d = dep.get(key)
        if d:
            if isinstance(d, list):
                for s_, v_ in d:
                    WW(e, s_, v_)
            else:
                WW(e, d[0], d[1])
    return W, WW


def emit_b2(ctx, zbuf, xbcbuf, dtbuf, prodbuf, cumdram):
    nc, block, sem = ctx["nc"], ctx["block"], ctx["sem"]
    identf, identb = ctx["identf"], ctx["identb"]
    ones, cw, cbias, zb, dbc, ccol = (ctx["ones"], ctx["cw"], ctx["cb"], ctx["zb"],
                                      ctx["dbc"], ctx["ccol"])
    trilu = ctx["trilu"]
    CH = 256
    NCH2 = L // CH

    tmm, vv, ss, gg = (ctx["geng"]["t"], ctx["geng"]["v"], ctx["geng"]["s"],
                       ctx["geng"]["g"])
    ldz = [ctx["gswd"][1], Sem(ctx["stack"], ctx["nc"], "b2_zswd")]
    ldx = ctx["gdma"][2:4]
    ldd = ctx["gdma"][4:6]
    stp = ctx["gdma"][6:8]

    es = contextlib.ExitStack()
    sb = lambda n, sh, dt: es.enter_context(nc.sbuf_tensor(f"b2_{n}", sh, dt))
    Hst = sb("Hst", [128, (NH // 2) * 64], F32)
    zt = [sb(f"zt{i}", [128, GB, CH], BF16) for i in range(2)]
    zsil = sb("zsil", [128, GB, CH], BF16)
    xbc = [sb(f"xbc{i}", [128, 48, CH + 4], BF16) for i in range(2)]
    dtt = [sb(f"dtt{i}", [32, CH], F32) for i in range(2)]
    convA = sb("convA", [128, 48, 128], BF16)
    convT = sb("convT", [128, 48, 128], BF16)
    convB = [sb(f"convB{i}", [128, 48, 128], BF16) for i in range(2)]
    xT = sb("xT", [128, GB * 128], BF16)
    BT = sb("BT", [128, GB * 128], BF16)
    Bd_all = sb("Bd_all", [128, NH, 64], BF16)
    cum = sb("cum", [32, 128], F32)
    sp_t = sb("sp", [32, 128], F32)
    spe = sb("spe", [32, 128], F32)
    spl = sb("spl", [32, 128], F32)
    spr = sb("spr", [32, 128], F32)
    cum_tok = sb("cum_tok", [128, 32], F32)
    ncum_tok = sb("ncum_tok", [128, 32], F32)
    dec_fm = sb("dec_fm", [32, 128], F32)
    dec_tok = sb("dec_tok", [128, 32], F32)
    acol = sb("acol", [32, 1], F32)
    abct = sb("abct", [32, 128], F32)
    A_bc = sb("A_bc", [128, 32], F32)
    seg_all = sb("seg_all", [128, 32, 128], BF16)
    sc_m = [sb(f"sc_m{i}", [128, 128], BF16) for i in range(2)]
    R_all = sb("R_all", [128, 32, 128], F32)
    E_all = sb("E_all", [128, 32, 128], BF16)
    Cs = [sb(f"Cs{i}", [128, 128], F32) for i in range(2)]
    y_sb = [sb(f"y_sb{i}", [128, 128], BF16) for i in range(2)]
    y_fm = sb("y_fm", [128, GB, CH], BF16)

    pes = contextlib.ExitStack()
    psal = lambda n, sh, dt=F32: pes.enter_context(nc.psum_tensor(f"b2_{n}", sh, dt))
    ps_cbS = [psal(f"cbS{i}", [128, 192]) for i in range(2)]
    ps_y = [psal(f"y{i}", [128, 128]) for i in range(2)]
    ps_tr = [psal(f"tr{i}", [128, 128], BF16) for i in range(2)]
    ps_trf = [psal("trf0", [128, 128])]

    dep = {}
    W, WW = mk_W(ctx, dep)

    for b in range(SEQ_PER_CORE):
        @block.vector
        def _(v):
            W(v, "H")
            vv.inc(v.memset(Hst[:], 0.0), 1)
        dep["H"] = (vv, vv.v)
        dep["Hread"] = (vv, vv.v)

        if b == 0 and ctx.get("b2dbg"):
            dsem2 = ctx["gdma"][11]
            mk = (vv, vv.v)

            @block.sync
            def _(s):
                WW(s, mk[0], mk[1])
                dsem2.inc(s.dma_start(out=ctx["b2dbg"]["Hst0"][:], in_=Hst[:]), 16)
            dep["Hread"] = [dep["Hread"], (dsem2, dsem2.v)]

        for cc in range(NCH2):
            it = b * NCH2 + cc
            t0 = b * L + cc * CH
            par = it % 2
            nz = it // 2 + 1

            @block.gpsimd
            def _(g):
                W(g, f"zt{par}")
                for gq in range(4):
                    ldz[par].inc(g.dma_start(
                        out=zt[par][:, 4 * gq:4 * gq + 4, :],
                        in_=zbuf.rearrange("(g p) t -> p g t", p=128)
                            [:, 4 * gq:4 * gq + 4, t0:t0 + CH]), 16)

            @block.sync
            def _(s):
                W(s, f"xbc{par}")
                if cc == 0:
                    for gq in range(4):
                        ldx[par].inc(s.dma_start(
                            out=xbc[par][:, 12 * gq:12 * gq + 12, 4:4 + CH],
                            in_=xbcbuf.rearrange("(g p) t -> p g t", p=128)
                                [:, 12 * gq:12 * gq + 12, t0:t0 + CH]), 16)
                else:
                    for gq in range(4):
                        ldx[par].inc(s.dma_start(
                            out=xbc[par][:, 12 * gq:12 * gq + 12, 1:4 + CH],
                            in_=xbcbuf.rearrange("(g p) t -> p g t", p=128)
                                [:, 12 * gq:12 * gq + 12, t0 - 3:t0 + CH]), 16)
                W(s, f"dtt{par}")
                ldd[par].inc(s.dma_start(out=dtt[par][:], in_=dtbuf[:, t0:t0 + CH]), 16)
                dep[f"zld{par}"] = (ldz[par], ldz[par].v)
                dep[f"xld{par}"] = (ldx[par], ldx[par].v)
                dep[f"dld{par}"] = (ldd[par], ldd[par].v)

            if cc == 0:
                @block.vector
                def _(v):
                    W(v, f"xld{par}")
                    vv.inc(v.memset(xbc[par][:, :, 0:4], 0.0), 1)

            xbc_deps = []
            zt_deps = []
            dtt_deps = []

            for sc in range(2):
                j0 = sc * 128
                isc = it * 2 + sc
                par2 = isc % 2

                if b == 0 and cc == 0 and sc == 1 and ctx.get("b2dbg"):
                    dsem2 = ctx["gdma"][11]
                    mk = (vv, vv.v)

                    @block.sync
                    def _(s):
                        WW(s, mk[0], mk[1])
                        dsem2.inc(s.dma_start(out=ctx["b2dbg"]["Hmid"][:], in_=Hst[:]), 16)
                    hr = dep.get("Hread")
                    dep["Hread"] = (hr if isinstance(hr, list) else [hr]) + [(dsem2, dsem2.v)]

                @block.vector
                def _(v):
                    W(v, f"xld{par}")
                    W(v, f"convB{par2}")
                    xsl = lambda k: xbc[par][:, :, j0 + 1 + k:j0 + 129 + k]
                    cwB = lambda k: cw[:, :, k:k + 1].to_broadcast([128, 48, 128])
                    cbB = cbias[:, :, None].to_broadcast([128, 48, 128])
                    v.tensor_tensor(convA[:], xsl(0), cwB(0), ALU.mult)
                    v.tensor_tensor(convT[:], xsl(1), cwB(1), ALU.mult)
                    v.tensor_tensor(convA[:], convA[:], convT[:], ALU.add)
                    v.tensor_tensor(convT[:], xsl(2), cwB(2), ALU.mult)
                    v.tensor_tensor(convA[:], convA[:], convT[:], ALU.add)
                    v.tensor_tensor(convT[:], xsl(3), cwB(3), ALU.mult)
                    v.tensor_tensor(convA[:], convA[:], convT[:], ALU.add)
                    ins = v.tensor_tensor(convB[par2][:], convA[:], cbB, ALU.add)
                    vv.inc(ins, 1)
                convB_ready = (vv, vv.v)
                xbc_deps.append(convB_ready)

                @block.scalar
                def _(s):
                    W(s, f"dld{par}")
                    W(s, "spbufs")
                    dts = dtt[par][:, j0:j0 + 128]
                    ss.inc(s.activation(spe[:], dts, AF.Abs), 1)
                    s.wait_ge(ss.h, ss.v)
                    ss.inc(s.activation(spl[:], spe[:], AF.Exp, scale=-1.0), 1)
                    s.wait_ge(ss.h, ss.v)
                    ss.inc(s.activation(spe[:], spl[:], AF.Ln,
                                        bias=ccol[0:32, 3:4]), 1)
                    s.wait_ge(ss.h, ss.v)
                    ss.inc(s.activation(spr[:], dts, AF.Relu), 1)
                    s.wait_ge(ss.h, ss.v)
                    ss.inc(s.copy(ccol[0:32, 6:7], spr[:, 0:1]), 1)
                sp_ready = (ss, ss.v)
                dtt_deps.append(sp_ready)

                @block.vector
                def _(v):
                    WW(v, sp_ready[0], sp_ready[1])
                    W(v, "cumbufs")
                    vv.inc(v.tensor_tensor(sp_t[:], spr[:], spe[:], ALU.add), 1)
                    v.wait_ge(vv.h, vv.v)
                    vv.inc(v.tensor_tensor_scan(cum[:], ones[0:32, :], sp_t[:],
                                                0.0, ALU.mult, ALU.subtract), 1)
                    v.wait_ge(vv.h, vv.v)
                    vv.inc(v.tensor_scalar(dec_fm[:], cum[:], cum[:, 127:128], None,
                                           ALU.subtract), 1)
                    v.wait_ge(vv.h, vv.v)
                dep["spbufs"] = (vv, vv.v)
                cum_ready = (vv, vv.v)

                @block.tensor
                def _(t_):
                    WW(t_, cum_ready[0], cum_ready[1])
                    W(t_, "trfuse0")
                    tmm.inc(t_.transpose(ps_trf[0][0:128, 0:32], cum[:], identf[0:32, 0:32]), 1)
                cumtr = (tmm, tmm.v)

                @block.vector
                def _(v):
                    WW(v, cumtr[0], cumtr[1])
                    W(v, "cumtokbufs")
                    vv.inc(v.tensor_copy(cum_tok[:], ps_trf[0][0:128, 0:32]), 1)
                    v.wait_ge(vv.h, vv.v)
                    vv.inc(v.tensor_scalar(ncum_tok[:], cum_tok[:], -1.0, None,
                                           ALU.mult), 1)
                    v.wait_ge(vv.h, vv.v)
                dep["trfuse0"] = (vv, vv.v)
                cumtok_ready = (vv, vv.v)

                cdsem = ctx["gdma"][8 + par2]

                @block.sync
                def _(s):
                    WW(s, cum_ready[0], cum_ready[1])
                    W(s, f"cumd{par2}")
                    cdsem.inc(s.dma_start(out=cumdram[par2], in_=cum[:]), 16)
                    s.wait_ge(cdsem.h, cdsem.v)
                    W(s, "Ruse")
                    W(s, "RuseE")
                    cdsem.inc(s.dma_start(
                        out=R_all[:].rearrange("p h f -> p (h f)"),
                        in_=cumdram.rearrange("s h f -> s (h f)")[par2][None, :]
                            .to_broadcast([128, 32 * 128])), 16)
                    dep[f"cumd{par2}"] = (cdsem, cdsem.v)
                rload = (cdsem, cdsem.v)

                @block.scalar
                def _(s):
                    WW(s, rload[0], rload[1])
                    W(s, "Eallguard")
                    ss.inc(s.activation(E_all[:].rearrange("p h f -> p (h f)"),
                                        R_all[:].rearrange("p h f -> p (h f)"),
                                        AF.Exp), 1)
                eall_ready = (ss, ss.v)
                dep["RuseE"] = eall_ready

                @block.vector
                def _(v):
                    WW(v, rload[0], rload[1])
                    WW(v, cumtok_ready[0], cumtok_ready[1])
                    W(v, "segallE")
                    ncB = ncum_tok[:, :, None].to_broadcast([128, 32, 128])
                    vv.inc(v.tensor_tensor(seg_all[:], R_all[:], ncB,
                                           ALU.add), 1)
                    v.wait_ge(vv.h, vv.v)
                    vv.inc(v.tensor_scalar(
                        seg_all[:].rearrange("p h f -> p (h f)"),
                        seg_all[:].rearrange("p h f -> p (h f)"),
                        0.0, None, ALU.min), 1)
                segadd_ready = (vv, vv.v)
                dep["Ruse"] = segadd_ready

                @block.scalar
                def _(s):
                    WW(s, segadd_ready[0], segadd_ready[1])
                    ss.inc(s.activation(seg_all[:].rearrange("p h f -> p (h f)"),
                                        seg_all[:].rearrange("p h f -> p (h f)"),
                                        AF.Exp), 1)
                segexp_ready = (ss, ss.v)
                dep["segallE"] = segexp_ready

                @block.vector
                def _(v):
                    WW(v, segexp_ready[0], segexp_ready[1])
                    trilB = trilu[:, None, :].to_broadcast([128, 32, 128])
                    vv.inc(v.tensor_tensor(seg_all[:], seg_all[:], trilB,
                                           ALU.mult), 1)
                segm_ready = (vv, vv.v)

                @block.scalar
                def _(s):
                    WW(s, cum_ready[0], cum_ready[1])
                    ss.inc(s.activation(dec_fm[:], dec_fm[:], AF.Exp, scale=-1.0), 1)
                    s.wait_ge(ss.h, ss.v)
                    ss.inc(s.activation(acol[:], cum[:, 127:128], AF.Exp), 1)
                    s.wait_ge(ss.h, ss.v)
                    ss.inc(s.copy(ccol[0:32, 6:7], acol[:, 0:1]), 1)
                dec_ready = (ss, ss.v)

                @block.vector
                def _(v):
                    WW(v, dec_ready[0], dec_ready[1])
                    vv.inc(v.tensor_scalar(abct[:], ones[0:32, :], acol[:, 0:1],
                                           None, ALU.mult), 1)
                    v.wait_ge(vv.h, vv.v)
                abct_ready = (vv, vv.v)

                @block.tensor
                def _(t_):
                    WW(t_, abct_ready[0], abct_ready[1])
                    W(t_, "trfuse0")
                    tmm.inc(t_.transpose(ps_trf[0][0:128, 0:32], abct[:],
                                         identf[0:32, 0:32]), 1)
                abctr = (tmm, tmm.v)

                @block.vector
                def _(v):
                    WW(v, abctr[0], abctr[1])
                    vv.inc(v.tensor_copy(A_bc[:], ps_trf[0][0:128, 0:32]), 1)
                    v.wait_ge(vv.h, vv.v)
                dep["trfuse0"] = (vv, vv.v)
                abc_ready = (vv, vv.v)

                @block.tensor
                def _(t_):
                    WW(t_, dec_ready[0], dec_ready[1])
                    W(t_, "trfuse0")
                    tmm.inc(t_.transpose(ps_trf[0][0:128, 0:32], dec_fm[:], identf[0:32, 0:32]), 1)
                dectr = (tmm, tmm.v)

                @block.vector
                def _(v):
                    WW(v, dectr[0], dectr[1])
                    vv.inc(v.tensor_copy(dec_tok[:], ps_trf[0][0:128, 0:32]), 1)
                    v.wait_ge(vv.h, vv.v)
                dep["trfuse0"] = (vv, vv.v)
                dectok_ready = (vv, vv.v)


                @block.tensor
                def _(t_):
                    WW(t_, convB_ready[0], convB_ready[1])
                    for g in range(GB):
                        W(t_, f"tr{g % 2}")
                        tmm.inc(t_.transpose(ps_tr[g % 2][:], convB[par2][:, g, :],
                                             identb[:]), 1)
                        dep[f"trmm{g % 2}"] = (tmm, tmm.v)

                        @block.vector
                        def _(v, g=g):
                            v.wait_ge(dep[f"trmm{g % 2}"][0].h, dep[f"trmm{g % 2}"][1])
                            vv.inc(v.tensor_copy(xT[:, 128 * g:128 * g + 128],
                                                 ps_tr[g % 2][:]), 1)
                            dep[f"tr{g % 2}"] = (vv, vv.v)
                    for g in range(GB):
                        W(t_, f"tr{g % 2}")
                        tmm.inc(t_.transpose(ps_tr[g % 2][:], convB[par2][:, 16 + g, :],
                                             identb[:]), 1)
                        dep[f"trmm{g % 2}"] = (tmm, tmm.v)

                        @block.vector
                        def _(v, g=g):
                            v.wait_ge(dep[f"trmm{g % 2}"][0].h, dep[f"trmm{g % 2}"][1])
                            vv.inc(v.tensor_copy(BT[:, 128 * g:128 * g + 128],
                                                 ps_tr[g % 2][:]), 1)
                            dep[f"tr{g % 2}"] = (vv, vv.v)
                xbt_ready = (vv, vv.v)

                @block.vector
                def _(v):
                    WW(v, dectok_ready[0], dectok_ready[1])
                    W(v, "Bdall")
                    dec64 = dec_tok[:, :, None].to_broadcast([128, NH, 64])
                    vv.inc(v.tensor_tensor(
                        Bd_all[:], BT[:].rearrange("p (h c) -> p h c", c=64),
                        dec64, ALU.mult), 1)
                bdall_ready = (vv, vv.v)

                pipe = {}

                def emit_g0(h):
                    r = h % 2
                    po = r * 64
                    q = h // 2
                    bb = 16 + q
                    cbk = 32 + q

                    @block.tensor
                    def _(t_):
                        WW(t_, convB_ready[0], convB_ready[1])
                        W(t_, f"cbuse{r}")
                        W(t_, f"Sevac{r}")
                        tmm.inc(t_.matmul(ps_cbS[r][:, 0:128],
                                          convB[par2][po:po + 64, bb, :],
                                          convB[par2][po:po + 64, cbk, :],
                                          start=True, stop=True), 1)
                    cb_done = (tmm, tmm.v)
                    pipe[h] = dict(cb_done=cb_done)

                def emit_g1(h):
                    r = h % 2
                    po = r * 64
                    q = h // 2
                    p2 = q % 2
                    cb_done = pipe[h]["cb_done"]

                    @block.vector
                    def _(v):
                        WW(v, cb_done[0], cb_done[1])
                        WW(v, segm_ready[0], segm_ready[1])
                        W(v, f"scmuse{r}")
                        vv.inc(v.tensor_tensor(sc_m[r][:], seg_all[:, h, :],
                                               ps_cbS[r][:, 0:128], ALU.mult), 1)
                        dep[f"cbuse{r}"] = (vv, vv.v)
                    pipe[h]["scm_ready"] = (vv, vv.v)
                    pipe[h]["cb_masked"] = (vv, vv.v)
                    pipe[h]["segE_ready"] = (vv, vv.v)

                def emit_g2(h):
                    r = h % 2
                    po = r * 64
                    scm_ready = pipe[h]["scm_ready"]
                    cb_masked = pipe[h]["cb_masked"]

                    @block.tensor
                    def _(t_):
                        WW(t_, cb_masked[0], cb_masked[1])
                        WW(t_, bdall_ready[0], bdall_ready[1])
                        tmm.inc(t_.matmul(ps_cbS[r][0:64, 128:192], Bd_all[:, h, :],
                                          xT[:, 64 * h:64 * h + 64],
                                          start=True, stop=True), 1)
                        if h >= NH - 1:
                            dep["Bdall"] = (tmm, tmm.v)
                        WW(t_, scm_ready[0], scm_ready[1])
                        WW(t_, xbt_ready[0], xbt_ready[1])
                        W(t_, f"yps{r}")
                        tmm.inc(t_.matmul(ps_y[r][:, 0:64], sc_m[r][:],
                                          xT[:, 64 * h:64 * h + 64],
                                          start=True, stop=False), 1)
                        dep[f"scmuse{r}"] = (tmm, tmm.v)
                    smm_done = (tmm, tmm.v)
                    pipe[h]["smm"] = smm_done

                def emit_pair(q):
                    p2 = q % 2
                    segE_ready = pipe[2 * q + 1]["segE_ready"]
                    smm_done = pipe[2 * q + 1]["smm"]

                    @block.vector
                    def _(v):
                        WW(v, eall_ready[0], eall_ready[1])
                        W(v, f"Csuse{p2}")
                        vv.inc(v.tensor_tensor(Cs[p2][0:64, :],
                                               convB[par2][0:64, 32 + q, :],
                                               E_all[0:64, 2 * q, :], ALU.mult), 1)
                        vv.inc(v.tensor_tensor(Cs[p2][64:128, :],
                                               convB[par2][64:128, 32 + q, :],
                                               E_all[64:128, 2 * q + 1, :], ALU.mult), 1)
                        dep["Eallguard"] = (vv, vv.v)
                    cs_ready = (vv, vv.v)

                    @block.tensor
                    def _(t_):
                        WW(t_, cs_ready[0], cs_ready[1])
                        W(t_, "Hread")
                        for h in (2 * q, 2 * q + 1):
                            po = (h % 2) * 64
                            tmm.inc(t_.matmul(ps_y[h % 2][:, 0:64],
                                              Cs[p2][po:po + 64, :],
                                              Hst[po:po + 64, 64 * q:64 * q + 64],
                                              start=False, stop=True), 1)
                        dep[f"Csuse{p2}"] = (tmm, tmm.v)
                    yfull_done = (tmm, tmm.v)

                    @block.vector
                    def _(v):
                        WW(v, yfull_done[0], yfull_done[1])
                        WW(v, smm_done[0], smm_done[1])
                        WW(v, abc_ready[0], abc_ready[1])
                        for h in (2 * q, 2 * q + 1):
                            po = (h % 2) * 64
                            vv.inc(v.scalar_tensor_tensor(
                                Hst[po:po + 64, 64 * q:64 * q + 64],
                                Hst[po:po + 64, 64 * q:64 * q + 64],
                                A_bc[po:po + 64, h:h + 1],
                                ps_cbS[h % 2][0:64, 128:192],
                                ALU.mult, ALU.add), 1)
                        v.wait_ge(vv.h, vv.v)
                        dep["Sevac0"] = (vv, vv.v)
                        dep["Sevac1"] = (vv, vv.v)
                        dep["Hread"] = (vv, vv.v)
                        W(v, f"ysbuse{p2}")
                        for h, po2 in ((2 * q, 0), (2 * q + 1, 64)):
                            vv.inc(v.scalar_tensor_tensor(
                                y_sb[p2][:, po2:po2 + 64],
                                xT[:, 64 * h:64 * h + 64],
                                dbc[:, h:h + 1],
                                ps_y[h % 2][:, 0:64],
                                ALU.mult, ALU.add), 1)
                        dep["yps0"] = (vv, vv.v)
                        dep["yps1"] = (vv, vv.v)
                    ysb_ready = (vv, vv.v)

                    @block.tensor
                    def _(t_):
                        WW(t_, ysb_ready[0], ysb_ready[1])
                        W(t_, f"tr{p2}")
                        tmm.inc(t_.transpose(ps_tr[p2][:], y_sb[p2][:], identb[:]), 1)
                        dep[f"ysbuse{p2}"] = (tmm, tmm.v)
                    ytr_done = (tmm, tmm.v)

                    @block.vector
                    def _(v):
                        WW(v, ytr_done[0], ytr_done[1])
                        W(v, "prodt")
                        vv.inc(v.tensor_copy(y_fm[:, q, j0:j0 + 128], ps_tr[p2][:]), 1)
                    dep[f"tr{p2}"] = (vv, vv.v)

                for step in range(NH + 2):
                    if 0 <= step - 2 < NH:
                        emit_g2(step - 2)
                        if (step - 2) % 2 == 1:
                            emit_pair((step - 2) // 2)
                    if 0 <= step - 1 < NH:
                        emit_g1(step - 1)
                    if step < NH:
                        emit_g0(step)

                dep[f"convB{par2}"] = [(tmm, tmm.v), (vv, vv.v)]

            yfm_ready = (vv, vv.v)

            @block.vector
            def _(v):
                W(v, f"zld{par}")
                W(v, "zsiluse")
                zbB = zb[:, :, None].to_broadcast([128, GB, CH])
                vv.inc(v.tensor_tensor(zsil[:], zt[par][:], zbB, ALU.add), 1)
            zsil_ready = (vv, vv.v)
            zt_deps.append(zsil_ready)

            @block.scalar
            def _(s):
                WW(s, zsil_ready[0], zsil_ready[1])
                ss.inc(s.activation(zsil[:].rearrange("p g t -> p (g t)"),
                                    zsil[:].rearrange("p g t -> p (g t)"),
                                    AF.Silu), 1)
            sil_done = (ss, ss.v)
            dep["zsiluse"] = sil_done

            @block.vector
            def _(v):
                WW(v, sil_done[0], sil_done[1])
                W(v, "prodt")
                vv.inc(v.tensor_tensor(y_fm[:], y_fm[:], zsil[:], ALU.mult), 1)
                dep["zsiluse"] = (vv, vv.v)
            prod_ready = (vv, vv.v)

            @block.sync
            def _(s):
                WW(s, prod_ready[0], prod_ready[1])
                W(s, f"stp{par}")
                for gq in range(4):
                    stp[par].inc(s.dma_start(
                        out=prodbuf.rearrange("(g p) t -> p g t", p=128)
                            [:, 4 * gq:4 * gq + 4, t0:t0 + CH],
                        in_=y_fm[:, 4 * gq:4 * gq + 4, :]), 16)
            dep["prodt"] = (stp[par], stp[par].v)
            dep[f"stp{par}"] = (stp[par], stp[par].v)

            dep[f"zt{par}"] = zt_deps
            dep[f"xbc{par}"] = xbc_deps
            dep[f"dtt{par}"] = dtt_deps
            dep["cumbufs"] = [(vv, vv.v), (ss, ss.v)]
            dep["cumtokbufs"] = [(vv, vv.v), (ss, ss.v)]

            if b == 0 and cc == 0 and ctx.get("b2dbg"):
                d2 = ctx["b2dbg"]
                dsem = ctx["gdma"][10]
                marks = [(vv, vv.v), (ss, ss.v), (tmm, tmm.v), (gg, gg.v)]

                @block.sync
                def _(s):
                    for sm, mv in marks:
                        s.wait_ge(sm.h, mv)
                    dsem.inc(s.dma_start(out=d2["Hst"][:], in_=Hst[:]), 16)
                    dsem.inc(s.dma_start(out=d2["y_fm"][:], in_=y_fm[:]), 16)
                    dsem.inc(s.dma_start(out=d2["xT"][:], in_=xT[:]), 16)
                    dsem.inc(s.dma_start(out=d2["A_bc"][:], in_=A_bc[:]), 16)
                    dsem.inc(s.dma_start(out=d2["cum_tok"][:], in_=cum_tok[:]), 16)
                dep["b2dump"] = (dsem, dsem.v)
                for k in ("H", "trfuse0", "trfuse1", "prodt", "convB", "ysbuse0",
                          "ysbuse1", "scmuse0", "scmuse1", "Csuse0", "Csuse1",
                          "Euse0", "Euse1", "yps0", "yps1", "Ssbuse", "Sevac0",
                          "Sevac1", "Hread", "cumtokbufs", "cumbufs"):
                    old = dep.get(k)
                    add = (dsem, dsem.v)
                    if old is None:
                        dep[k] = add
                    elif isinstance(old, list):
                        dep[k] = old + [add]
                    else:
                        dep[k] = [old, add]

    es.close()
    pes.close()
    _stage_out(ctx, tmm, vv, ss, gg, *ldz, *ldx, *ldd, *stp, ctx["gdma"][10], ctx["gdma"][11])


def emit_b3(ctx, prodbuf, woutT, vbuf, xbuf1, resbuf, h2fmbuf):
    nc, block, sem = ctx["nc"], ctx["block"], ctx["sem"]
    identb = ctx["identb"]
    ntile = T // 128

    lwo = ctx["gswd"][0]
    lps = ctx["gdma"][0:2]
    lvs = ctx["gdma"][2:4]
    lx1 = ctx["gdma"][4:6]
    mmo = ctx["geng"]["g"]
    tmm, vv, ss = ctx["geng"]["t"], ctx["geng"]["v"], ctx["geng"]["s"]
    sts = ctx["gdma"][6:8]
    sth = ctx["gdma"][8:10]

    es = contextlib.ExitStack()
    sb = lambda n, sh, dt: es.enter_context(nc.sbuf_tensor(f"b3_{n}", sh, dt))
    wout_sb = sb("wout", [128, GB, D], BF16)
    prodt = [sb(f"prod{i}", [128, GB, 128], BF16) for i in range(2)]
    vt = [sb(f"vt{i}", [128, D], F32) for i in range(2)]
    x1t = [sb(f"x1t{i}", [128, D], F32) for i in range(2)]
    a_t = [sb(f"at{i}", [128, D], F32) for i in range(2)]
    s_t = [sb(f"st{i}", [128, D], F32) for i in range(2)]
    h2tm = [sb(f"h2tm{i}", [128, D], BF16) for i in range(2)]
    h2f = [sb(f"h2f{i}", [128, GB, 128], BF16) for i in range(2)]
    colb = [sb(f"col{i}", [128, 8], F32) for i in range(2)]
    scr_f = sb("scr_f", [128, D], F32)

    pes = contextlib.ExitStack()
    ps_o = [pes.enter_context(nc.psum_tensor(f"b3_o{i}", [128, 512], F32))
            for i in range(4)]
    ps_tr_t = [pes.enter_context(nc.psum_tensor(f"b3_tr{i}", [128, 128], BF16))
               for i in range(2)]
    ps_tr = [ps_tr_t[0][:], ps_tr_t[1][:]]

    dep = {}
    W, WW = mk_W(ctx, dep)

    @block.gpsimd
    def _(g):
        for k in range(GB):
            lwo.inc(g.dma_start(out=wout_sb[:, k, :],
                                in_=woutT[128 * k:128 * k + 128, :]), 16)

    for gt in range(ntile):
        par = gt % 2

        @block.sync
        def _(s):
            W(s, f"prod{par}")
            for gq in range(2):
                lps[par].inc(s.dma_start(
                    out=prodt[par][:, 8 * gq:8 * gq + 8, :],
                    in_=prodbuf.rearrange("(g p) t -> p g t", p=128)
                        [:, 8 * gq:8 * gq + 8, 128 * gt:128 * gt + 128]), 16)
            W(s, f"vt{par}")
            lvs[par].inc(s.dma_start(out=vt[par][:],
                                     in_=vbuf[128 * gt:128 * gt + 128, :]), 16)
            W(s, f"x1t{par}")
            lx1[par].inc(s.dma_start(out=x1t[par][:],
                                     in_=xbuf1[128 * gt:128 * gt + 128, :]), 16)
            dep[f"pld{par}"] = (lps[par], lps[par].v)
            dep[f"vld{par}"] = (lvs[par], lvs[par].v)
            dep[f"xld{par}"] = (lx1[par], lx1[par].v)

        @block.tensor
        def _(t_):
            if gt == 0:
                t_.wait_ge(lwo.h, lwo.v)
            W(t_, f"pld{par}")
            W(t_, "ps_o")
            for k in range(GB):
                for n4 in range(4):
                    mm = t_.matmul(ps_o[n4][:], prodt[par][:, k, :],
                                   wout_sb[:, k, 512 * n4:512 * n4 + 512],
                                   start=(k == 0), stop=(k == GB - 1))
            mmo.inc(mm, 1)
        mm_done = (mmo, mmo.v)
        dep[f"prod{par}"] = mm_done

        @block.vector
        def _(v):
            WW(v, mm_done[0], mm_done[1])
            W(v, f"vld{par}")
            W(v, f"at{par}")
            for n4 in range(4):
                v.tensor_tensor(a_t[par][:, 512 * n4:512 * n4 + 512], ps_o[n4][:],
                                vt[par][:, 512 * n4:512 * n4 + 512], ALU.add)
            vv.inc(v.tensor_copy(scr_f[:, 0:2], scr_f[:, 0:2]), 1)
            dep["ps_o"] = (vv, vv.v)
            dep[f"vt{par}"] = (vv, vv.v)
            W(v, f"xld{par}")
            W(v, f"st{par}")
            vv.inc(v.tensor_tensor(s_t[par][:], a_t[par][:], x1t[par][:], ALU.add), 1)
            v.wait_ge(vv.h, vv.v)
            dep[f"x1t{par}"] = (vv, vv.v)
            cb = colb[par]
            vv.inc(v.scalar_tensor_tensor(scr_f[:], a_t[par][:], 1.0, a_t[par][:],
                                          ALU.mult, ALU.mult, accum_out=cb[:, 0:1]), 1)
            v.wait_ge(vv.h, vv.v)
            vv.inc(v.tensor_copy(cb[:, 1:2], cb[:, 0:1]), 1)
        stats_done = (vv, vv.v)
        s_ready = (vv, vv.v)

        @block.scalar
        def _(s):
            WW(s, stats_done[0], stats_done[1])
            ss.inc(s.activation(colb[par][:, 2:3], colb[par][:, 0:1], AF.Sqrt,
                                scale=1.0 / D, bias=ctx["ccol"][:, 2:3]), 1)
            s.wait_ge(ss.h, ss.v)
            ss.inc(s.copy(ctx["ccol"][:, 6:7], colb[par][:, 2:3]), 1)
        sq_done = (ss, ss.v)

        @block.vector
        def _(v):
            WW(v, sq_done[0], sq_done[1])
            vv.inc(v.reciprocal(colb[par][:, 3:4], colb[par][:, 2:3]), 1)
            v.wait_ge(vv.h, vv.v)
            W(v, f"h2tm{par}")
            vv.inc(v.tensor_scalar(scr_f[:], a_t[par][:], colb[par][:, 3:4], None,
                                   ALU.mult), 1)
            vv.inc(v.tensor_tensor(h2tm[par][:], scr_f[:], ctx["n2w"][:], ALU.mult), 1)
            dep[f"at{par}"] = (vv, vv.v)
        h2_done = (vv, vv.v)

        @block.tensor
        def _(t_):
            WW(t_, h2_done[0], h2_done[1])
            for g in range(GB):
                W(t_, f"tr{g % 2}")
                tmm.inc(t_.transpose(ps_tr[g % 2], h2tm[par][:, 128 * g:128 * g + 128],
                                     identb[:]), 1)
                dep[f"trmm{g % 2}"] = (tmm, tmm.v)

                if g % 2 == 0:
                    @block.vector
                    def _(v, g=g):
                        v.wait_ge(dep[f"trmm{g % 2}"][0].h, dep[f"trmm{g % 2}"][1])
                        W(v, f"h2f{par}")
                        vv.inc(v.tensor_copy(h2f[par][:, g, :], ps_tr[g % 2]), 1)
                        dep[f"tr{g % 2}"] = (vv, vv.v)
                else:
                    @block.scalar
                    def _(s, g=g):
                        s.wait_ge(dep[f"trmm{g % 2}"][0].h, dep[f"trmm{g % 2}"][1])
                        W(s, f"h2f{par}")
                        ss.inc(s.copy(h2f[par][:, g, :], ps_tr[g % 2]), 1)
                        dep[f"tr{g % 2}"] = (ss, ss.v)
        h2f_done = [(vv, vv.v), (ss, ss.v)]

        @block.sync
        def _(s):
            for d in h2f_done:
                WW(s, d[0], d[1])
            W(s, f"sth{par}")
            sth[par].inc(s.dma_start(
                out=h2fmbuf[:, :, 128 * gt:128 * gt + 128].rearrange("g p c -> p g c"),
                in_=h2f[par][:]), 16)
            WW(s, s_ready[0], s_ready[1])
            W(s, f"sts{par}")
            sts[par].inc(s.dma_start(out=resbuf[128 * gt:128 * gt + 128, :],
                                     in_=s_t[par][:]), 16)
        dep[f"h2f{par}"] = (sth[par], sth[par].v)
        dep[f"sth{par}"] = (sth[par], sth[par].v)
        dep[f"st{par}"] = (sts[par], sts[par].v)
        dep[f"sts{par}"] = (sts[par], sts[par].v)
        dep[f"h2tm{par}"] = (tmm, tmm.v)

    es.close()
    pes.close()
    _stage_out(ctx, mmo, tmm, vv, ss, lwo, *lps, *lvs, *lx1, *sts, *sth)


def emit_b4(ctx, h2fmbuf, wg_scr, wu_scr, wd_scr, resbuf, xbuf2):
    nc, block, sem = ctx["nc"], ctx["block"], ctx["sem"]
    NT4 = PANEL // 128
    NP4 = T // PANEL
    NS = FF // 128

    lh2 = ctx["gdma"][0:2]
    lgw = ctx["gdma"][2:4]
    ldw = ctx["gdma"][4:6]
    lrs = ctx["gdma"][6:8]
    mmg = ctx["geng"]["t"]
    mmd = ctx["geng"]["g"]
    vv, ss = ctx["geng"]["v"], ctx["geng"]["s"]
    stx = ctx["gdma"][8:10]

    es = contextlib.ExitStack()
    sb = lambda n, sh, dt: es.enter_context(nc.sbuf_tensor(f"b4_{n}", sh, dt))
    h2p = [sb(f"h2p{i}", [128, GB, PANEL], BF16) for i in range(2)]
    wg_t = [sb(f"wg{i}", [128, GB * 128], BF16) for i in range(2)]
    wu_t = [sb(f"wu{i}", [128, GB * 128], BF16) for i in range(2)]
    hfm = sb("hfm", [128, NS, PANEL], BF16)
    sil = [sb(f"sil{i}", [128, PANEL], BF16) for i in range(2)]
    wd_t = [sb(f"wd{i}", [128, 512], BF16) for i in range(2)]
    yd = [sb(f"yd{i}", [128, D], F32) for i in range(NT4)]
    rs_t = [sb(f"rs{i}", [128, D], F32) for i in range(2)]
    xo = [sb(f"xo{i}", [128, D], F32) for i in range(2)]

    pes = contextlib.ExitStack()
    ps = [pes.enter_context(nc.psum_tensor(f"b4_ps{i}", [128, PANEL], F32))
          for i in range(4)]

    dep = {}
    W, WW = mk_W(ctx, dep)

    for p in range(NP4):
        par = p % 2
        tb = p * NT4

        @block.sync
        def _(s):
            W(s, f"h2p{par}")
            for gq in range(2):
                lh2[par].inc(s.dma_start(
                    out=h2p[par][:, 8 * gq:8 * gq + 8, :],
                    in_=h2fmbuf.rearrange("g p t -> p g t")
                        [:, 8 * gq:8 * gq + 8, PANEL * p:PANEL * p + PANEL]), 16)
            dep[f"h2ld{par}"] = (lh2[par], lh2[par].v)

        for s_ in range(NS):
            it = p * NS + s_
            spar = it % 2

            @block.sync
            def _(s):
                W(s, f"wg{spar}")
                lgw[spar].inc(s.dma_start(out=wg_t[spar][:], in_=wg_scr[s_]), 16)
                lgw[spar].inc(s.dma_start(out=wu_t[spar][:], in_=wu_scr[s_]), 16)
                dep[f"gld{spar}"] = (lgw[spar], lgw[spar].v)

            @block.tensor
            def _(t_):
                W(t_, f"gld{spar}")
                W(t_, f"h2ld{par}")
                W(t_, "psd")
                W(t_, f"psg{s_ % 2}")
                for g in range(GB):
                    mm = t_.matmul(ps[s_ % 2][:], wg_t[spar][:, 128 * g:128 * g + 128],
                                   h2p[par][:, g, :], start=(g == 0), stop=(g == GB - 1))
                mmg.inc(mm, 1)
                W(t_, f"psu{s_ % 2}")
                for g in range(GB):
                    mm = t_.matmul(ps[2 + s_ % 2][:],
                                   wu_t[spar][:, 128 * g:128 * g + 128],
                                   h2p[par][:, g, :], start=(g == 0), stop=(g == GB - 1))
                mmg.inc(mm, 1)
            gu_done = (mmg, mmg.v)
            dep[f"wg{spar}"] = gu_done

            @block.scalar
            def _(s):
                s.wait_ge(gu_done[0].h, gu_done[1] - 1)
                W(s, f"sil{s_ % 2}")
                ss.inc(s.activation(sil[s_ % 2][:], ps[s_ % 2][:], AF.Silu), 1)
                dep[f"psg{s_ % 2}"] = (ss, ss.v)
            sil_done = (ss, ss.v)

            @block.vector
            def _(v):
                WW(v, sil_done[0], sil_done[1])
                WW(v, gu_done[0], gu_done[1])
                W(v, "hfm_rd")
                vv.inc(v.tensor_tensor(hfm[:, s_, :], sil[s_ % 2][:],
                                       ps[2 + s_ % 2][:], ALU.mult), 1)
                dep[f"psu{s_ % 2}"] = (vv, vv.v)
                dep[f"sil{s_ % 2}"] = (vv, vv.v)
        h_ready = (vv, vv.v)
        dep[f"h2p{par}"] = (mmg, mmg.v)

        for of4 in range(4):
            for s_ in range(NS):
                it2 = (p * 4 + of4) * NS + s_
                dpar = it2 % 2

                @block.sync
                def _(s):
                    W(s, f"wd{dpar}")
                    ldw[dpar].inc(s.dma_start(
                        out=wd_t[dpar][:],
                        in_=wd_scr[128 * s_:128 * s_ + 128,
                                   512 * of4:512 * of4 + 512]), 16)
                    dep[f"dwld{dpar}"] = (ldw[dpar], ldw[dpar].v)

                @block.tensor
                def _(t_):
                    W(t_, f"dwld{dpar}")
                    if s_ == 0:
                        WW(t_, h_ready[0], h_ready[1])
                        W(t_, "psd")
                    for t in range(NT4):
                        mm = t_.matmul(ps[t][:, 0:512], hfm[:, s_, 128 * t:128 * t + 128],
                                       wd_t[dpar][:], start=(s_ == 0), stop=(s_ == NS - 1))
                    mmd.inc(mm, 1)
                dep[f"wd{dpar}"] = (mmd, mmd.v)
            dmm_done = (mmd, mmd.v)

            @block.vector
            def _(v):
                WW(v, dmm_done[0], dmm_done[1])
                W(v, "yd")
                for t in range(NT4):
                    vv.inc(v.tensor_copy(yd[t][:, 512 * of4:512 * of4 + 512],
                                         ps[t][:, 0:512]), 1)
                v.wait_ge(vv.h, vv.v)
            dep["psd"] = (vv, vv.v)
        yd_ready = (vv, vv.v)
        dep["hfm_rd"] = (mmd, mmd.v)

        for t in range(NT4):
            gt = tb + t
            tpar = gt % 2

            @block.sync
            def _(s):
                W(s, f"rs{tpar}")
                lrs[tpar].inc(s.dma_start(out=rs_t[tpar][:],
                                          in_=resbuf[128 * gt:128 * gt + 128, :]), 16)
                dep[f"rld{tpar}"] = (lrs[tpar], lrs[tpar].v)

            @block.vector
            def _(v):
                WW(v, yd_ready[0], yd_ready[1])
                W(v, f"rld{tpar}")
                W(v, f"xo{tpar}")
                vv.inc(v.tensor_tensor(xo[tpar][:], yd[t][:], rs_t[tpar][:],
                                       ALU.add), 1)
                v.wait_ge(vv.h, vv.v)
                dep[f"rs{tpar}"] = (vv, vv.v)
                if t == NT4 - 1:
                    dep["yd"] = (vv, vv.v)
            xo_done = (vv, vv.v)

            @block.sync
            def _(s):
                WW(s, xo_done[0], xo_done[1])
                W(s, f"stx{tpar}")
                stx[tpar].inc(s.dma_start(out=xbuf2[128 * gt:128 * gt + 128, :],
                                          in_=xo[tpar][:]), 16)
            dep[f"xo{tpar}"] = (stx[tpar], stx[tpar].v)
            dep[f"stx{tpar}"] = (stx[tpar], stx[tpar].v)

    es.close()
    pes.close()
    _stage_out(ctx, mmg, mmd, vv, ss, *lh2, *lgw, *ldw, *lrs, *stx)


def prep_inputs(inp):
    x = np.asarray(inp["L0_attn_hidden_states"], np.float32)
    pi, pif = PI_D, PI_FF
    xp = np.ascontiguousarray(x[..., pi])

    win = np.asarray(inp["L0_attn_in_proj_w"], np.float32)
    winp = win[:, pi]
    winp_pad = np.zeros((65 * 128, D), np.float32)
    winp_pad[:DIP] = winp
    win2 = np.ascontiguousarray(
        winp_pad.reshape(65, 128, GB, 128).transpose(0, 3, 2, 1).reshape(65, 128, GB * 128))

    wout = np.asarray(inp["L0_attn_out_proj_w"], np.float32)
    woutT = np.ascontiguousarray(wout[pi, :].T)

    def tile_lhsT(w):
        wp = w[:, pi]
        return np.ascontiguousarray(
            wp.reshape(64, 128, GB, 128).transpose(0, 3, 2, 1).reshape(64, 128, GB * 128))

    wgate = tile_lhsT(np.asarray(inp["L0_attn_gate_w"], np.float32))
    wup = tile_lhsT(np.asarray(inp["L0_attn_up_w"], np.float32))
    wdownT = np.ascontiguousarray(np.asarray(inp["L0_attn_down_w"], np.float32)[pi, :].T)

    gup = np.asarray(inp["L0_ffn_gate_up_packed"], np.int32)
    gus = np.asarray(inp["L0_ffn_gate_up_scales"], np.float16)
    perm_rows = np.concatenate([pif, FFQ + pif])
    gup_p = gup[perm_rows]
    gus_p = np.ascontiguousarray(gus[:, perm_rows])
    gu_u8t = np.ascontiguousarray(gup_p.view(np.uint8).reshape(2 * FFQ, D // 2).T)

    dnp = np.asarray(inp["L0_ffn_down_packed"], np.int32)
    dns = np.asarray(inp["L0_ffn_down_scales"], np.float16)
    dnp_p = dnp[pi]
    dns_p = np.ascontiguousarray(dns[:, pi])
    dn_u8t = np.ascontiguousarray(dnp_p.view(np.uint8).reshape(D, FFQ // 2).T)

    com = dict(
        win2=win2, woutT=woutT, wgate=wgate, wup=wup, wdownT=wdownT,
        gu_u8t=gu_u8t, gu_sc=gus_p, dn_u8t=dn_u8t, dn_sc=dns_p,
        zbias=np.asarray(inp["L0_attn_z_bias"], np.float32),
        convw=np.asarray(inp["L0_attn_conv_w"], np.float32),
        convb=np.asarray(inp["L0_attn_conv_b"], np.float32),
        dvec=np.asarray(inp["L0_attn_D"], np.float32),
        n1w=np.asarray(inp["L0_attn_norm1_w"], np.float32)[pi],
        n2w=np.asarray(inp["L0_attn_norm2_w"], np.float32)[pi],
        identf=np.eye(128, dtype=np.float32),
        onesf=np.ones((128, 128), np.float32),
        trilu=np.triu(np.ones((128, 128), np.float32)),
    )
    in_maps = []
    for c in range(NCORES):
        m = dict(com)
        m["xin"] = np.ascontiguousarray(
            xp[c * SEQ_PER_CORE:(c + 1) * SEQ_PER_CORE].reshape(T, D))
        in_maps.append(m)
    return in_maps


def kernel(**inputs):
    in_maps = prep_inputs(inputs)
    nc = build_program()
    res = run_bass_kernel_spmd(nc, in_maps, core_ids=list(range(NCORES)))
    outs = [r["out"] for r in res.results]
    y = np.concatenate([o.reshape(SEQ_PER_CORE, L, D) for o in outs], axis=0)
    return y
